# version 16
# baseline (speedup 1.0000x reference)
import numpy as np
import concourse.bass as bass
import concourse.mybir as mybir
from concourse.bass_utils import run_bass_kernel_spmd

F32 = mybir.dt.float32
BF16 = mybir.dt.bfloat16
I32 = mybir.dt.int32
AF = mybir.ActivationFunctionType
ALU = mybir.AluOpType
AX = mybir.AxisListType

D = 1024
S = 8192
H_LOC = 4
HD = 64
DLOC = H_LOC * HD
TOK = 2048
FH = 3072
CW = 256
QB = 512
NQB = S // QB
NKCH = S // 128
GRP = 2
EPS = 1e-5
DEQ = 1.0 / 127.0


class Ctr:
    def __init__(self, nc, name):
        self.h = nc.alloc_semaphore(name=name)
        self.n = 0


class EngQ:

    def __init__(self, self_ctr=None):
        self.fns = []
        self.self_ctr = self_ctr
        self.last_wait = {}

    def wait(self, ctr, val=None):
        v = ctr.n if val is None else val
        if v > 0 and v > self.last_wait.get(ctr.h.num, 0):
            self.last_wait[ctr.h.num] = v
            self.fns.append(lambda e, s=ctr.h, vv=v: e.wait_ge(s, vv))

    def emit(self, build, ctr=None, inc=1):
        if self.self_ctr is not None:
            prev = self.self_ctr.n
            if prev > 0:
                self.fns.append(lambda e, s=self.self_ctr.h, vv=prev: e.wait_ge(s, vv))
            if ctr is None:
                ctr, inc = self.self_ctr, 1
        if ctr is None:
            self.fns.append(build)
            return None
        ctr.n += inc
        tok = ctr.n

        def fn(e, b=build, s=ctr.h, k=inc):
            b(e).then_inc(s, k)

        self.fns.append(fn)
        return tok


STAGES = ["p0", "p1", "p1c", "p2", "p3", "cc", "p1b", "p5", "p6"]
LIMIT = "p6"


def build_program(limit=None):
    limit = LIMIT if limit is None else limit
    lim_i = STAGES.index(limit)

    def on(stage):
        return STAGES.index(stage) <= lim_i

    nc = bass.Bass("TRN2", target_bir_lowering=False)

    xT = nc.dram_tensor("xT", [D, S], F32, kind="ExternalInput")
    xoT = nc.dram_tensor("xoT", [D, TOK], F32, kind="ExternalInput")
    xrows = nc.dram_tensor("xrows", [TOK, D], F32, kind="ExternalInput")
    wqT = nc.dram_tensor("wqT", [D, DLOC], F32, kind="ExternalInput")
    wkT = nc.dram_tensor("wkT", [D, DLOC], F32, kind="ExternalInput")
    wvT = nc.dram_tensor("wvT", [D, DLOC], F32, kind="ExternalInput")
    woT = nc.dram_tensor("woT", [DLOC, D], F32, kind="ExternalInput")
    bqc = nc.dram_tensor("bqc", [128, 2], F32, kind="ExternalInput")
    bkc = nc.dram_tensor("bkc", [128, 2], F32, kind="ExternalInput")
    bvr = nc.dram_tensor("bvr", [1, DLOC], F32, kind="ExternalInput")
    bor = nc.dram_tensor("bor", [1, D], F32, kind="ExternalInput")
    hmr = nc.dram_tensor("hmr", [1, H_LOC], F32, kind="ExternalInput")
    wguT = nc.dram_tensor("wguT", [24, 128, 2, 8, 128], I32, kind="ExternalInput")
    wdT = nc.dram_tensor("wdT", [FH, D], I32, kind="ExternalInput")
    out = nc.dram_tensor("out", [TOK, D], F32, kind="ExternalOutput")
    cc_in = nc.dram_tensor("cc_in", [S, D], BF16)
    cc_out = nc.dram_tensor("cc_out", [TOK, D], BF16)
    mxd = nc.dram_tensor("mxd", [128, 1], F32)

    TEN = Ctr(nc, "TEN")
    ACT = Ctr(nc, "ACT")
    VEC = Ctr(nc, "VEC")
    POO = Ctr(nc, "POO")
    CCS = Ctr(nc, "CCS")
    PH = Ctr(nc, "PH")
    EPI = Ctr(nc, "EPI")
    SYC = Ctr(nc, "SYC")
    SXF = [Ctr(nc, f"SXF{i}") for i in range(2)]
    GXB = [Ctr(nc, f"GXB{i}") for i in range(2)]
    GW = Ctr(nc, "GW")
    GF = [Ctr(nc, f"GF{i}") for i in range(6)]
    SSP = [Ctr(nc, f"SSP{i}") for i in range(2)]
    SMX = Ctr(nc, "SMX")
    SS6 = [Ctr(nc, f"SS6{i}") for i in range(2)]
    SO6 = [Ctr(nc, f"SO6{i}") for i in range(2)]

    QSY, QT, QG = EngQ(), EngQ(), EngQ()
    QV, QA = EngQ(self_ctr=VEC), EngQ(self_ctr=ACT)
    allq = {"sync": QSY, "tensor": QT, "vector": QV, "scalar": QA, "gpsimd": QG}
    nbar = [0]

    def barrier():
        nbar[0] += 1
        tgt = 5 * nbar[0]
        for q in allq.values():
            q.emit(lambda e: e.nop(), PH)
            q.wait(PH, tgt)

    sb = lambda name, shape, dt=BF16: nc.alloc_sbuf_tensor(name, shape, dt)
    ps = lambda name, shape: nc.alloc_psum_tensor(name, shape, F32)

    def _replay():
        with nc.Block() as block:
            @block.sync
            def _(e):
                for fn in QSY.fns:
                    fn(e)

            @block.tensor
            def _(e):
                for fn in QT.fns:
                    fn(e)

            @block.vector
            def _(e):
                for fn in QV.fns:
                    fn(e)

            @block.scalar
            def _(e):
                for fn in QA.fns:
                    fn(e)

            @block.gpsimd
            def _(e):
                for fn in QG.fns:
                    fn(e)
        return nc

    def _truncate():
        od = nc.alloc_sbuf_tensor("o_dummy", [128, D], F32)
        tz = QV.emit(lambda e: e.memset(od[:, :], 0.0), VEC)
        QSY.wait(VEC, tz)
        for tt in range(TOK // 128):
            QSY.emit(lambda e, t2=tt: e.dma_start(
                out=out[t2 * 128:(t2 + 1) * 128, :], in_=od[:, :]), SO6[0], 16)
        QSY.wait(SO6[0], SO6[0].n)
        return _replay()

    ones_mean = sb("ones_mean", [128, 128])
    ones_col = sb("ones_col", [128, 1])
    ones_row = sb("ones_row", [1, 128], F32)
    sel33 = sb("sel33", [33, 128], F32)
    rb33 = sb("rb33", [33, 512], F32)
    bq_sb = sb("bq_sb", [128, 2], F32)
    bk_sb = sb("bk_sb", [128, 2], F32)
    bvrow = sb("bvrow", [1, DLOC], F32)
    borow = sb("borow", [1, D], F32)
    hm_sb = sb("hm_sb", [1, H_LOC], F32)
    bvrep = sb("bvrep", [128, DLOC], F32)
    borep = sb("borep", [128, D], F32)
    crep = sb("crep", [128, 1], F32)
    xnf = sb("xnf", [128, 8, TOK])
    xnf_flat = xnf
    sbuf_M0 = nc.sbuf_base
    kt = [sb(f"kt{i}", [128, S]) for i in range(2)]
    qt_ = [sb(f"qt{i}", [128, S]) for i in range(2)]
    vv = sb("vv", [128, NKCH, DLOC])
    sbuf_M1 = nc.sbuf_base

    QV.emit(lambda e: e.memset(ones_mean[:, :], 1.0 / 1024.0))
    QV.emit(lambda e: e.memset(ones_col[:, :], 1.0))
    QV.emit(lambda e: e.memset(ones_row[:, :], 1.0))
    QV.emit(lambda e: e.memset(sel33[:, :], 0.0))
    QV.emit(lambda e: e.memset(sel33[0:1, 0:64], 1.0))
    QV.emit(lambda e: e.memset(sel33[32:33, 64:128], 1.0))
    QV.emit(lambda e: e.memset(rb33[:, :], 0.0))
    QV.emit(lambda e: e.memset(crep[:, :], 0.0), VEC)
    v_consts = VEC.n

    for (dst, src) in ((bq_sb, bqc), (bk_sb, bkc), (bvrow, bvr),
                       (borow, bor), (hm_sb, hmr)):
        QSY.emit(lambda e, d=dst, s=src: e.dma_start(out=d[:, :], in_=s[:, :]), SYC, 16)
    sy_consts = SYC.n

    psum_base0 = nc.psum_base

    ps_rep = ps("ps_rep", [128, 512])
    QT.wait(SYC, sy_consts)
    QT.wait(VEC, v_consts)
    QT.emit(lambda e: e.matmul(ps_rep[:, 0:DLOC], ones_row[:, :], bvrow[:, :],
                               start=True, stop=True), TEN)
    QV.wait(TEN)
    t = QV.emit(lambda e: e.tensor_copy(bvrep[:, :], ps_rep[:, 0:DLOC]), VEC)
    for half in range(2):
        QT.wait(VEC, t)
        QT.emit(lambda e, h=half: e.matmul(ps_rep[:, :], ones_row[:, :],
                                           borow[:, h * 512:(h + 1) * 512],
                                           start=True, stop=True), TEN)
        QV.wait(TEN)
        t = QV.emit(lambda e, h=half: e.tensor_copy(borep[:, h * 512:(h + 1) * 512],
                                                    ps_rep[:, :]), VEC)
    barrier()
    nc.psum_base = psum_base0
    if not on("p1"):
        return _truncate()

    def ln_pipeline(tag, src_dram, n_chunks, rstd_bias, consumers):
        xf = [sb(f"xf_{tag}{i}", [128, 8, CW], F32) for i in range(2)]
        xb = [sb(f"xb_{tag}{i}", [128, 8, CW]) for i in range(2)]
        xq = [sb(f"xq_{tag}{i}", [128, 8, CW]) for i in range(2)]
        xn = [sb(f"xn_{tag}{i}", [128, 8, CW]) for i in range(2)]
        m_sb = [sb(f"m_{tag}{i}", [128, CW], F32) for i in range(2)]
        msq = [sb(f"msq_{tag}{i}", [128, CW], F32) for i in range(2)]
        var = [sb(f"var_{tag}{i}", [128, CW], F32) for i in range(2)]
        lnv = [sb(f"lnv_{tag}{i}", [128, CW], F32) for i in range(2)]
        rstd = [sb(f"rstd_{tag}{i}", [128, CW], F32) for i in range(2)]
        tmp = sb(f"tmp_{tag}", [128, 4, CW], F32)
        eps_col = sb(f"eps_{tag}", [128, 1], F32)
        rb_col = [sb(f"rbias_{tag}{i}", [128, 1], F32) for i in range(2)]
        QV.emit(lambda e: e.memset(eps_col[:, :], float(EPS)))
        QV.emit(lambda e: e.memset(rb_col[0][:, :], 0.0))
        t0m = QV.emit(lambda e: e.memset(rb_col[1][:, :], float(np.log(DEQ))), VEC)
        QA.wait(VEC, t0m)
        ps_m = [ps(f"ps_m_{tag}{i}", [128, CW]) for i in range(2)]
        ps_s = [ps(f"ps_s_{tag}{i}", [128, CW]) for i in range(2)]

        t_xf_cons = [0, 0]
        t_xb_cons_t = [0, 0]
        t_xb_cons_p = [0, 0]
        t_xq_cons = [0, 0]
        t_xn_cons = [0, 0]
        t_xn_cons_ctr = [None, None]
        t_stats_read2 = [0, 0]
        t_rstd_read = [0, 0]
        t_act_rows = [0, 0]
        pending_cons = []

        for ci in range(n_chunks):
            bi = ci % 2
            src = src_dram(ci)
            QSY.wait(VEC, t_xf_cons[bi])
            QSY.emit(lambda e, a=src, d=xf[bi]: e.dma_start(out=d[:, :, :], in_=a), SXF[bi], 16)
            t_xf = SXF[bi].n
            QG.wait(TEN, t_xb_cons_t[bi])
            QG.wait(ACT, t_xb_cons_p[bi])
            QG.emit(lambda e, a=src, d=xb[bi]: e.dma_start(out=d[:, :, :], in_=a), GXB[bi], 16)
            t_xb = GXB[bi].n

            QA.wait(GXB[bi], t_xb)
            QA.wait(TEN, t_xq_cons[bi])
            QA.emit(lambda e, b=bi: e.activation(
                xq[b][:, :, :], xb[b][:, :, :], AF.Square, bias=0.0, scale=1.0), ACT)
            t_xsq = ACT.n

            QT.wait(GXB[bi], t_xb)
            QT.wait(ACT, t_xsq)
            QT.wait(VEC, t_stats_read2[bi])
            for tt in range(8):
                QT.emit(lambda e, b=bi, m=tt: e.matmul(
                    ps_m[b][:, :], ones_mean[:, :], xb[b][:, m, :],
                    start=(m == 0), stop=(m == 7)), TEN)
            t_xb_cons_save = TEN.n
            for tt in range(8):
                QT.emit(lambda e, b=bi, m=tt: e.matmul(
                    ps_s[b][:, :], ones_mean[:, :], xq[b][:, m, :],
                    start=(m == 0), stop=(m == 7)), TEN)
            t_xb_cons_t[bi] = t_xb_cons_save
            t_xb_cons_p[bi] = t_xsq
            t_xq_cons[bi] = TEN.n
            t_stats = TEN.n

            QV.wait(TEN, t_stats)
            QV.wait(ACT, t_act_rows[bi])
            QV.emit(lambda e, b=bi: e.tensor_copy(m_sb[b][:, :], ps_m[b][:, :]), VEC)
            QV.emit(lambda e, b=bi: e.tensor_tensor(msq[b][:, :], m_sb[b][:, :], m_sb[b][:, :], ALU.mult), VEC)
            QV.emit(lambda e, b=bi: e.tensor_tensor(var[b][:, :], ps_s[b][:, :], msq[b][:, :], ALU.subtract), VEC)
            t_stats_read2[bi] = VEC.n
            QA.wait(VEC, t_stats_read2[bi])
            QA.wait(VEC, t_rstd_read[bi])
            QA.emit(lambda e, b=bi: e.activation(lnv[b][:, :], var[b][:, :], AF.Ln,
                                                 bias=eps_col[:, 0:1], scale=1.0), ACT)
            QA.emit(lambda e, b=bi, rc=rb_col[1 if rstd_bias(ci) else 0]: e.activation(
                rstd[b][:, :], lnv[b][:, :], AF.Exp,
                bias=rc[:, 0:1], scale=-0.5), ACT)
            t_act_rows[bi] = ACT.n

            QV.wait(SXF[bi], t_xf)
            QV.wait(ACT, t_act_rows[bi])
            if t_xn_cons_ctr[bi] is not None:
                QV.wait(t_xn_cons_ctr[bi], t_xn_cons[bi])
            for hh in range(2):
                QV.emit(lambda e, b=bi, h=hh: e.tensor_tensor(
                    tmp[:, :, :], xf[b][:, 4 * h:4 * h + 4, :],
                    m_sb[b][:, None, :].broadcast_to([128, 4, CW]), ALU.subtract), VEC)
                QV.emit(lambda e, b=bi, h=hh: e.tensor_tensor(
                    xn[b][:, 4 * h:4 * h + 4, :], tmp[:, :, :],
                    rstd[b][:, None, :].broadcast_to([128, 4, CW]), ALU.mult), VEC)
            t_xn = VEC.n
            t_xf_cons[bi] = t_xn
            t_rstd_read[bi] = t_xn

            pending_cons.append((ci, bi, xn[bi], t_xn))
            if len(pending_cons) > 1:
                pci, pbi, pxn, ptok = pending_cons.pop(0)
                ctr, tok = consumers(pci, pxn, ptok)
                t_xn_cons_ctr[pbi] = ctr
                t_xn_cons[pbi] = tok
            yield ci

        while pending_cons:
            pci, pbi, pxn, ptok = pending_cons.pop(0)
            ctr, tok = consumers(pci, pxn, ptok)
            t_xn_cons_ctr[pbi] = ctr
            t_xn_cons[pbi] = tok

    sbuf_p1 = nc.sbuf_base
    psum_p1 = nc.psum_base

    wq_sb = sb("wq_sb", [128, 8, DLOC])
    wk_sb = sb("wk_sb", [128, 8, DLOC])
    wv_sb = sb("wv_sb", [128, 8, DLOC])
    for (dst, srcw) in ((wq_sb, wqT), (wk_sb, wkT), (wv_sb, wvT)):
        QG.emit(lambda e, d=dst, s=srcw: e.dma_start(
            out=d[:, :, :], in_=s[:, :].rearrange("(t p) c -> p t c", p=128)), GW, 16)
    gp_w = GW.n

    ps_q = ps("ps_q", [128, 512])
    ps_k = ps("ps_k", [128, 512])
    ps_v = ps("ps_v", [128, 512])
    qkv_state = {"evac": 0, "evac_a": 0}

    def qkv_consumers(ci, xn, t_xn):
        QT.wait(VEC, t_xn)
        if ci == 0:
            QT.wait(GW, gp_w)
        QT.wait(VEC, qkv_state["evac"])
        QT.wait(ACT, qkv_state["evac_a"])
        for (pps, wsb) in ((ps_q, wq_sb), (ps_k, wk_sb)):
            for dt_ in range(2):
                for tt in range(8):
                    QT.emit(lambda e, p=pps, w=wsb, d=dt_, m=tt, b=xn: e.matmul(
                        p[:, d * 256:(d + 1) * 256],
                        w[:, m, d * 128:(d + 1) * 128],
                        b[:, m, :],
                        start=(m == 0), stop=(m == 7)), TEN)
        for s_ in range(2):
            for tt in range(8):
                QT.emit(lambda e, s2=s_, m=tt, b=xn: e.matmul(
                    ps_v[:, s2 * 256:(s2 + 1) * 256],
                    b[:, m, s2 * 128:(s2 + 1) * 128],
                    wv_sb[:, m, :],
                    start=(m == 0), stop=(m == 7)), TEN)
        t_mm = TEN.n
        QA.wait(TEN, t_mm)
        for dt_ in range(2):
            QA.emit(lambda e, d=dt_, c=ci: e.activation(
                qt_[d][:, c * CW:(c + 1) * CW], ps_q[:, d * 256:(d + 1) * 256],
                AF.Identity, bias=bq_sb[:, d:d + 1], scale=1.0), ACT)
            QA.emit(lambda e, d=dt_, c=ci: e.activation(
                kt[d][:, c * CW:(c + 1) * CW], ps_k[:, d * 256:(d + 1) * 256],
                AF.Identity, bias=bk_sb[:, d:d + 1], scale=1.0), ACT)
        qkv_state["evac_a"] = ACT.n
        QV.wait(TEN, t_mm)
        for s_ in range(2):
            QV.emit(lambda e, s2=s_, c=ci: e.tensor_tensor(
                vv[:, 2 * c + s2, :], ps_v[:, s2 * 256:(s2 + 1) * 256],
                bvrep[:, :], ALU.add), VEC)
        qkv_state["evac"] = VEC.n
        return TEN, t_mm

    NCH_B = S // CW

    def xnf_consumers(ci, xn, t_xn):
        c = ci - NCH_B
        for tt in range(8):
            QV.emit(lambda e, m=tt, cc2=c: e.tensor_copy(
                xnf[:, m, cc2 * CW:(cc2 + 1) * CW], xn[:, m, :]), VEC)
        return VEC, VEC.n

    def p1_src(ci):
        if ci < NCH_B:
            return xT[:, ci * CW:(ci + 1) * CW].rearrange("(t p) c -> p t c", p=128)
        c = ci - NCH_B
        return xoT[:, c * CW:(c + 1) * CW].rearrange("(t p) c -> p t c", p=128)

    def p1_consumers(ci, xn, t_xn):
        if ci < NCH_B:
            return qkv_consumers(ci, xn, t_xn)
        return xnf_consumers(ci, xn, t_xn)

    for _ in ln_pipeline(
            "a", p1_src, NCH_B + TOK // CW,
            lambda ci: ci >= NCH_B, p1_consumers):
        pass
    barrier()
    nc.sbuf_base = sbuf_M1
    nc.psum_base = psum_p1
    if not on("p1c"):
        return _truncate()

    ps_smp = [ps(f"ps_smp{i}", [128, 256]) for i in range(2)]
    runmax = sb("runmax", [128, 1], F32)
    colmax = sb("colmax", [128, 1], F32)
    rowback = sb("rowback", [1, 128], F32)
    cval = sb("cval", [1, 1], F32)
    t_red = 0
    for pr in range(2):
        for kb in range(NKCH // 2):
            QT.wait(VEC, t_red)
            for sub in range(2):
                k = 2 * kb + sub
                QT.emit(lambda e, p=pr, kk=k, s_=sub: e.matmul(
                    ps_smp[0][:, s_ * 128:(s_ + 1) * 128],
                    kt[p][0:64, kk * 128:(kk + 1) * 128],
                    qt_[p][0:64, 0:128],
                    start=True, stop=True, tile_position=(0, 0)), TEN)
                QT.emit(lambda e, p=pr, kk=k, s_=sub: e.matmul(
                    ps_smp[1][:, s_ * 128:(s_ + 1) * 128],
                    kt[p][64:128, kk * 128:(kk + 1) * 128],
                    qt_[p][64:128, 0:128],
                    start=True, stop=True, tile_position=(64, 0)), TEN)
            t_mm = TEN.n
            QV.wait(TEN, t_mm)
            for ii in range(2):
                QV.emit(lambda e, i=ii: e.reduce_max(colmax[:, :], ps_smp[i][:, :], AX.X), VEC)
                if pr == 0 and kb == 0 and ii == 0:
                    QV.emit(lambda e: e.tensor_copy(runmax[:, :], colmax[:, :]), VEC)
                else:
                    QV.emit(lambda e: e.tensor_tensor(runmax[:, :], runmax[:, :],
                                                      colmax[:, :], ALU.max), VEC)
            t_red = VEC.n
    QSY.wait(VEC, t_red)
    QSY.emit(lambda e: e.dma_start(out=mxd[:, :], in_=runmax[:, :]), SMX, 16)
    t_st = SMX.n
    QSY.wait(SMX, t_st)
    QSY.emit(lambda e: e.dma_start(
        out=rowback[:, :], in_=mxd[:, :].rearrange("p one -> one p")), SMX, 16)
    t_ld = SMX.n
    QV.wait(SMX, t_ld)
    QV.emit(lambda e: e.reduce_max(cval[:, :], rowback[:, :], AX.X), VEC)
    QV.emit(lambda e: e.tensor_scalar(cval[:, :], cval[:, :], -0.125, None, ALU.mult), VEC)
    t_c = VEC.n
    QT.wait(VEC, t_c)
    QT.emit(lambda e: e.matmul(ps_smp[0][:, 0:1], ones_row[:, :], cval[:, :],
                               start=True, stop=True), TEN)
    QV.wait(TEN)
    QV.emit(lambda e: e.tensor_copy(crep[:, :], ps_smp[0][:, 0:1]), VEC)
    barrier()
    nc.sbuf_base = sbuf_M1
    nc.psum_base = psum_p1
    if not on("p2"):
        return _truncate()

    ot = [sb(f"ot{i}", [128, S]) for i in range(2)]
    sbuf_M2 = nc.sbuf_base
    PBW = 6 * 512
    pbuf = [sb(f"pbuf{i}", [128, PBW]) for i in range(3)]
    acc6 = sb("acc6", [128, PBW])
    of_sb = sb("of_sb", [128, 512], F32)
    re_row = sb("re_row", [1, 512], F32)
    re2_row = sb("re2_row", [1, 512], F32)
    rep_sb = sb("rep_sb", [128, 512], F32)
    s_ps = [ps(f"s_ps{i}", [128, GRP * 512]) for i in range(3)]
    o_ps = ps("o_ps", [128, 1024])

    t_sfree = [0, 0, 0]
    t_pv_pb = [0, 0, 0]
    t_add_pb = [0, 0, 0]
    t_of_copy = 0

    NT = 2 * NKCH
    NG = (NT + GRP - 1) // GRP
    pending_epi = [None]

    for qb in range(NQB):
        for pr in range(2):
            unit_idx = qb * 2 + pr
            grp_mm_tok = [0] * NG
            grp_exp_tok = [0] * NG
            slot_first_add = True

            def tloc(j):
                g = j // GRP
                return g, (j % GRP) * 512, (g // 3) % 3, (g % 3) * 1024

            pv_emitted = 0
            pv_enabled = [pending_epi[0] is None]

            def emit_pv_upto(gdone):
                nonlocal pv_emitted
                if not pv_enabled[0]:
                    return
                while pv_emitted < NKCH:
                    k = pv_emitted
                    jo = 2 * k + 1
                    gmax = jo // GRP
                    if gmax > gdone:
                        break
                    ge, ce, sle, bce = tloc(2 * k)
                    go, co, slo, bco = tloc(jo)
                    QT.wait(ACT, grp_exp_tok[gmax])
                    if k == 0:
                        QT.wait(EPI, unit_idx)
                    stf = (k == 0)
                    spf = (k == NKCH - 1)
                    QT.emit(lambda e, kk=k, p=pr, pi=sle, bc=bce + ce, s_=stf, f=spf: e.matmul(
                        o_ps[0:64, 0:512],
                        vv[:, kk, 2 * p * 64: 2 * p * 64 + 64],
                        pbuf[pi][:, bc:bc + 512],
                        start=s_, stop=f, tile_position=(0, 0)), TEN)
                    QT.emit(lambda e, kk=k, p=pr, pi=slo, bc=bco + co, s_=stf, f=spf: e.matmul(
                        o_ps[64:128, 512:1024],
                        vv[:, kk, (2 * p + 1) * 64: (2 * p + 1) * 64 + 64],
                        pbuf[pi][:, bc:bc + 512],
                        start=s_, stop=f, tile_position=(0, 64)), TEN)
                    t_pv_pb[sle] = TEN.n
                    t_pv_pb[slo] = TEN.n
                    pv_emitted += 1

            for j0 in range(0, NT, 2):
                kch = j0 // 2
                for (j, row0, tp) in ((j0, 0, (0, 0)), (j0 + 1, 64, (64, 0))):
                    g, col, slot, bufcol = tloc(j)
                    par = g % 3
                    if j % GRP == 0:
                        QT.wait(ACT, t_sfree[par])
                    QT.emit(lambda e, p=par, c=col, r=row0, k=kch, t=pr, q=qb, tp_=tp: e.matmul(
                        s_ps[p][:, c:c + 512],
                        kt[t][r:r + 64, k * 128:(k + 1) * 128],
                        qt_[t][r:r + 64, q * QB:(q + 1) * QB],
                        start=True, stop=True, tile_position=tp_), TEN)
                    grp_mm_tok[g] = TEN.n
                    if (j % GRP == GRP - 1) or (j == NT - 1):
                        width = col + 512
                        QA.wait(TEN, grp_mm_tok[g])
                        QA.wait(TEN, t_pv_pb[slot])
                        QA.wait(VEC, t_add_pb[slot])
                        QA.emit(lambda e, p=par, w=width, pi=slot, bc=bufcol: e.activation(
                            pbuf[pi][:, bc:bc + w], s_ps[p][:, 0:w], AF.Exp,
                            bias=crep[:, 0:1], scale=0.125), ACT)
                        grp_exp_tok[g] = ACT.n
                        t_sfree[par] = ACT.n
                        if (g % 3 == 2) or (j == NT - 1):
                            cols = bufcol + width
                            QV.wait(ACT, grp_exp_tok[g])
                            if slot_first_add:
                                QV.emit(lambda e, pi=slot, cl=cols: e.tensor_copy(
                                    acc6[:, 0:cl], pbuf[pi][:, 0:cl]), VEC)
                                slot_first_add = False
                            else:
                                QV.emit(lambda e, pi=slot, cl=cols: e.tensor_tensor(
                                    acc6[:, 0:cl], acc6[:, 0:cl], pbuf[pi][:, 0:cl],
                                    ALU.add), VEC)
                            t_add_pb[slot] = VEC.n
                        emit_pv_upto(g - 1)
                    if j == 2 and pending_epi[0] is not None:
                        pending_epi[0][0]()
                    if j == 8 and pending_epi[0] is not None:
                        pending_epi[0][1]()
                        pending_epi[0] = None
                        pv_enabled[0] = True
                        emit_pv_upto((j // GRP) - 1)

            emit_pv_upto(NG)

            QV.wait(TEN)
            QV.emit(lambda e: e.tensor_copy(of_sb[0:64, :], o_ps[0:64, 0:512]), VEC)
            QV.emit(lambda e: e.tensor_copy(of_sb[64:128, :], o_ps[64:128, 512:1024]), VEC)
            t_of_copy = VEC.n

            def mk_epilogue(pr=pr, qb=qb, t_of=t_of_copy):
                state = {}

                def flush_sums():
                    QT.wait(VEC, t_of)
                    for r6 in range(3):
                        QT.emit(lambda e, r=r6: e.matmul(
                            o_ps[0:1, 0:512], ones_col[:, :],
                            acc6[:, r * 1024:r * 1024 + 512],
                            start=(r == 0), stop=(r == 2), tile_position=(0, 0)), TEN)
                    for r6 in range(3):
                        QT.emit(lambda e, r=r6: e.matmul(
                            o_ps[32:33, 512:1024], ones_col[:, :],
                            acc6[:, r * 1024 + 512:r * 1024 + 1024],
                            start=(r == 0), stop=(r == 2), tile_position=(0, 32)), TEN)
                    t_sums = TEN.n

                    QV.wait(TEN, t_sums)
                    for (row, coff, hloc) in ((0, 0, 2 * pr), (32, 512, 2 * pr + 1)):
                        QV.emit(lambda e, r=row, c=coff: e.reciprocal(
                            re_row[:, :], o_ps[r:r + 1, c:c + 512]), VEC)
                        QV.emit(lambda e, h=hloc: e.tensor_scalar(
                            re2_row[:, :], re_row[:, :], hm_sb[:, h:h + 1], None,
                            ALU.mult), VEC)
                        QV.emit(lambda e, r=row: e.tensor_copy(
                            rb33[r:r + 1, :], re2_row[:, :]), VEC)
                    state["t_rb"] = VEC.n

                def flush_norm():
                    QT.wait(VEC, state["t_rb"])
                    QT.emit(lambda e: e.matmul(o_ps[:, 0:512], sel33[:, :], rb33[:, :],
                                               start=True, stop=True), TEN)
                    t_bcast = TEN.n

                    QV.wait(TEN, t_bcast)
                    epi_tok = QV.emit(lambda e: e.tensor_copy(
                        rep_sb[:, :], o_ps[:, 0:512]), EPI)
                    QV.wait(EPI, epi_tok)
                    QV.emit(lambda e, p=pr, q=qb: e.tensor_tensor(
                        ot[p][:, q * QB:(q + 1) * QB], of_sb[:, :], rep_sb[:, :],
                        ALU.mult), VEC)
                return (flush_sums, flush_norm)

            pending_epi[0] = mk_epilogue()

    pending_epi[0][0]()
    pending_epi[0][1]()
    pending_epi[0] = None
    barrier()
    nc.sbuf_base = sbuf_M2
    nc.psum_base = psum_p1
    if not on("p3"):
        return _truncate()

    wo_sb = sb("wo_sb", [128, 2, D])
    sp_sb = [sb(f"sp_sb{i}", [128, D]) for i in range(2)]
    ps_sp = [ps(f"ps_sp{i}", [128, D]) for i in range(2)]

    QG.emit(lambda e: e.dma_start(
        out=wo_sb[:, :, :], in_=woT[:, :].rearrange("(t p) c -> p t c", p=128)), GW, 16)
    gp_wo = GW.n

    t_sp_dma = [0, 0]
    t_sp_evac = [0, 0]
    t_sp_evac_a = [0, 0]
    QT.wait(GW, gp_wo)
    for tt in range(S // 128):
        bi = tt % 2
        QT.wait(VEC, t_sp_evac[bi])
        QT.wait(ACT, t_sp_evac_a[bi])
        for ch in range(2):
            for half in range(2):
                QT.emit(lambda e, t2=tt, c=ch, h=half, b=bi: e.matmul(
                    ps_sp[b][:, h * 512:(h + 1) * 512],
                    ot[c][:, t2 * 128:(t2 + 1) * 128],
                    wo_sb[:, c, h * 512:(h + 1) * 512],
                    start=(c == 0), stop=(c == 1)), TEN)
        t_mm = TEN.n
        QV.wait(TEN, t_mm)
        QV.wait(SSP[bi], t_sp_dma[bi])
        QV.emit(lambda e, b=bi: e.tensor_copy(sp_sb[b][:, 0:512], ps_sp[b][:, 0:512]), VEC)
        t_ev_v = VEC.n
        QA.wait(TEN, t_mm)
        QA.wait(SSP[bi], t_sp_dma[bi])
        QA.emit(lambda e, b=bi: e.activation(
            sp_sb[b][:, 512:1024], ps_sp[b][:, 512:1024], AF.Identity,
            bias=0.0, scale=1.0), ACT)
        t_ev_a = ACT.n
        t_sp_evac[bi] = t_ev_v
        t_sp_evac_a[bi] = t_ev_a
        QSY.wait(VEC, t_ev_v)
        QSY.wait(ACT, t_ev_a)
        QSY.emit(lambda e, b=bi, t2=tt: e.dma_start(
            out=cc_in[t2 * 128:(t2 + 1) * 128, :], in_=sp_sb[b][:, :]), SSP[bi], 16)
        t_sp_dma[bi] = SSP[bi].n

    if not on("cc"):
        return _truncate()
    barrier()
    nc.sbuf_base = sbuf_M0
    nc.psum_base = psum_p1

    g_sb = sb("g_sb", [128, 16, D])
    sbuf_M5 = nc.sbuf_base
    wgu_sb = [sb(f"wgu_sb{i}", [128, 2, 8, 128]) for i in range(6)]
    wd_all = sb("wd_all", [128, 24, D])
    sg_sb = sb("sg_sb", [128, 512], F32)
    hp = sb("hp", [128, 24, 512])
    ps_g = ps("ps_g", [128, 512])
    ps_u = ps("ps_u", [128, 512])
    ps_d = [ps(f"ps_d{i}", [128, D]) for i in range(2)]

    QG.wait(SSP[0], SSP[0].n)
    QG.wait(SSP[1], SSP[1].n)
    QG.emit(lambda e: e.dma_start(
        out=wd_all[:, :, :], in_=wdT[:, :].rearrange("(t p) c -> p t c", p=128)), GW, 16)
    gp_wd = GW.n
    t_wgu_tok = [0] * 6
    for h in range(6):
        QG.emit(lambda e, b=h: e.dma_start(
            out=wgu_sb[b][:, :, :, :], in_=wguT[b, :, :, :, :]), GF[h], 16)
        t_wgu_tok[h] = GF[h].n
    QG.emit(lambda e: e.collective_compute(
        "ReduceScatter", ALU.add,
        replica_groups=[[0, 1, 2, 3], [4, 5, 6, 7]],
        ins=[cc_in[:, :]], outs=[cc_out[:, :]]), CCS)
    if not on("p5"):
        return _truncate()

    t_wgu_used = [0] * 6
    t_gate_act = 0
    t_hpm_last = 0
    t_hp_used = 0
    t_g_act = [0, 0]

    for tokb in range(4):
        for hc in range(24):
            bi = hc % 6
            if tokb == 0 and hc < 6:
                t_wpair = t_wgu_tok[bi]
            else:
                QG.wait(TEN, t_wgu_used[bi])
                QG.emit(lambda e, b=bi, h=hc: e.dma_start(
                    out=wgu_sb[b][:, :, :, :], in_=wguT[h, :, :, :, :]), GF[bi], 16)
                t_wpair = GF[bi].n

            QT.wait(GF[bi], t_wpair)
            QT.wait(ACT, t_gate_act)
            for m in range(8):
                QT.emit(lambda e, b=bi, mm=m, tb=tokb: e.matmul(
                    ps_g[:, :], wgu_sb[b][:, 0, mm, :],
                    xnf_flat[:, mm, tb * 512:(tb + 1) * 512],
                    start=(mm == 0), stop=(mm == 7)), TEN)
            t_gmm = TEN.n
            QT.wait(VEC, t_hpm_last)
            for m in range(8):
                QT.emit(lambda e, b=bi, mm=m, tb=tokb: e.matmul(
                    ps_u[:, :], wgu_sb[b][:, 1, mm, :],
                    xnf_flat[:, mm, tb * 512:(tb + 1) * 512],
                    start=(mm == 0), stop=(mm == 7)), TEN)
            t_wgu_used[bi] = TEN.n
            t_umm = TEN.n

            QA.wait(TEN, t_gmm)
            QA.wait(VEC, t_hpm_last)
            QA.emit(lambda e: e.activation(sg_sb[:, :], ps_g[:, :], AF.Silu,
                                           bias=0.0, scale=1.0), ACT)
            t_gate_act = ACT.n
            QV.wait(ACT, t_gate_act)
            QV.wait(TEN, t_umm)
            QV.wait(TEN, t_hp_used)
            QV.emit(lambda e, h=hc: e.tensor_tensor(
                hp[:, h, :], sg_sb[:, :], ps_u[:, :], ALU.mult), VEC)
            t_hpm_last = VEC.n

        t_hp_all = VEC.n
        QT.wait(VEC, t_hp_all)
        if tokb == 0:
            QT.wait(GW, gp_wd)
        for st128 in range(4):
            bi = st128 % 2
            QT.wait(ACT, t_g_act[bi])
            for hc in range(24):
                for half in range(2):
                    QT.emit(lambda e, b=bi, h=hc, s_=st128, hf=half: e.matmul(
                        ps_d[b][:, hf * 512:(hf + 1) * 512],
                        hp[:, h, s_ * 128:(s_ + 1) * 128],
                        wd_all[:, h, hf * 512:(hf + 1) * 512],
                        start=(h == 0), stop=(h == 23)), TEN)
            t_dmm = TEN.n
            QA.wait(TEN, t_dmm)
            QA.emit(lambda e, b=bi, tb=tokb, s_=st128: e.activation(
                g_sb[:, tb * 4 + s_, :], ps_d[b][:, :], AF.Silu,
                bias=0.0, scale=DEQ), ACT)
            t_g_act[bi] = ACT.n
        t_hp_used = TEN.n

    t_g_done = max(t_g_act)
    barrier()
    nc.sbuf_base = sbuf_M5
    nc.psum_base = psum_p1
    if not on("p6"):
        return _truncate()

    s_tile = [sb(f"s_tile{i}", [128, D]) for i in range(2)]
    x_tile = [sb(f"x_tile{i}", [128, D], F32) for i in range(2)]
    o_tile = [sb(f"o_tile{i}", [128, D], F32) for i in range(2)]
    c_tmp = [sb(f"c_tmp{i}", [128, D], F32) for i in range(2)]
    d_tmp = [sb(f"d_tmp{i}", [128, D], F32) for i in range(2)]

    QSY.wait(CCS, 1)
    t_used = [0, 0]
    t_osent = [0, 0]
    for tt in range(TOK // 128):
        bi = tt % 2
        QSY.wait(VEC, t_used[bi])
        QSY.emit(lambda e, b=bi, t2=tt: e.dma_start(
            out=s_tile[b][:, :], in_=cc_out[t2 * 128:(t2 + 1) * 128, :]), SS6[bi], 16)
        QSY.emit(lambda e, b=bi, t2=tt: e.dma_start(
            out=x_tile[b][:, :], in_=xrows[t2 * 128:(t2 + 1) * 128, :]), SS6[bi], 16)
        t_ld = SS6[bi].n
        QV.wait(SS6[bi], t_ld)
        QV.wait(SO6[bi], t_osent[bi])
        QV.emit(lambda e, b=bi: e.tensor_tensor(
            c_tmp[b][:, :], s_tile[b][:, :], borep[:, :], ALU.add), VEC)
        QV.emit(lambda e, b=bi, t2=tt: e.tensor_tensor(
            d_tmp[b][:, :], c_tmp[b][:, :], g_sb[:, t2, :], ALU.mult), VEC)
        QV.emit(lambda e, b=bi: e.tensor_tensor(
            o_tile[b][:, :], d_tmp[b][:, :], x_tile[b][:, :], ALU.add), VEC)
        t_used[bi] = VEC.n
        QSY.wait(VEC, t_used[bi])
        QSY.emit(lambda e, b=bi, t2=tt: e.dma_start(
            out=out[t2 * 128:(t2 + 1) * 128, :], in_=o_tile[b][:, :]), SO6[bi], 16)
        t_osent[bi] = SO6[bi].n

    QSY.wait(SO6[0], SO6[0].n)
    QSY.wait(SO6[1], SO6[1].n)
    _ = t_g_done

    return _replay()


_NC_CACHE = [None]


def _get_nc():
    if _NC_CACHE[0] is None:
        _NC_CACHE[0] = build_program()
    return _NC_CACHE[0]


def host_in_maps(L0_attn_x, L0_attn_Wq, L0_attn_bq, L0_attn_Wk, L0_attn_bk,
                 L0_attn_Wv, L0_attn_bv, L0_attn_Wo, L0_attn_bo, L0_attn_head_mask,
                 L0_ffn_w_gate, L0_ffn_w_up, L0_ffn_w_down):
    x = np.asarray(L0_attn_x, dtype=np.float32)
    Wq = np.asarray(L0_attn_Wq, dtype=np.float32)
    Wk = np.asarray(L0_attn_Wk, dtype=np.float32)
    Wv = np.asarray(L0_attn_Wv, dtype=np.float32)
    Wo = np.asarray(L0_attn_Wo, dtype=np.float32)
    bq = np.asarray(L0_attn_bq, dtype=np.float32)
    bk = np.asarray(L0_attn_bk, dtype=np.float32)
    bv = np.asarray(L0_attn_bv, dtype=np.float32)
    bo = np.asarray(L0_attn_bo, dtype=np.float32)
    hm = np.asarray(L0_attn_head_mask, dtype=np.float32)
    wg = np.asarray(L0_ffn_w_gate, dtype=np.int32)
    wu = np.asarray(L0_ffn_w_up, dtype=np.int32)
    wd = np.asarray(L0_ffn_w_down, dtype=np.int32)

    C = np.ascontiguousarray
    wgu_h = np.empty((24, 128, 2, 8, 128), dtype=np.int32)
    wgu_h[:, :, 0] = wg.reshape(24, 128, 8, 128).transpose(0, 3, 2, 1)
    wgu_h[:, :, 1] = wu.reshape(24, 128, 8, 128).transpose(0, 3, 2, 1)
    wdT_h = C(wd.T)
    in_maps = []
    for c in range(8):
        b = c // 4
        g = c % 4
        qt = c % 4
        xb = x[b]
        xTb = C(xb.T)
        in_maps.append({
            "xT": xTb,
            "xoT": C(xb[qt * TOK:(qt + 1) * TOK].T),
            "xrows": C(xb[qt * TOK:(qt + 1) * TOK]),
            "wqT": C(Wq[g * DLOC:(g + 1) * DLOC, :].T),
            "wkT": C(Wk[g * DLOC:(g + 1) * DLOC, :].T),
            "wvT": C(Wv[g * DLOC:(g + 1) * DLOC, :].T),
            "woT": C(Wo[:, g * DLOC:(g + 1) * DLOC].T),
            "bqc": C(bq[g * DLOC:(g + 1) * DLOC].reshape(2, 128).T),
            "bkc": C(bk[g * DLOC:(g + 1) * DLOC].reshape(2, 128).T),
            "bvr": C(bv[g * DLOC:(g + 1) * DLOC].reshape(1, DLOC)),
            "bor": C(bo.reshape(1, D)),
            "hmr": C(hm[g * H_LOC:(g + 1) * H_LOC].reshape(1, H_LOC)),
            "wguT": wgu_h,
            "wdT": wdT_h,
        })
    return in_maps


def kernel(**inputs):
    in_maps = host_in_maps(**inputs)
    nc = _get_nc()
    res = run_bass_kernel_spmd(nc, in_maps, core_ids=list(range(8)))
    outp = np.empty((2, S, D), dtype=np.float32)
    for c in range(8):
        b = c // 4
        qt = c % 4
        outp[b, qt * TOK:(qt + 1) * TOK, :] = res.results[c]["out"]
    return outp


# version 17
# speedup vs baseline: 1.0009x; 1.0009x over previous
import numpy as np
import concourse.bass as bass
import concourse.mybir as mybir
from concourse.bass_utils import run_bass_kernel_spmd

F32 = mybir.dt.float32
BF16 = mybir.dt.bfloat16
I32 = mybir.dt.int32
AF = mybir.ActivationFunctionType
ALU = mybir.AluOpType
AX = mybir.AxisListType

D = 1024
S = 8192
H_LOC = 4
HD = 64
DLOC = H_LOC * HD
TOK = 2048
FH = 3072
CW = 256
QB = 512
NQB = S // QB
NKCH = S // 128
GRP = 3
EPS = 1e-5
DEQ = 1.0 / 127.0


class Ctr:
    def __init__(self, nc, name):
        self.h = nc.alloc_semaphore(name=name)
        self.n = 0


class EngQ:

    def __init__(self, self_ctr=None):
        self.fns = []
        self.self_ctr = self_ctr
        self.last_wait = {}

    def wait(self, ctr, val=None):
        v = ctr.n if val is None else val
        if v > 0 and v > self.last_wait.get(ctr.h.num, 0):
            self.last_wait[ctr.h.num] = v
            self.fns.append(lambda e, s=ctr.h, vv=v: e.wait_ge(s, vv))

    def emit(self, build, ctr=None, inc=1):
        if self.self_ctr is not None:
            prev = self.self_ctr.n
            if prev > 0:
                self.fns.append(lambda e, s=self.self_ctr.h, vv=prev: e.wait_ge(s, vv))
            if ctr is None:
                ctr, inc = self.self_ctr, 1
        if ctr is None:
            self.fns.append(build)
            return None
        ctr.n += inc
        tok = ctr.n

        def fn(e, b=build, s=ctr.h, k=inc):
            b(e).then_inc(s, k)

        self.fns.append(fn)
        return tok


STAGES = ["p0", "p1", "p1c", "p2", "p3", "cc", "p1b", "p5", "p6"]
LIMIT = "p6"


def build_program(limit=None):
    limit = LIMIT if limit is None else limit
    lim_i = STAGES.index(limit)

    def on(stage):
        return STAGES.index(stage) <= lim_i

    nc = bass.Bass("TRN2", target_bir_lowering=False)

    xT = nc.dram_tensor("xT", [D, S], F32, kind="ExternalInput")
    xoT = nc.dram_tensor("xoT", [D, TOK], F32, kind="ExternalInput")
    xrows = nc.dram_tensor("xrows", [TOK, D], F32, kind="ExternalInput")
    wqT = nc.dram_tensor("wqT", [D, DLOC], F32, kind="ExternalInput")
    wkT = nc.dram_tensor("wkT", [D, DLOC], F32, kind="ExternalInput")
    wvT = nc.dram_tensor("wvT", [D, DLOC], F32, kind="ExternalInput")
    woT = nc.dram_tensor("woT", [DLOC, D], F32, kind="ExternalInput")
    bqc = nc.dram_tensor("bqc", [128, 2], F32, kind="ExternalInput")
    bkc = nc.dram_tensor("bkc", [128, 2], F32, kind="ExternalInput")
    bvr = nc.dram_tensor("bvr", [1, DLOC], F32, kind="ExternalInput")
    bor = nc.dram_tensor("bor", [1, D], F32, kind="ExternalInput")
    hmr = nc.dram_tensor("hmr", [1, H_LOC], F32, kind="ExternalInput")
    wguT = nc.dram_tensor("wguT", [24, 128, 2, 8, 128], I32, kind="ExternalInput")
    wdT = nc.dram_tensor("wdT", [FH, D], I32, kind="ExternalInput")
    out = nc.dram_tensor("out", [TOK, D], F32, kind="ExternalOutput")
    cc_in = nc.dram_tensor("cc_in", [S, D], BF16)
    cc_out = nc.dram_tensor("cc_out", [TOK, D], BF16)
    mxd = nc.dram_tensor("mxd", [128, 1], F32)

    TEN = Ctr(nc, "TEN")
    ACT = Ctr(nc, "ACT")
    VEC = Ctr(nc, "VEC")
    POO = Ctr(nc, "POO")
    CCS = Ctr(nc, "CCS")
    PH = Ctr(nc, "PH")
    EPI = Ctr(nc, "EPI")
    SYC = Ctr(nc, "SYC")
    SXF = [Ctr(nc, f"SXF{i}") for i in range(2)]
    GXB = [Ctr(nc, f"GXB{i}") for i in range(2)]
    GW = Ctr(nc, "GW")
    GF = [Ctr(nc, f"GF{i}") for i in range(6)]
    SSP = [Ctr(nc, f"SSP{i}") for i in range(2)]
    SMX = Ctr(nc, "SMX")
    SS6 = [Ctr(nc, f"SS6{i}") for i in range(2)]
    SO6 = [Ctr(nc, f"SO6{i}") for i in range(2)]

    QSY, QT, QG = EngQ(), EngQ(), EngQ()
    QV, QA = EngQ(self_ctr=VEC), EngQ(self_ctr=ACT)
    allq = {"sync": QSY, "tensor": QT, "vector": QV, "scalar": QA, "gpsimd": QG}
    nbar = [0]

    def barrier():
        nbar[0] += 1
        tgt = 5 * nbar[0]
        for q in allq.values():
            q.emit(lambda e: e.nop(), PH)
            q.wait(PH, tgt)

    sb = lambda name, shape, dt=BF16: nc.alloc_sbuf_tensor(name, shape, dt)
    ps = lambda name, shape: nc.alloc_psum_tensor(name, shape, F32)

    def _replay():
        with nc.Block() as block:
            @block.sync
            def _(e):
                for fn in QSY.fns:
                    fn(e)

            @block.tensor
            def _(e):
                for fn in QT.fns:
                    fn(e)

            @block.vector
            def _(e):
                for fn in QV.fns:
                    fn(e)

            @block.scalar
            def _(e):
                for fn in QA.fns:
                    fn(e)

            @block.gpsimd
            def _(e):
                for fn in QG.fns:
                    fn(e)
        return nc

    def _truncate():
        od = nc.alloc_sbuf_tensor("o_dummy", [128, D], F32)
        tz = QV.emit(lambda e: e.memset(od[:, :], 0.0), VEC)
        QSY.wait(VEC, tz)
        for tt in range(TOK // 128):
            QSY.emit(lambda e, t2=tt: e.dma_start(
                out=out[t2 * 128:(t2 + 1) * 128, :], in_=od[:, :]), SO6[0], 16)
        QSY.wait(SO6[0], SO6[0].n)
        return _replay()

    ones_mean = sb("ones_mean", [128, 128])
    ones_col = sb("ones_col", [128, 1])
    ones_row = sb("ones_row", [1, 128], F32)
    sel33 = sb("sel33", [33, 128], F32)
    rb33 = sb("rb33", [33, 512], F32)
    bq_sb = sb("bq_sb", [128, 2], F32)
    bk_sb = sb("bk_sb", [128, 2], F32)
    bvrow = sb("bvrow", [1, DLOC], F32)
    borow = sb("borow", [1, D], F32)
    hm_sb = sb("hm_sb", [1, H_LOC], F32)
    bvrep = sb("bvrep", [128, DLOC], F32)
    borep = sb("borep", [128, D], F32)
    crep = sb("crep", [128, 1], F32)
    xnf = sb("xnf", [128, 8, TOK])
    xnf_flat = xnf
    sbuf_M0 = nc.sbuf_base
    kt = [sb(f"kt{i}", [128, S]) for i in range(2)]
    qt_ = [sb(f"qt{i}", [128, S]) for i in range(2)]
    vv = sb("vv", [128, NKCH, DLOC])
    sbuf_M1 = nc.sbuf_base

    QV.emit(lambda e: e.memset(ones_mean[:, :], 1.0 / 1024.0))
    QV.emit(lambda e: e.memset(ones_col[:, :], 1.0))
    QV.emit(lambda e: e.memset(ones_row[:, :], 1.0))
    QV.emit(lambda e: e.memset(sel33[:, :], 0.0))
    QV.emit(lambda e: e.memset(sel33[0:1, 0:64], 1.0))
    QV.emit(lambda e: e.memset(sel33[32:33, 64:128], 1.0))
    QV.emit(lambda e: e.memset(rb33[:, :], 0.0))
    QV.emit(lambda e: e.memset(crep[:, :], 0.0), VEC)
    v_consts = VEC.n

    for (dst, src) in ((bq_sb, bqc), (bk_sb, bkc), (bvrow, bvr),
                       (borow, bor), (hm_sb, hmr)):
        QSY.emit(lambda e, d=dst, s=src: e.dma_start(out=d[:, :], in_=s[:, :]), SYC, 16)
    sy_consts = SYC.n

    psum_base0 = nc.psum_base

    ps_rep = ps("ps_rep", [128, 512])
    QT.wait(SYC, sy_consts)
    QT.wait(VEC, v_consts)
    QT.emit(lambda e: e.matmul(ps_rep[:, 0:DLOC], ones_row[:, :], bvrow[:, :],
                               start=True, stop=True), TEN)
    QV.wait(TEN)
    t = QV.emit(lambda e: e.tensor_copy(bvrep[:, :], ps_rep[:, 0:DLOC]), VEC)
    for half in range(2):
        QT.wait(VEC, t)
        QT.emit(lambda e, h=half: e.matmul(ps_rep[:, :], ones_row[:, :],
                                           borow[:, h * 512:(h + 1) * 512],
                                           start=True, stop=True), TEN)
        QV.wait(TEN)
        t = QV.emit(lambda e, h=half: e.tensor_copy(borep[:, h * 512:(h + 1) * 512],
                                                    ps_rep[:, :]), VEC)
    barrier()
    nc.psum_base = psum_base0
    if not on("p1"):
        return _truncate()

    def ln_pipeline(tag, src_dram, n_chunks, rstd_bias, consumers):
        xf = [sb(f"xf_{tag}{i}", [128, 8, CW], F32) for i in range(2)]
        xb = [sb(f"xb_{tag}{i}", [128, 8, CW]) for i in range(2)]
        xq = [sb(f"xq_{tag}{i}", [128, 8, CW]) for i in range(2)]
        xn = [sb(f"xn_{tag}{i}", [128, 8, CW]) for i in range(2)]
        m_sb = [sb(f"m_{tag}{i}", [128, CW], F32) for i in range(2)]
        msq = [sb(f"msq_{tag}{i}", [128, CW], F32) for i in range(2)]
        var = [sb(f"var_{tag}{i}", [128, CW], F32) for i in range(2)]
        lnv = [sb(f"lnv_{tag}{i}", [128, CW], F32) for i in range(2)]
        rstd = [sb(f"rstd_{tag}{i}", [128, CW], F32) for i in range(2)]
        tmp = sb(f"tmp_{tag}", [128, 4, CW], F32)
        eps_col = sb(f"eps_{tag}", [128, 1], F32)
        rb_col = [sb(f"rbias_{tag}{i}", [128, 1], F32) for i in range(2)]
        QV.emit(lambda e: e.memset(eps_col[:, :], float(EPS)))
        QV.emit(lambda e: e.memset(rb_col[0][:, :], 0.0))
        t0m = QV.emit(lambda e: e.memset(rb_col[1][:, :], float(np.log(DEQ))), VEC)
        QA.wait(VEC, t0m)
        ps_m = [ps(f"ps_m_{tag}{i}", [128, CW]) for i in range(2)]
        ps_s = [ps(f"ps_s_{tag}{i}", [128, CW]) for i in range(2)]

        t_xf_cons = [0, 0]
        t_xb_cons_t = [0, 0]
        t_xb_cons_p = [0, 0]
        t_xq_cons = [0, 0]
        t_xn_cons = [0, 0]
        t_xn_cons_ctr = [None, None]
        t_stats_read2 = [0, 0]
        t_rstd_read = [0, 0]
        t_act_rows = [0, 0]
        pending_cons = []

        for ci in range(n_chunks):
            bi = ci % 2
            src = src_dram(ci)
            QSY.wait(VEC, t_xf_cons[bi])
            QSY.emit(lambda e, a=src, d=xf[bi]: e.dma_start(out=d[:, :, :], in_=a), SXF[bi], 16)
            t_xf = SXF[bi].n
            QG.wait(TEN, t_xb_cons_t[bi])
            QG.wait(ACT, t_xb_cons_p[bi])
            QG.emit(lambda e, a=src, d=xb[bi]: e.dma_start(out=d[:, :, :], in_=a), GXB[bi], 16)
            t_xb = GXB[bi].n

            QA.wait(GXB[bi], t_xb)
            QA.wait(TEN, t_xq_cons[bi])
            QA.emit(lambda e, b=bi: e.activation(
                xq[b][:, :, :], xb[b][:, :, :], AF.Square, bias=0.0, scale=1.0), ACT)
            t_xsq = ACT.n

            QT.wait(GXB[bi], t_xb)
            QT.wait(ACT, t_xsq)
            QT.wait(VEC, t_stats_read2[bi])
            for tt in range(8):
                QT.emit(lambda e, b=bi, m=tt: e.matmul(
                    ps_m[b][:, :], ones_mean[:, :], xb[b][:, m, :],
                    start=(m == 0), stop=(m == 7)), TEN)
            t_xb_cons_save = TEN.n
            for tt in range(8):
                QT.emit(lambda e, b=bi, m=tt: e.matmul(
                    ps_s[b][:, :], ones_mean[:, :], xq[b][:, m, :],
                    start=(m == 0), stop=(m == 7)), TEN)
            t_xb_cons_t[bi] = t_xb_cons_save
            t_xb_cons_p[bi] = t_xsq
            t_xq_cons[bi] = TEN.n
            t_stats = TEN.n

            QV.wait(TEN, t_stats)
            QV.wait(ACT, t_act_rows[bi])
            QV.emit(lambda e, b=bi: e.tensor_copy(m_sb[b][:, :], ps_m[b][:, :]), VEC)
            QV.emit(lambda e, b=bi: e.tensor_tensor(msq[b][:, :], m_sb[b][:, :], m_sb[b][:, :], ALU.mult), VEC)
            QV.emit(lambda e, b=bi: e.tensor_tensor(var[b][:, :], ps_s[b][:, :], msq[b][:, :], ALU.subtract), VEC)
            t_stats_read2[bi] = VEC.n
            QA.wait(VEC, t_stats_read2[bi])
            QA.wait(VEC, t_rstd_read[bi])
            QA.emit(lambda e, b=bi: e.activation(lnv[b][:, :], var[b][:, :], AF.Ln,
                                                 bias=eps_col[:, 0:1], scale=1.0), ACT)
            QA.emit(lambda e, b=bi, rc=rb_col[1 if rstd_bias(ci) else 0]: e.activation(
                rstd[b][:, :], lnv[b][:, :], AF.Exp,
                bias=rc[:, 0:1], scale=-0.5), ACT)
            t_act_rows[bi] = ACT.n

            QV.wait(SXF[bi], t_xf)
            QV.wait(ACT, t_act_rows[bi])
            if t_xn_cons_ctr[bi] is not None:
                QV.wait(t_xn_cons_ctr[bi], t_xn_cons[bi])
            for hh in range(2):
                QV.emit(lambda e, b=bi, h=hh: e.tensor_tensor(
                    tmp[:, :, :], xf[b][:, 4 * h:4 * h + 4, :],
                    m_sb[b][:, None, :].broadcast_to([128, 4, CW]), ALU.subtract), VEC)
                QV.emit(lambda e, b=bi, h=hh: e.tensor_tensor(
                    xn[b][:, 4 * h:4 * h + 4, :], tmp[:, :, :],
                    rstd[b][:, None, :].broadcast_to([128, 4, CW]), ALU.mult), VEC)
            t_xn = VEC.n
            t_xf_cons[bi] = t_xn
            t_rstd_read[bi] = t_xn

            pending_cons.append((ci, bi, xn[bi], t_xn))
            if len(pending_cons) > 1:
                pci, pbi, pxn, ptok = pending_cons.pop(0)
                ctr, tok = consumers(pci, pxn, ptok)
                t_xn_cons_ctr[pbi] = ctr
                t_xn_cons[pbi] = tok
            yield ci

        while pending_cons:
            pci, pbi, pxn, ptok = pending_cons.pop(0)
            ctr, tok = consumers(pci, pxn, ptok)
            t_xn_cons_ctr[pbi] = ctr
            t_xn_cons[pbi] = tok

    sbuf_p1 = nc.sbuf_base
    psum_p1 = nc.psum_base

    wq_sb = sb("wq_sb", [128, 8, DLOC])
    wk_sb = sb("wk_sb", [128, 8, DLOC])
    wv_sb = sb("wv_sb", [128, 8, DLOC])
    for (dst, srcw) in ((wq_sb, wqT), (wk_sb, wkT), (wv_sb, wvT)):
        QG.emit(lambda e, d=dst, s=srcw: e.dma_start(
            out=d[:, :, :], in_=s[:, :].rearrange("(t p) c -> p t c", p=128)), GW, 16)
    gp_w = GW.n

    ps_q = ps("ps_q", [128, 512])
    ps_k = ps("ps_k", [128, 512])
    ps_v = ps("ps_v", [128, 512])
    qkv_state = {"evac": 0, "evac_a": 0}

    def qkv_consumers(ci, xn, t_xn):
        QT.wait(VEC, t_xn)
        if ci == 0:
            QT.wait(GW, gp_w)
        QT.wait(VEC, qkv_state["evac"])
        QT.wait(ACT, qkv_state["evac_a"])
        for (pps, wsb) in ((ps_q, wq_sb), (ps_k, wk_sb)):
            for dt_ in range(2):
                for tt in range(8):
                    QT.emit(lambda e, p=pps, w=wsb, d=dt_, m=tt, b=xn: e.matmul(
                        p[:, d * 256:(d + 1) * 256],
                        w[:, m, d * 128:(d + 1) * 128],
                        b[:, m, :],
                        start=(m == 0), stop=(m == 7)), TEN)
        for s_ in range(2):
            for tt in range(8):
                QT.emit(lambda e, s2=s_, m=tt, b=xn: e.matmul(
                    ps_v[:, s2 * 256:(s2 + 1) * 256],
                    b[:, m, s2 * 128:(s2 + 1) * 128],
                    wv_sb[:, m, :],
                    start=(m == 0), stop=(m == 7)), TEN)
        t_mm = TEN.n
        QA.wait(TEN, t_mm)
        for dt_ in range(2):
            QA.emit(lambda e, d=dt_, c=ci: e.activation(
                qt_[d][:, c * CW:(c + 1) * CW], ps_q[:, d * 256:(d + 1) * 256],
                AF.Identity, bias=bq_sb[:, d:d + 1], scale=1.0), ACT)
            QA.emit(lambda e, d=dt_, c=ci: e.activation(
                kt[d][:, c * CW:(c + 1) * CW], ps_k[:, d * 256:(d + 1) * 256],
                AF.Identity, bias=bk_sb[:, d:d + 1], scale=1.0), ACT)
        qkv_state["evac_a"] = ACT.n
        QV.wait(TEN, t_mm)
        for s_ in range(2):
            QV.emit(lambda e, s2=s_, c=ci: e.tensor_tensor(
                vv[:, 2 * c + s2, :], ps_v[:, s2 * 256:(s2 + 1) * 256],
                bvrep[:, :], ALU.add), VEC)
        qkv_state["evac"] = VEC.n
        return TEN, t_mm

    NCH_B = S // CW

    def xnf_consumers(ci, xn, t_xn):
        c = ci - NCH_B
        for tt in range(8):
            QV.emit(lambda e, m=tt, cc2=c: e.tensor_copy(
                xnf[:, m, cc2 * CW:(cc2 + 1) * CW], xn[:, m, :]), VEC)
        return VEC, VEC.n

    def p1_src(ci):
        if ci < NCH_B:
            return xT[:, ci * CW:(ci + 1) * CW].rearrange("(t p) c -> p t c", p=128)
        c = ci - NCH_B
        return xoT[:, c * CW:(c + 1) * CW].rearrange("(t p) c -> p t c", p=128)

    def p1_consumers(ci, xn, t_xn):
        if ci < NCH_B:
            return qkv_consumers(ci, xn, t_xn)
        return xnf_consumers(ci, xn, t_xn)

    for _ in ln_pipeline(
            "a", p1_src, NCH_B + TOK // CW,
            lambda ci: ci >= NCH_B, p1_consumers):
        pass
    barrier()
    nc.sbuf_base = sbuf_M1
    nc.psum_base = psum_p1
    if not on("p1c"):
        return _truncate()

    ps_smp = [ps(f"ps_smp{i}", [128, 256]) for i in range(2)]
    runmax = sb("runmax", [128, 1], F32)
    colmax = sb("colmax", [128, 1], F32)
    rowback = sb("rowback", [1, 128], F32)
    cval = sb("cval", [1, 1], F32)
    t_red = 0
    for pr in range(2):
        for kb in range(NKCH // 2):
            QT.wait(VEC, t_red)
            for sub in range(2):
                k = 2 * kb + sub
                QT.emit(lambda e, p=pr, kk=k, s_=sub: e.matmul(
                    ps_smp[0][:, s_ * 128:(s_ + 1) * 128],
                    kt[p][0:64, kk * 128:(kk + 1) * 128],
                    qt_[p][0:64, 0:128],
                    start=True, stop=True, tile_position=(0, 0)), TEN)
                QT.emit(lambda e, p=pr, kk=k, s_=sub: e.matmul(
                    ps_smp[1][:, s_ * 128:(s_ + 1) * 128],
                    kt[p][64:128, kk * 128:(kk + 1) * 128],
                    qt_[p][64:128, 0:128],
                    start=True, stop=True, tile_position=(64, 0)), TEN)
            t_mm = TEN.n
            QV.wait(TEN, t_mm)
            for ii in range(2):
                QV.emit(lambda e, i=ii: e.reduce_max(colmax[:, :], ps_smp[i][:, :], AX.X), VEC)
                if pr == 0 and kb == 0 and ii == 0:
                    QV.emit(lambda e: e.tensor_copy(runmax[:, :], colmax[:, :]), VEC)
                else:
                    QV.emit(lambda e: e.tensor_tensor(runmax[:, :], runmax[:, :],
                                                      colmax[:, :], ALU.max), VEC)
            t_red = VEC.n
    QSY.wait(VEC, t_red)
    QSY.emit(lambda e: e.dma_start(out=mxd[:, :], in_=runmax[:, :]), SMX, 16)
    t_st = SMX.n
    QSY.wait(SMX, t_st)
    QSY.emit(lambda e: e.dma_start(
        out=rowback[:, :], in_=mxd[:, :].rearrange("p one -> one p")), SMX, 16)
    t_ld = SMX.n
    QV.wait(SMX, t_ld)
    QV.emit(lambda e: e.reduce_max(cval[:, :], rowback[:, :], AX.X), VEC)
    QV.emit(lambda e: e.tensor_scalar(cval[:, :], cval[:, :], -0.125, None, ALU.mult), VEC)
    t_c = VEC.n
    QT.wait(VEC, t_c)
    QT.emit(lambda e: e.matmul(ps_smp[0][:, 0:1], ones_row[:, :], cval[:, :],
                               start=True, stop=True), TEN)
    QV.wait(TEN)
    QV.emit(lambda e: e.tensor_copy(crep[:, :], ps_smp[0][:, 0:1]), VEC)
    barrier()
    nc.sbuf_base = sbuf_M1
    nc.psum_base = psum_p1
    if not on("p2"):
        return _truncate()

    ot = [sb(f"ot{i}", [128, S]) for i in range(2)]
    sbuf_M2 = nc.sbuf_base
    PBW = 6 * 512
    pbuf = [sb(f"pbuf{i}", [128, PBW]) for i in range(3)]
    acc6 = sb("acc6", [128, PBW])
    of_sb = sb("of_sb", [128, 512], F32)
    re_row = sb("re_row", [1, 512], F32)
    re2_row = sb("re2_row", [1, 512], F32)
    rep_sb = sb("rep_sb", [128, 512], F32)
    s_ps = [ps(f"s_ps{i}", [128, GRP * 512]) for i in range(2)]
    o_ps = ps("o_ps", [128, 1024])

    t_sfree_a = [0, 0]
    t_sfree_b = [0, 0]
    t_pv_pb = [0, 0, 0]
    t_add_pb = [0, 0, 0]
    t_of_copy = 0

    NT = 2 * NKCH
    NG = (NT + GRP - 1) // GRP
    pending_epi = [None]

    for qb in range(NQB):
        for pr in range(2):
            unit_idx = qb * 2 + pr
            grp_mm_tok = [0] * NG
            grp_exp_tok = [0] * NG
            slot_first_add = True

            def tloc(j):
                g = j // GRP
                return g, (j % GRP) * 512, (g // 2) % 3, (g % 2) * 1536

            pv_emitted = 0
            pv_enabled = [pending_epi[0] is None]

            def emit_pv_upto(gdone):
                nonlocal pv_emitted
                if not pv_enabled[0]:
                    return
                while pv_emitted < NKCH:
                    k = pv_emitted
                    jo = 2 * k + 1
                    gmax = jo // GRP
                    if gmax > gdone:
                        break
                    ge, ce, sle, bce = tloc(2 * k)
                    go, co, slo, bco = tloc(jo)
                    QT.wait(ACT, grp_exp_tok[gmax])
                    if k == 0:
                        QT.wait(EPI, unit_idx)
                    stf = (k == 0)
                    spf = (k == NKCH - 1)
                    QT.emit(lambda e, kk=k, p=pr, pi=sle, bc=bce + ce, s_=stf, f=spf: e.matmul(
                        o_ps[0:64, 0:512],
                        vv[:, kk, 2 * p * 64: 2 * p * 64 + 64],
                        pbuf[pi][:, bc:bc + 512],
                        start=s_, stop=f, tile_position=(0, 0)), TEN)
                    QT.emit(lambda e, kk=k, p=pr, pi=slo, bc=bco + co, s_=stf, f=spf: e.matmul(
                        o_ps[64:128, 512:1024],
                        vv[:, kk, (2 * p + 1) * 64: (2 * p + 1) * 64 + 64],
                        pbuf[pi][:, bc:bc + 512],
                        start=s_, stop=f, tile_position=(0, 64)), TEN)
                    t_pv_pb[sle] = TEN.n
                    t_pv_pb[slo] = TEN.n
                    pv_emitted += 1

            for j0 in range(0, NT, 2):
                kch = j0 // 2
                for (j, row0, tp) in ((j0, 0, (0, 0)), (j0 + 1, 64, (64, 0))):
                    g, col, slot, bufcol = tloc(j)
                    par = g % 2
                    if j % GRP == 0:
                        QT.wait(ACT, t_sfree_a[par])
                    elif j % GRP == 1:
                        QT.wait(ACT, t_sfree_b[par])
                    QT.emit(lambda e, p=par, c=col, r=row0, k=kch, t=pr, q=qb, tp_=tp: e.matmul(
                        s_ps[p][:, c:c + 512],
                        kt[t][r:r + 64, k * 128:(k + 1) * 128],
                        qt_[t][r:r + 64, q * QB:(q + 1) * QB],
                        start=True, stop=True, tile_position=tp_), TEN)
                    grp_mm_tok[g] = TEN.n
                    if j % GRP == 0:
                        QA.wait(TEN, grp_mm_tok[g])
                        QA.wait(TEN, t_pv_pb[slot])
                        QA.wait(VEC, t_add_pb[slot])
                        QA.emit(lambda e, p=par, pi=slot, bc=bufcol: e.activation(
                            pbuf[pi][:, bc:bc + 512], s_ps[p][:, 0:512], AF.Exp,
                            bias=crep[:, 0:1], scale=0.125), ACT)
                        t_sfree_a[par] = ACT.n
                    if (j % GRP == GRP - 1) or (j == NT - 1):
                        width = col + 512
                        QA.wait(TEN, grp_mm_tok[g])
                        QA.wait(TEN, t_pv_pb[slot])
                        QA.wait(VEC, t_add_pb[slot])
                        QA.emit(lambda e, p=par, w=width, pi=slot, bc=bufcol: e.activation(
                            pbuf[pi][:, bc + 512:bc + w], s_ps[p][:, 512:w], AF.Exp,
                            bias=crep[:, 0:1], scale=0.125), ACT)
                        grp_exp_tok[g] = ACT.n
                        t_sfree_b[par] = ACT.n
                        if (g % 2 == 1) or (j == NT - 1):
                            cols = bufcol + width
                            QV.wait(ACT, grp_exp_tok[g])
                            if slot_first_add:
                                QV.emit(lambda e, pi=slot, cl=cols: e.tensor_copy(
                                    acc6[:, 0:cl], pbuf[pi][:, 0:cl]), VEC)
                                slot_first_add = False
                            else:
                                QV.emit(lambda e, pi=slot, cl=cols: e.tensor_tensor(
                                    acc6[:, 0:cl], acc6[:, 0:cl], pbuf[pi][:, 0:cl],
                                    ALU.add), VEC)
                            t_add_pb[slot] = VEC.n
                        emit_pv_upto(g - 1)
                    if j == 2 and pending_epi[0] is not None:
                        pending_epi[0][0]()
                    if j == 8 and pending_epi[0] is not None:
                        pending_epi[0][1]()
                        pending_epi[0] = None
                        pv_enabled[0] = True
                        emit_pv_upto((j // GRP) - 1)

            emit_pv_upto(NG)

            QV.wait(TEN)
            QV.emit(lambda e: e.tensor_copy(of_sb[0:64, :], o_ps[0:64, 0:512]), VEC)
            QV.emit(lambda e: e.tensor_copy(of_sb[64:128, :], o_ps[64:128, 512:1024]), VEC)
            t_of_copy = VEC.n

            def mk_epilogue(pr=pr, qb=qb, t_of=t_of_copy):
                state = {}

                def flush_sums():
                    QT.wait(VEC, t_of)
                    for r6 in range(3):
                        QT.emit(lambda e, r=r6: e.matmul(
                            o_ps[0:1, 0:512], ones_col[:, :],
                            acc6[:, r * 1024:r * 1024 + 512],
                            start=(r == 0), stop=(r == 2), tile_position=(0, 0)), TEN)
                    for r6 in range(3):
                        QT.emit(lambda e, r=r6: e.matmul(
                            o_ps[32:33, 512:1024], ones_col[:, :],
                            acc6[:, r * 1024 + 512:r * 1024 + 1024],
                            start=(r == 0), stop=(r == 2), tile_position=(0, 32)), TEN)
                    t_sums = TEN.n

                    QV.wait(TEN, t_sums)
                    for (row, coff, hloc) in ((0, 0, 2 * pr), (32, 512, 2 * pr + 1)):
                        QV.emit(lambda e, r=row, c=coff: e.reciprocal(
                            re_row[:, :], o_ps[r:r + 1, c:c + 512]), VEC)
                        QV.emit(lambda e, h=hloc: e.tensor_scalar(
                            re2_row[:, :], re_row[:, :], hm_sb[:, h:h + 1], None,
                            ALU.mult), VEC)
                        QV.emit(lambda e, r=row: e.tensor_copy(
                            rb33[r:r + 1, :], re2_row[:, :]), VEC)
                    state["t_rb"] = VEC.n

                def flush_norm():
                    QT.wait(VEC, state["t_rb"])
                    QT.emit(lambda e: e.matmul(o_ps[:, 0:512], sel33[:, :], rb33[:, :],
                                               start=True, stop=True), TEN)
                    t_bcast = TEN.n

                    QV.wait(TEN, t_bcast)
                    epi_tok = QV.emit(lambda e: e.tensor_copy(
                        rep_sb[:, :], o_ps[:, 0:512]), EPI)
                    QV.wait(EPI, epi_tok)
                    QV.emit(lambda e, p=pr, q=qb: e.tensor_tensor(
                        ot[p][:, q * QB:(q + 1) * QB], of_sb[:, :], rep_sb[:, :],
                        ALU.mult), VEC)
                return (flush_sums, flush_norm)

            pending_epi[0] = mk_epilogue()

    pending_epi[0][0]()
    pending_epi[0][1]()
    pending_epi[0] = None
    barrier()
    nc.sbuf_base = sbuf_M2
    nc.psum_base = psum_p1
    if not on("p3"):
        return _truncate()

    wo_sb = sb("wo_sb", [128, 2, D])
    sp_sb = [sb(f"sp_sb{i}", [128, D]) for i in range(2)]
    ps_sp = [ps(f"ps_sp{i}", [128, D]) for i in range(2)]

    QG.emit(lambda e: e.dma_start(
        out=wo_sb[:, :, :], in_=woT[:, :].rearrange("(t p) c -> p t c", p=128)), GW, 16)
    gp_wo = GW.n

    t_sp_dma = [0, 0]
    t_sp_evac = [0, 0]
    t_sp_evac_a = [0, 0]
    QT.wait(GW, gp_wo)
    for tt in range(S // 128):
        bi = tt % 2
        QT.wait(VEC, t_sp_evac[bi])
        QT.wait(ACT, t_sp_evac_a[bi])
        for ch in range(2):
            for half in range(2):
                QT.emit(lambda e, t2=tt, c=ch, h=half, b=bi: e.matmul(
                    ps_sp[b][:, h * 512:(h + 1) * 512],
                    ot[c][:, t2 * 128:(t2 + 1) * 128],
                    wo_sb[:, c, h * 512:(h + 1) * 512],
                    start=(c == 0), stop=(c == 1)), TEN)
        t_mm = TEN.n
        QV.wait(TEN, t_mm)
        QV.wait(SSP[bi], t_sp_dma[bi])
        QV.emit(lambda e, b=bi: e.tensor_copy(sp_sb[b][:, 0:512], ps_sp[b][:, 0:512]), VEC)
        t_ev_v = VEC.n
        QA.wait(TEN, t_mm)
        QA.wait(SSP[bi], t_sp_dma[bi])
        QA.emit(lambda e, b=bi: e.activation(
            sp_sb[b][:, 512:1024], ps_sp[b][:, 512:1024], AF.Identity,
            bias=0.0, scale=1.0), ACT)
        t_ev_a = ACT.n
        t_sp_evac[bi] = t_ev_v
        t_sp_evac_a[bi] = t_ev_a
        QSY.wait(VEC, t_ev_v)
        QSY.wait(ACT, t_ev_a)
        QSY.emit(lambda e, b=bi, t2=tt: e.dma_start(
            out=cc_in[t2 * 128:(t2 + 1) * 128, :], in_=sp_sb[b][:, :]), SSP[bi], 16)
        t_sp_dma[bi] = SSP[bi].n

    if not on("cc"):
        return _truncate()
    barrier()
    nc.sbuf_base = sbuf_M0
    nc.psum_base = psum_p1

    g_sb = sb("g_sb", [128, 16, D])
    sbuf_M5 = nc.sbuf_base
    wgu_sb = [sb(f"wgu_sb{i}", [128, 2, 8, 128]) for i in range(6)]
    wd_all = sb("wd_all", [128, 24, D])
    sg_sb = sb("sg_sb", [128, 512], F32)
    hp = sb("hp", [128, 24, 512])
    ps_g = ps("ps_g", [128, 512])
    ps_u = ps("ps_u", [128, 512])
    ps_d = [ps(f"ps_d{i}", [128, D]) for i in range(2)]

    QG.wait(SSP[0], SSP[0].n)
    QG.wait(SSP[1], SSP[1].n)
    QG.emit(lambda e: e.dma_start(
        out=wd_all[:, :, :], in_=wdT[:, :].rearrange("(t p) c -> p t c", p=128)), GW, 16)
    gp_wd = GW.n
    t_wgu_tok = [0] * 6
    for h in range(6):
        QG.emit(lambda e, b=h: e.dma_start(
            out=wgu_sb[b][:, :, :, :], in_=wguT[b, :, :, :, :]), GF[h], 16)
        t_wgu_tok[h] = GF[h].n
    QG.emit(lambda e: e.collective_compute(
        "ReduceScatter", ALU.add,
        replica_groups=[[0, 1, 2, 3], [4, 5, 6, 7]],
        ins=[cc_in[:, :]], outs=[cc_out[:, :]]), CCS)
    if not on("p5"):
        return _truncate()

    t_wgu_used = [0] * 6
    t_gate_act = 0
    t_hpm_last = 0
    t_hp_used = 0
    t_g_act = [0, 0]

    for tokb in range(4):
        for hc in range(24):
            bi = hc % 6
            if tokb == 0 and hc < 6:
                t_wpair = t_wgu_tok[bi]
            else:
                QG.wait(TEN, t_wgu_used[bi])
                QG.emit(lambda e, b=bi, h=hc: e.dma_start(
                    out=wgu_sb[b][:, :, :, :], in_=wguT[h, :, :, :, :]), GF[bi], 16)
                t_wpair = GF[bi].n

            QT.wait(GF[bi], t_wpair)
            QT.wait(ACT, t_gate_act)
            for m in range(8):
                QT.emit(lambda e, b=bi, mm=m, tb=tokb: e.matmul(
                    ps_g[:, :], wgu_sb[b][:, 0, mm, :],
                    xnf_flat[:, mm, tb * 512:(tb + 1) * 512],
                    start=(mm == 0), stop=(mm == 7)), TEN)
            t_gmm = TEN.n
            QT.wait(VEC, t_hpm_last)
            for m in range(8):
                QT.emit(lambda e, b=bi, mm=m, tb=tokb: e.matmul(
                    ps_u[:, :], wgu_sb[b][:, 1, mm, :],
                    xnf_flat[:, mm, tb * 512:(tb + 1) * 512],
                    start=(mm == 0), stop=(mm == 7)), TEN)
            t_wgu_used[bi] = TEN.n
            t_umm = TEN.n

            QA.wait(TEN, t_gmm)
            QA.wait(VEC, t_hpm_last)
            QA.emit(lambda e: e.activation(sg_sb[:, :], ps_g[:, :], AF.Silu,
                                           bias=0.0, scale=1.0), ACT)
            t_gate_act = ACT.n
            QV.wait(ACT, t_gate_act)
            QV.wait(TEN, t_umm)
            QV.wait(TEN, t_hp_used)
            QV.emit(lambda e, h=hc: e.tensor_tensor(
                hp[:, h, :], sg_sb[:, :], ps_u[:, :], ALU.mult), VEC)
            t_hpm_last = VEC.n

        t_hp_all = VEC.n
        QT.wait(VEC, t_hp_all)
        if tokb == 0:
            QT.wait(GW, gp_wd)
        for st128 in range(4):
            bi = st128 % 2
            QT.wait(ACT, t_g_act[bi])
            for hc in range(24):
                for half in range(2):
                    QT.emit(lambda e, b=bi, h=hc, s_=st128, hf=half: e.matmul(
                        ps_d[b][:, hf * 512:(hf + 1) * 512],
                        hp[:, h, s_ * 128:(s_ + 1) * 128],
                        wd_all[:, h, hf * 512:(hf + 1) * 512],
                        start=(h == 0), stop=(h == 23)), TEN)
            t_dmm = TEN.n
            QA.wait(TEN, t_dmm)
            QA.emit(lambda e, b=bi, tb=tokb, s_=st128: e.activation(
                g_sb[:, tb * 4 + s_, :], ps_d[b][:, :], AF.Silu,
                bias=0.0, scale=DEQ), ACT)
            t_g_act[bi] = ACT.n
        t_hp_used = TEN.n

    t_g_done = max(t_g_act)
    barrier()
    nc.sbuf_base = sbuf_M5
    nc.psum_base = psum_p1
    if not on("p6"):
        return _truncate()

    s_tile = [sb(f"s_tile{i}", [128, D]) for i in range(2)]
    x_tile = [sb(f"x_tile{i}", [128, D], F32) for i in range(2)]
    o_tile = [sb(f"o_tile{i}", [128, D], F32) for i in range(2)]
    c_tmp = [sb(f"c_tmp{i}", [128, D], F32) for i in range(2)]
    d_tmp = [sb(f"d_tmp{i}", [128, D], F32) for i in range(2)]

    QSY.wait(CCS, 1)
    t_used = [0, 0]
    t_osent = [0, 0]
    for tt in range(TOK // 128):
        bi = tt % 2
        QSY.wait(VEC, t_used[bi])
        QSY.emit(lambda e, b=bi, t2=tt: e.dma_start(
            out=s_tile[b][:, :], in_=cc_out[t2 * 128:(t2 + 1) * 128, :]), SS6[bi], 16)
        QSY.emit(lambda e, b=bi, t2=tt: e.dma_start(
            out=x_tile[b][:, :], in_=xrows[t2 * 128:(t2 + 1) * 128, :]), SS6[bi], 16)
        t_ld = SS6[bi].n
        QV.wait(SS6[bi], t_ld)
        QV.wait(SO6[bi], t_osent[bi])
        QV.emit(lambda e, b=bi: e.tensor_tensor(
            c_tmp[b][:, :], s_tile[b][:, :], borep[:, :], ALU.add), VEC)
        QV.emit(lambda e, b=bi, t2=tt: e.tensor_tensor(
            d_tmp[b][:, :], c_tmp[b][:, :], g_sb[:, t2, :], ALU.mult), VEC)
        QV.emit(lambda e, b=bi: e.tensor_tensor(
            o_tile[b][:, :], d_tmp[b][:, :], x_tile[b][:, :], ALU.add), VEC)
        t_used[bi] = VEC.n
        QSY.wait(VEC, t_used[bi])
        QSY.emit(lambda e, b=bi, t2=tt: e.dma_start(
            out=out[t2 * 128:(t2 + 1) * 128, :], in_=o_tile[b][:, :]), SO6[bi], 16)
        t_osent[bi] = SO6[bi].n

    QSY.wait(SO6[0], SO6[0].n)
    QSY.wait(SO6[1], SO6[1].n)
    _ = t_g_done

    return _replay()


_NC_CACHE = [None]


def _get_nc():
    if _NC_CACHE[0] is None:
        _NC_CACHE[0] = build_program()
    return _NC_CACHE[0]


def host_in_maps(L0_attn_x, L0_attn_Wq, L0_attn_bq, L0_attn_Wk, L0_attn_bk,
                 L0_attn_Wv, L0_attn_bv, L0_attn_Wo, L0_attn_bo, L0_attn_head_mask,
                 L0_ffn_w_gate, L0_ffn_w_up, L0_ffn_w_down):
    x = np.asarray(L0_attn_x, dtype=np.float32)
    Wq = np.asarray(L0_attn_Wq, dtype=np.float32)
    Wk = np.asarray(L0_attn_Wk, dtype=np.float32)
    Wv = np.asarray(L0_attn_Wv, dtype=np.float32)
    Wo = np.asarray(L0_attn_Wo, dtype=np.float32)
    bq = np.asarray(L0_attn_bq, dtype=np.float32)
    bk = np.asarray(L0_attn_bk, dtype=np.float32)
    bv = np.asarray(L0_attn_bv, dtype=np.float32)
    bo = np.asarray(L0_attn_bo, dtype=np.float32)
    hm = np.asarray(L0_attn_head_mask, dtype=np.float32)
    wg = np.asarray(L0_ffn_w_gate, dtype=np.int32)
    wu = np.asarray(L0_ffn_w_up, dtype=np.int32)
    wd = np.asarray(L0_ffn_w_down, dtype=np.int32)

    C = np.ascontiguousarray
    wgu_h = np.empty((24, 128, 2, 8, 128), dtype=np.int32)
    wgu_h[:, :, 0] = wg.reshape(24, 128, 8, 128).transpose(0, 3, 2, 1)
    wgu_h[:, :, 1] = wu.reshape(24, 128, 8, 128).transpose(0, 3, 2, 1)
    wdT_h = C(wd.T)
    in_maps = []
    for c in range(8):
        b = c // 4
        g = c % 4
        qt = c % 4
        xb = x[b]
        xTb = C(xb.T)
        in_maps.append({
            "xT": xTb,
            "xoT": C(xb[qt * TOK:(qt + 1) * TOK].T),
            "xrows": C(xb[qt * TOK:(qt + 1) * TOK]),
            "wqT": C(Wq[g * DLOC:(g + 1) * DLOC, :].T),
            "wkT": C(Wk[g * DLOC:(g + 1) * DLOC, :].T),
            "wvT": C(Wv[g * DLOC:(g + 1) * DLOC, :].T),
            "woT": C(Wo[:, g * DLOC:(g + 1) * DLOC].T),
            "bqc": C(bq[g * DLOC:(g + 1) * DLOC].reshape(2, 128).T),
            "bkc": C(bk[g * DLOC:(g + 1) * DLOC].reshape(2, 128).T),
            "bvr": C(bv[g * DLOC:(g + 1) * DLOC].reshape(1, DLOC)),
            "bor": C(bo.reshape(1, D)),
            "hmr": C(hm[g * H_LOC:(g + 1) * H_LOC].reshape(1, H_LOC)),
            "wguT": wgu_h,
            "wdT": wdT_h,
        })
    return in_maps


def kernel(**inputs):
    in_maps = host_in_maps(**inputs)
    nc = _get_nc()
    res = run_bass_kernel_spmd(nc, in_maps, core_ids=list(range(8)))
    outp = np.empty((2, S, D), dtype=np.float32)
    for c in range(8):
        b = c // 4
        qt = c % 4
        outp[b, qt * TOK:(qt + 1) * TOK, :] = res.results[c]["out"]
    return outp


# version 18
# speedup vs baseline: 1.0292x; 1.0283x over previous
import numpy as np
import concourse.bass as bass
import concourse.mybir as mybir
from concourse.bass_utils import run_bass_kernel_spmd

F32 = mybir.dt.float32
BF16 = mybir.dt.bfloat16
I32 = mybir.dt.int32
AF = mybir.ActivationFunctionType
ALU = mybir.AluOpType
AX = mybir.AxisListType

D = 1024
S = 8192
H_LOC = 4
HD = 64
DLOC = H_LOC * HD
TOK = 2048
FH = 3072
CW = 256
QB = 512
NQB = S // QB
NKCH = S // 128
GRP = 3
EPS = 1e-5
DEQ = 1.0 / 127.0


class Ctr:
    def __init__(self, nc, name):
        self.h = nc.alloc_semaphore(name=name)
        self.n = 0


class EngQ:

    def __init__(self, self_ctr=None):
        self.fns = []
        self.self_ctr = self_ctr
        self.last_wait = {}

    def wait(self, ctr, val=None):
        v = ctr.n if val is None else val
        if v > 0 and v > self.last_wait.get(ctr.h.num, 0):
            self.last_wait[ctr.h.num] = v
            self.fns.append(lambda e, s=ctr.h, vv=v: e.wait_ge(s, vv))

    def emit(self, build, ctr=None, inc=1):
        if self.self_ctr is not None:
            prev = self.self_ctr.n
            if prev > 0:
                self.fns.append(lambda e, s=self.self_ctr.h, vv=prev: e.wait_ge(s, vv))
            if ctr is None:
                ctr, inc = self.self_ctr, 1
        if ctr is None:
            self.fns.append(build)
            return None
        ctr.n += inc
        tok = ctr.n

        def fn(e, b=build, s=ctr.h, k=inc):
            b(e).then_inc(s, k)

        self.fns.append(fn)
        return tok


STAGES = ["p0", "p1", "p1c", "p2", "p3", "cc", "p1b", "p5", "p6"]
LIMIT = "p6"


def build_program(limit=None):
    limit = LIMIT if limit is None else limit
    lim_i = STAGES.index(limit)

    def on(stage):
        return STAGES.index(stage) <= lim_i

    nc = bass.Bass("TRN2", target_bir_lowering=False)

    xT = nc.dram_tensor("xT", [D, S], F32, kind="ExternalInput")
    xoT = nc.dram_tensor("xoT", [D, TOK], F32, kind="ExternalInput")
    xrows = nc.dram_tensor("xrows", [TOK, D], F32, kind="ExternalInput")
    wqT = nc.dram_tensor("wqT", [D, DLOC], F32, kind="ExternalInput")
    wkT = nc.dram_tensor("wkT", [D, DLOC], F32, kind="ExternalInput")
    wvT = nc.dram_tensor("wvT", [D, DLOC], F32, kind="ExternalInput")
    woT = nc.dram_tensor("woT", [DLOC, D], F32, kind="ExternalInput")
    bqc = nc.dram_tensor("bqc", [128, 2], F32, kind="ExternalInput")
    bkc = nc.dram_tensor("bkc", [128, 2], F32, kind="ExternalInput")
    bvr = nc.dram_tensor("bvr", [1, DLOC], F32, kind="ExternalInput")
    bor = nc.dram_tensor("bor", [1, D], F32, kind="ExternalInput")
    hmr = nc.dram_tensor("hmr", [1, H_LOC], F32, kind="ExternalInput")
    wguT = nc.dram_tensor("wguT", [24, 128, 2, 8, 128], I32, kind="ExternalInput")
    wdT = nc.dram_tensor("wdT", [FH, D], I32, kind="ExternalInput")
    out = nc.dram_tensor("out", [TOK, D], F32, kind="ExternalOutput")
    cc_in = nc.dram_tensor("cc_in", [S, D], BF16)
    cc_out = nc.dram_tensor("cc_out", [TOK, D], BF16)
    mxd = nc.dram_tensor("mxd", [128, 1], F32)

    TEN = Ctr(nc, "TEN")
    ACT = Ctr(nc, "ACT")
    VEC = Ctr(nc, "VEC")
    POO = Ctr(nc, "POO")
    CCS = Ctr(nc, "CCS")
    PH = Ctr(nc, "PH")
    EPI = Ctr(nc, "EPI")
    SYC = Ctr(nc, "SYC")
    SXF = [Ctr(nc, f"SXF{i}") for i in range(2)]
    GXB = [Ctr(nc, f"GXB{i}") for i in range(2)]
    GW = Ctr(nc, "GW")
    GF = [Ctr(nc, f"GF{i}") for i in range(12)]
    SSP = [Ctr(nc, f"SSP{i}") for i in range(4)]
    SMX = Ctr(nc, "SMX")
    SS6 = [Ctr(nc, f"SS6{i}") for i in range(2)]
    SO6 = [Ctr(nc, f"SO6{i}") for i in range(2)]

    QSY, QT, QG = EngQ(), EngQ(), EngQ()
    QV, QA = EngQ(self_ctr=VEC), EngQ(self_ctr=ACT)
    allq = {"sync": QSY, "tensor": QT, "vector": QV, "scalar": QA, "gpsimd": QG}
    nbar = [0]

    def barrier():
        nbar[0] += 1
        tgt = 5 * nbar[0]
        for q in allq.values():
            q.emit(lambda e: e.nop(), PH)
            q.wait(PH, tgt)

    sb = lambda name, shape, dt=BF16: nc.alloc_sbuf_tensor(name, shape, dt)
    ps = lambda name, shape: nc.alloc_psum_tensor(name, shape, F32)

    def _replay():
        with nc.Block() as block:
            @block.sync
            def _(e):
                for fn in QSY.fns:
                    fn(e)

            @block.tensor
            def _(e):
                for fn in QT.fns:
                    fn(e)

            @block.vector
            def _(e):
                for fn in QV.fns:
                    fn(e)

            @block.scalar
            def _(e):
                for fn in QA.fns:
                    fn(e)

            @block.gpsimd
            def _(e):
                for fn in QG.fns:
                    fn(e)
        return nc

    def _truncate():
        od = nc.alloc_sbuf_tensor("o_dummy", [128, D], F32)
        tz = QV.emit(lambda e: e.memset(od[:, :], 0.0), VEC)
        QSY.wait(VEC, tz)
        for tt in range(TOK // 128):
            QSY.emit(lambda e, t2=tt: e.dma_start(
                out=out[t2 * 128:(t2 + 1) * 128, :], in_=od[:, :]), SO6[0], 16)
        QSY.wait(SO6[0], SO6[0].n)
        return _replay()

    ones_mean = sb("ones_mean", [128, 128])
    ones_col = sb("ones_col", [128, 1])
    ones_row = sb("ones_row", [1, 128], F32)
    sel33 = sb("sel33", [33, 128], F32)
    rb33 = sb("rb33", [33, 512], F32)
    bq_sb = sb("bq_sb", [128, 2], F32)
    bk_sb = sb("bk_sb", [128, 2], F32)
    bvrow = sb("bvrow", [1, DLOC], F32)
    borow = sb("borow", [1, D], F32)
    hm_sb = sb("hm_sb", [1, H_LOC], F32)
    bvrep = sb("bvrep", [128, DLOC], F32)
    borep = sb("borep", [128, D], F32)
    crep = sb("crep", [128, 1], F32)
    xnf = sb("xnf", [128, 8, TOK])
    xnf_flat = xnf
    sbuf_M0 = nc.sbuf_base
    kt = [sb(f"kt{i}", [128, S]) for i in range(2)]
    qt_ = [sb(f"qt{i}", [128, S]) for i in range(2)]
    vv = sb("vv", [128, NKCH, DLOC])
    sbuf_M1 = nc.sbuf_base

    QV.emit(lambda e: e.memset(ones_mean[:, :], 1.0 / 1024.0))
    QV.emit(lambda e: e.memset(ones_col[:, :], 1.0))
    QV.emit(lambda e: e.memset(ones_row[:, :], 1.0))
    QV.emit(lambda e: e.memset(sel33[:, :], 0.0))
    QV.emit(lambda e: e.memset(sel33[0:1, 0:64], 1.0))
    QV.emit(lambda e: e.memset(sel33[32:33, 64:128], 1.0))
    QV.emit(lambda e: e.memset(rb33[:, :], 0.0))
    QV.emit(lambda e: e.memset(crep[:, :], 0.0), VEC)
    v_consts = VEC.n

    for (dst, src) in ((bq_sb, bqc), (bk_sb, bkc), (bvrow, bvr),
                       (borow, bor), (hm_sb, hmr)):
        QSY.emit(lambda e, d=dst, s=src: e.dma_start(out=d[:, :], in_=s[:, :]), SYC, 16)
    sy_consts = SYC.n

    psum_base0 = nc.psum_base

    ps_rep = ps("ps_rep", [128, 512])
    QT.wait(SYC, sy_consts)
    QT.wait(VEC, v_consts)
    QT.emit(lambda e: e.matmul(ps_rep[:, 0:DLOC], ones_row[:, :], bvrow[:, :],
                               start=True, stop=True), TEN)
    QV.wait(TEN)
    t = QV.emit(lambda e: e.tensor_copy(bvrep[:, :], ps_rep[:, 0:DLOC]), VEC)
    for half in range(2):
        QT.wait(VEC, t)
        QT.emit(lambda e, h=half: e.matmul(ps_rep[:, :], ones_row[:, :],
                                           borow[:, h * 512:(h + 1) * 512],
                                           start=True, stop=True), TEN)
        QV.wait(TEN)
        t = QV.emit(lambda e, h=half: e.tensor_copy(borep[:, h * 512:(h + 1) * 512],
                                                    ps_rep[:, :]), VEC)
    barrier()
    nc.psum_base = psum_base0
    if not on("p1"):
        return _truncate()

    def ln_pipeline(tag, src_dram, n_chunks, rstd_bias, consumers):
        xf = [sb(f"xf_{tag}{i}", [128, 8, CW], F32) for i in range(2)]
        xb = [sb(f"xb_{tag}{i}", [128, 8, CW]) for i in range(2)]
        xq = [sb(f"xq_{tag}{i}", [128, 8, CW]) for i in range(2)]
        xn = [sb(f"xn_{tag}{i}", [128, 8, CW]) for i in range(2)]
        m_sb = [sb(f"m_{tag}{i}", [128, CW], F32) for i in range(2)]
        msq = [sb(f"msq_{tag}{i}", [128, CW], F32) for i in range(2)]
        var = [sb(f"var_{tag}{i}", [128, CW], F32) for i in range(2)]
        lnv = [sb(f"lnv_{tag}{i}", [128, CW], F32) for i in range(2)]
        rstd = [sb(f"rstd_{tag}{i}", [128, CW], F32) for i in range(2)]
        tmp = sb(f"tmp_{tag}", [128, 4, CW], F32)
        eps_col = sb(f"eps_{tag}", [128, 1], F32)
        rb_col = [sb(f"rbias_{tag}{i}", [128, 1], F32) for i in range(2)]
        QV.emit(lambda e: e.memset(eps_col[:, :], float(EPS)))
        QV.emit(lambda e: e.memset(rb_col[0][:, :], 0.0))
        t0m = QV.emit(lambda e: e.memset(rb_col[1][:, :], float(np.log(DEQ))), VEC)
        QA.wait(VEC, t0m)
        ps_m = [ps(f"ps_m_{tag}{i}", [128, CW]) for i in range(2)]
        ps_s = [ps(f"ps_s_{tag}{i}", [128, CW]) for i in range(2)]

        t_xf_cons = [0, 0]
        t_xb_cons_t = [0, 0]
        t_xb_cons_p = [0, 0]
        t_xq_cons = [0, 0]
        t_xn_cons = [0, 0]
        t_xn_cons_ctr = [None, None]
        t_stats_read2 = [0, 0]
        t_rstd_read = [0, 0]
        t_act_rows = [0, 0]
        pending_cons = []

        for ci in range(n_chunks):
            bi = ci % 2
            src = src_dram(ci)
            QSY.wait(VEC, t_xf_cons[bi])
            QSY.emit(lambda e, a=src, d=xf[bi]: e.dma_start(out=d[:, :, :], in_=a), SXF[bi], 16)
            t_xf = SXF[bi].n
            QG.wait(TEN, t_xb_cons_t[bi])
            QG.wait(ACT, t_xb_cons_p[bi])
            QG.emit(lambda e, a=src, d=xb[bi]: e.dma_start(out=d[:, :, :], in_=a), GXB[bi], 16)
            t_xb = GXB[bi].n

            QA.wait(GXB[bi], t_xb)
            QA.wait(TEN, t_xq_cons[bi])
            QA.emit(lambda e, b=bi: e.activation(
                xq[b][:, :, :], xb[b][:, :, :], AF.Square, bias=0.0, scale=1.0), ACT)
            t_xsq = ACT.n

            QT.wait(GXB[bi], t_xb)
            QT.wait(ACT, t_xsq)
            QT.wait(VEC, t_stats_read2[bi])
            for tt in range(8):
                QT.emit(lambda e, b=bi, m=tt: e.matmul(
                    ps_m[b][:, :], ones_mean[:, :], xb[b][:, m, :],
                    start=(m == 0), stop=(m == 7)), TEN)
            t_xb_cons_save = TEN.n
            for tt in range(8):
                QT.emit(lambda e, b=bi, m=tt: e.matmul(
                    ps_s[b][:, :], ones_mean[:, :], xq[b][:, m, :],
                    start=(m == 0), stop=(m == 7)), TEN)
            t_xb_cons_t[bi] = t_xb_cons_save
            t_xb_cons_p[bi] = t_xsq
            t_xq_cons[bi] = TEN.n
            t_stats = TEN.n

            QV.wait(TEN, t_stats)
            QV.wait(ACT, t_act_rows[bi])
            QV.emit(lambda e, b=bi: e.tensor_copy(m_sb[b][:, :], ps_m[b][:, :]), VEC)
            QV.emit(lambda e, b=bi: e.tensor_tensor(msq[b][:, :], m_sb[b][:, :], m_sb[b][:, :], ALU.mult), VEC)
            QV.emit(lambda e, b=bi: e.tensor_tensor(var[b][:, :], ps_s[b][:, :], msq[b][:, :], ALU.subtract), VEC)
            t_stats_read2[bi] = VEC.n
            QA.wait(VEC, t_stats_read2[bi])
            QA.wait(VEC, t_rstd_read[bi])
            QA.emit(lambda e, b=bi: e.activation(lnv[b][:, :], var[b][:, :], AF.Ln,
                                                 bias=eps_col[:, 0:1], scale=1.0), ACT)
            QA.emit(lambda e, b=bi, rc=rb_col[1 if rstd_bias(ci) else 0]: e.activation(
                rstd[b][:, :], lnv[b][:, :], AF.Exp,
                bias=rc[:, 0:1], scale=-0.5), ACT)
            t_act_rows[bi] = ACT.n

            QV.wait(SXF[bi], t_xf)
            QV.wait(ACT, t_act_rows[bi])
            if t_xn_cons_ctr[bi] is not None:
                QV.wait(t_xn_cons_ctr[bi], t_xn_cons[bi])
            for hh in range(2):
                QV.emit(lambda e, b=bi, h=hh: e.tensor_tensor(
                    tmp[:, :, :], xf[b][:, 4 * h:4 * h + 4, :],
                    m_sb[b][:, None, :].broadcast_to([128, 4, CW]), ALU.subtract), VEC)
                QV.emit(lambda e, b=bi, h=hh: e.tensor_tensor(
                    xn[b][:, 4 * h:4 * h + 4, :], tmp[:, :, :],
                    rstd[b][:, None, :].broadcast_to([128, 4, CW]), ALU.mult), VEC)
            t_xn = VEC.n
            t_xf_cons[bi] = t_xn
            t_rstd_read[bi] = t_xn

            pending_cons.append((ci, bi, xn[bi], t_xn))
            if len(pending_cons) > 1:
                pci, pbi, pxn, ptok = pending_cons.pop(0)
                ctr, tok = consumers(pci, pxn, ptok)
                t_xn_cons_ctr[pbi] = ctr
                t_xn_cons[pbi] = tok
            yield ci

        while pending_cons:
            pci, pbi, pxn, ptok = pending_cons.pop(0)
            ctr, tok = consumers(pci, pxn, ptok)
            t_xn_cons_ctr[pbi] = ctr
            t_xn_cons[pbi] = tok

    sbuf_p1 = nc.sbuf_base
    psum_p1 = nc.psum_base

    wq_sb = sb("wq_sb", [128, 8, DLOC])
    wk_sb = sb("wk_sb", [128, 8, DLOC])
    wv_sb = sb("wv_sb", [128, 8, DLOC])
    for (dst, srcw) in ((wq_sb, wqT), (wk_sb, wkT), (wv_sb, wvT)):
        QG.emit(lambda e, d=dst, s=srcw: e.dma_start(
            out=d[:, :, :], in_=s[:, :].rearrange("(t p) c -> p t c", p=128)), GW, 16)
    gp_w = GW.n

    ps_q = ps("ps_q", [128, 512])
    ps_k = ps("ps_k", [128, 512])
    ps_v = ps("ps_v", [128, 512])
    qkv_state = {"evac": 0, "evac_a": 0}

    def qkv_consumers(ci, xn, t_xn):
        QT.wait(VEC, t_xn)
        if ci == 0:
            QT.wait(GW, gp_w)
        QT.wait(VEC, qkv_state["evac"])
        QT.wait(ACT, qkv_state["evac_a"])
        for (pps, wsb) in ((ps_q, wq_sb), (ps_k, wk_sb)):
            for dt_ in range(2):
                for tt in range(8):
                    QT.emit(lambda e, p=pps, w=wsb, d=dt_, m=tt, b=xn: e.matmul(
                        p[:, d * 256:(d + 1) * 256],
                        w[:, m, d * 128:(d + 1) * 128],
                        b[:, m, :],
                        start=(m == 0), stop=(m == 7)), TEN)
        for s_ in range(2):
            for tt in range(8):
                QT.emit(lambda e, s2=s_, m=tt, b=xn: e.matmul(
                    ps_v[:, s2 * 256:(s2 + 1) * 256],
                    b[:, m, s2 * 128:(s2 + 1) * 128],
                    wv_sb[:, m, :],
                    start=(m == 0), stop=(m == 7)), TEN)
        t_mm = TEN.n
        QA.wait(TEN, t_mm)
        for dt_ in range(2):
            QA.emit(lambda e, d=dt_, c=ci: e.activation(
                qt_[d][:, c * CW:(c + 1) * CW], ps_q[:, d * 256:(d + 1) * 256],
                AF.Identity, bias=bq_sb[:, d:d + 1], scale=1.0), ACT)
            QA.emit(lambda e, d=dt_, c=ci: e.activation(
                kt[d][:, c * CW:(c + 1) * CW], ps_k[:, d * 256:(d + 1) * 256],
                AF.Identity, bias=bk_sb[:, d:d + 1], scale=1.0), ACT)
        qkv_state["evac_a"] = ACT.n
        QV.wait(TEN, t_mm)
        for s_ in range(2):
            QV.emit(lambda e, s2=s_, c=ci: e.tensor_tensor(
                vv[:, 2 * c + s2, :], ps_v[:, s2 * 256:(s2 + 1) * 256],
                bvrep[:, :], ALU.add), VEC)
        qkv_state["evac"] = VEC.n
        return TEN, t_mm

    NCH_B = S // CW

    def xnf_consumers(ci, xn, t_xn):
        c = ci - NCH_B
        for tt in range(8):
            QV.emit(lambda e, m=tt, cc2=c: e.tensor_copy(
                xnf[:, m, cc2 * CW:(cc2 + 1) * CW], xn[:, m, :]), VEC)
        return VEC, VEC.n

    def p1_src(ci):
        if ci < NCH_B:
            return xT[:, ci * CW:(ci + 1) * CW].rearrange("(t p) c -> p t c", p=128)
        c = ci - NCH_B
        return xoT[:, c * CW:(c + 1) * CW].rearrange("(t p) c -> p t c", p=128)

    def p1_consumers(ci, xn, t_xn):
        if ci < NCH_B:
            return qkv_consumers(ci, xn, t_xn)
        return xnf_consumers(ci, xn, t_xn)

    for _ in ln_pipeline(
            "a", p1_src, NCH_B + TOK // CW,
            lambda ci: ci >= NCH_B, p1_consumers):
        pass
    barrier()
    nc.sbuf_base = sbuf_M1
    nc.psum_base = psum_p1
    if not on("p1c"):
        return _truncate()

    ps_smp = [ps(f"ps_smp{i}", [128, 256]) for i in range(2)]
    runmax = sb("runmax", [128, 1], F32)
    colmax = sb("colmax", [128, 1], F32)
    rowback = sb("rowback", [1, 128], F32)
    cval = sb("cval", [1, 1], F32)
    t_red = 0
    for pr in range(2):
        for kb in range(NKCH // 2):
            QT.wait(VEC, t_red)
            for sub in range(2):
                k = 2 * kb + sub
                QT.emit(lambda e, p=pr, kk=k, s_=sub: e.matmul(
                    ps_smp[0][:, s_ * 128:(s_ + 1) * 128],
                    kt[p][0:64, kk * 128:(kk + 1) * 128],
                    qt_[p][0:64, 0:128],
                    start=True, stop=True, tile_position=(0, 0)), TEN)
                QT.emit(lambda e, p=pr, kk=k, s_=sub: e.matmul(
                    ps_smp[1][:, s_ * 128:(s_ + 1) * 128],
                    kt[p][64:128, kk * 128:(kk + 1) * 128],
                    qt_[p][64:128, 0:128],
                    start=True, stop=True, tile_position=(64, 0)), TEN)
            t_mm = TEN.n
            QV.wait(TEN, t_mm)
            for ii in range(2):
                QV.emit(lambda e, i=ii: e.reduce_max(colmax[:, :], ps_smp[i][:, :], AX.X), VEC)
                if pr == 0 and kb == 0 and ii == 0:
                    QV.emit(lambda e: e.tensor_copy(runmax[:, :], colmax[:, :]), VEC)
                else:
                    QV.emit(lambda e: e.tensor_tensor(runmax[:, :], runmax[:, :],
                                                      colmax[:, :], ALU.max), VEC)
            t_red = VEC.n
    QSY.wait(VEC, t_red)
    QSY.emit(lambda e: e.dma_start(out=mxd[:, :], in_=runmax[:, :]), SMX, 16)
    t_st = SMX.n
    QSY.wait(SMX, t_st)
    QSY.emit(lambda e: e.dma_start(
        out=rowback[:, :], in_=mxd[:, :].rearrange("p one -> one p")), SMX, 16)
    t_ld = SMX.n
    QV.wait(SMX, t_ld)
    QV.emit(lambda e: e.reduce_max(cval[:, :], rowback[:, :], AX.X), VEC)
    QV.emit(lambda e: e.tensor_scalar(cval[:, :], cval[:, :], -0.125, None, ALU.mult), VEC)
    t_c = VEC.n
    QT.wait(VEC, t_c)
    QT.emit(lambda e: e.matmul(ps_smp[0][:, 0:1], ones_row[:, :], cval[:, :],
                               start=True, stop=True), TEN)
    QV.wait(TEN)
    QV.emit(lambda e: e.tensor_copy(crep[:, :], ps_smp[0][:, 0:1]), VEC)
    barrier()
    nc.sbuf_base = sbuf_M1
    nc.psum_base = psum_p1
    if not on("p2"):
        return _truncate()

    ot = [sb(f"ot{i}", [128, S]) for i in range(2)]
    sbuf_M2 = nc.sbuf_base
    PBW = 6 * 512
    pbuf = [sb(f"pbuf{i}", [128, PBW]) for i in range(3)]
    acc6 = sb("acc6", [128, PBW])
    of_sb = sb("of_sb", [128, 512], F32)
    re_row = sb("re_row", [1, 512], F32)
    re2_row = sb("re2_row", [1, 512], F32)
    rep_sb = sb("rep_sb", [128, 512], F32)
    s_ps = [ps(f"s_ps{i}", [128, GRP * 512]) for i in range(2)]
    o_ps = ps("o_ps", [128, 1024])

    t_sfree = [0, 0]
    t_pv_pb = [0, 0, 0]
    t_add_pb = [0, 0, 0]
    t_of_copy = 0

    NT = 2 * NKCH
    NG = (NT + GRP - 1) // GRP
    pending_epi = [None]

    for qb in range(NQB):
        for pr in range(2):
            unit_idx = qb * 2 + pr
            grp_mm_tok = [0] * NG
            grp_exp_tok = [0] * NG
            slot_first_add = True

            def tloc(j):
                g = j // GRP
                return g, (j % GRP) * 512, (g // 2) % 3, (g % 2) * 1536

            pv_emitted = 0
            pv_enabled = [pending_epi[0] is None]

            def emit_pv_upto(gdone):
                nonlocal pv_emitted
                if not pv_enabled[0]:
                    return
                while pv_emitted < NKCH:
                    k = pv_emitted
                    jo = 2 * k + 1
                    gmax = jo // GRP
                    if gmax > gdone:
                        break
                    ge, ce, sle, bce = tloc(2 * k)
                    go, co, slo, bco = tloc(jo)
                    QT.wait(ACT, grp_exp_tok[gmax])
                    if k == 0:
                        QT.wait(EPI, unit_idx)
                    stf = (k == 0)
                    spf = (k == NKCH - 1)
                    QT.emit(lambda e, kk=k, p=pr, pi=sle, bc=bce + ce, s_=stf, f=spf: e.matmul(
                        o_ps[0:64, 0:512],
                        vv[:, kk, 2 * p * 64: 2 * p * 64 + 64],
                        pbuf[pi][:, bc:bc + 512],
                        start=s_, stop=f, tile_position=(0, 0)), TEN)
                    QT.emit(lambda e, kk=k, p=pr, pi=slo, bc=bco + co, s_=stf, f=spf: e.matmul(
                        o_ps[64:128, 512:1024],
                        vv[:, kk, (2 * p + 1) * 64: (2 * p + 1) * 64 + 64],
                        pbuf[pi][:, bc:bc + 512],
                        start=s_, stop=f, tile_position=(0, 64)), TEN)
                    t_pv_pb[sle] = TEN.n
                    t_pv_pb[slo] = TEN.n
                    pv_emitted += 1

            for j0 in range(0, NT, 2):
                kch = j0 // 2
                for (j, row0, tp) in ((j0, 0, (0, 0)), (j0 + 1, 64, (64, 0))):
                    g, col, slot, bufcol = tloc(j)
                    par = g % 2
                    if j % GRP == 0:
                        QT.wait(ACT, t_sfree[par])
                    QT.emit(lambda e, p=par, c=col, r=row0, k=kch, t=pr, q=qb, tp_=tp: e.matmul(
                        s_ps[p][:, c:c + 512],
                        kt[t][r:r + 64, k * 128:(k + 1) * 128],
                        qt_[t][r:r + 64, q * QB:(q + 1) * QB],
                        start=True, stop=True, tile_position=tp_), TEN)
                    grp_mm_tok[g] = TEN.n
                    if (j % GRP == GRP - 1) or (j == NT - 1):
                        width = col + 512
                        QA.wait(TEN, grp_mm_tok[g])
                        QA.wait(TEN, t_pv_pb[slot])
                        QA.wait(VEC, t_add_pb[slot])
                        QA.emit(lambda e, p=par, w=width, pi=slot, bc=bufcol: e.activation(
                            pbuf[pi][:, bc:bc + w], s_ps[p][:, 0:w], AF.Exp,
                            bias=crep[:, 0:1], scale=0.125), ACT)
                        grp_exp_tok[g] = ACT.n
                        t_sfree[par] = ACT.n
                        if (g % 2 == 1) or (j == NT - 1):
                            cols = bufcol + width
                            QV.wait(ACT, grp_exp_tok[g])
                            if slot_first_add:
                                QV.emit(lambda e, pi=slot, cl=cols: e.tensor_copy(
                                    acc6[:, 0:cl], pbuf[pi][:, 0:cl]), VEC)
                                slot_first_add = False
                            else:
                                QV.emit(lambda e, pi=slot, cl=cols: e.tensor_tensor(
                                    acc6[:, 0:cl], acc6[:, 0:cl], pbuf[pi][:, 0:cl],
                                    ALU.add), VEC)
                            t_add_pb[slot] = VEC.n
                        emit_pv_upto(g - 1)
                    if j == 2 and pending_epi[0] is not None:
                        pending_epi[0][0]()
                    if j == 8 and pending_epi[0] is not None:
                        pending_epi[0][1]()
                        pending_epi[0] = None
                        pv_enabled[0] = True
                        emit_pv_upto((j // GRP) - 1)

            emit_pv_upto(NG)

            QV.wait(TEN)
            QV.emit(lambda e: e.tensor_copy(of_sb[0:64, :], o_ps[0:64, 0:512]), VEC)
            QV.emit(lambda e: e.tensor_copy(of_sb[64:128, :], o_ps[64:128, 512:1024]), VEC)
            t_of_copy = VEC.n

            def mk_epilogue(pr=pr, qb=qb, t_of=t_of_copy):
                state = {}

                def flush_sums():
                    QT.wait(VEC, t_of)
                    for r6 in range(3):
                        QT.emit(lambda e, r=r6: e.matmul(
                            o_ps[0:1, 0:512], ones_col[:, :],
                            acc6[:, r * 1024:r * 1024 + 512],
                            start=(r == 0), stop=(r == 2), tile_position=(0, 0)), TEN)
                    for r6 in range(3):
                        QT.emit(lambda e, r=r6: e.matmul(
                            o_ps[32:33, 512:1024], ones_col[:, :],
                            acc6[:, r * 1024 + 512:r * 1024 + 1024],
                            start=(r == 0), stop=(r == 2), tile_position=(0, 32)), TEN)
                    t_sums = TEN.n

                    QV.wait(TEN, t_sums)
                    for (row, coff, hloc) in ((0, 0, 2 * pr), (32, 512, 2 * pr + 1)):
                        QV.emit(lambda e, r=row, c=coff: e.reciprocal(
                            re_row[:, :], o_ps[r:r + 1, c:c + 512]), VEC)
                        QV.emit(lambda e, h=hloc: e.tensor_scalar(
                            re2_row[:, :], re_row[:, :], hm_sb[:, h:h + 1], None,
                            ALU.mult), VEC)
                        QV.emit(lambda e, r=row: e.tensor_copy(
                            rb33[r:r + 1, :], re2_row[:, :]), VEC)
                    state["t_rb"] = VEC.n

                def flush_norm():
                    QT.wait(VEC, state["t_rb"])
                    QT.emit(lambda e: e.matmul(o_ps[:, 0:512], sel33[:, :], rb33[:, :],
                                               start=True, stop=True), TEN)
                    t_bcast = TEN.n

                    QV.wait(TEN, t_bcast)
                    epi_tok = QV.emit(lambda e: e.tensor_copy(
                        rep_sb[:, :], o_ps[:, 0:512]), EPI)
                    QV.wait(EPI, epi_tok)
                    QV.emit(lambda e, p=pr, q=qb: e.tensor_tensor(
                        ot[p][:, q * QB:(q + 1) * QB], of_sb[:, :], rep_sb[:, :],
                        ALU.mult), VEC)
                return (flush_sums, flush_norm)

            pending_epi[0] = mk_epilogue()

    pending_epi[0][0]()
    pending_epi[0][1]()
    pending_epi[0] = None
    barrier()
    nc.sbuf_base = sbuf_M2
    nc.psum_base = psum_p1
    if not on("p3"):
        return _truncate()

    wo_sb = sb("wo_sb", [128, 2, D])
    sp_sb = [sb(f"sp_sb{i}", [128, D]) for i in range(4)]
    ps_sp = [ps(f"ps_sp{i}", [128, D]) for i in range(4)]

    QG.emit(lambda e: e.dma_start(
        out=wo_sb[:, :, :], in_=woT[:, :].rearrange("(t p) c -> p t c", p=128)), GW, 16)
    gp_wo = GW.n

    t_sp_dma = [0] * 4
    t_sp_evac = [0] * 4
    t_sp_evac_a = [0] * 4
    QT.wait(GW, gp_wo)
    for tt in range(S // 128):
        bi = tt % 4
        QT.wait(VEC, t_sp_evac[bi])
        QT.wait(ACT, t_sp_evac_a[bi])
        for ch in range(2):
            for half in range(2):
                QT.emit(lambda e, t2=tt, c=ch, h=half, b=bi: e.matmul(
                    ps_sp[b][:, h * 512:(h + 1) * 512],
                    ot[c][:, t2 * 128:(t2 + 1) * 128],
                    wo_sb[:, c, h * 512:(h + 1) * 512],
                    start=(c == 0), stop=(c == 1)), TEN)
        t_mm = TEN.n
        QV.wait(TEN, t_mm)
        QV.wait(SSP[bi], t_sp_dma[bi])
        QV.emit(lambda e, b=bi: e.tensor_copy(sp_sb[b][:, 0:512], ps_sp[b][:, 0:512]), VEC)
        t_ev_v = VEC.n
        QA.wait(TEN, t_mm)
        QA.wait(SSP[bi], t_sp_dma[bi])
        QA.emit(lambda e, b=bi: e.activation(
            sp_sb[b][:, 512:1024], ps_sp[b][:, 512:1024], AF.Identity,
            bias=0.0, scale=1.0), ACT)
        t_ev_a = ACT.n
        t_sp_evac[bi] = t_ev_v
        t_sp_evac_a[bi] = t_ev_a
        QSY.wait(VEC, t_ev_v)
        QSY.wait(ACT, t_ev_a)
        QSY.emit(lambda e, b=bi, t2=tt: e.dma_start(
            out=cc_in[t2 * 128:(t2 + 1) * 128, :], in_=sp_sb[b][:, :]), SSP[bi], 16)
        t_sp_dma[bi] = SSP[bi].n

    if not on("cc"):
        return _truncate()
    barrier()
    nc.sbuf_base = sbuf_M0
    nc.psum_base = psum_p1

    g_sb = sb("g_sb", [128, 16, D])
    sbuf_M5 = nc.sbuf_base
    wgu_sb = [sb(f"wgu_sb{i}", [128, 2, 8, 128]) for i in range(12)]
    wd_all = sb("wd_all", [128, 24, D])
    sg_sb = sb("sg_sb", [128, 512], F32)
    hp = sb("hp", [128, 24, 512])
    ps_g = ps("ps_g", [128, 512])
    ps_u = ps("ps_u", [128, 512])
    ps_d = [ps(f"ps_d{i}", [128, D]) for i in range(2)]

    for _s in SSP:
        QG.wait(_s, _s.n)
    QG.emit(lambda e: e.dma_start(
        out=wd_all[:, :, :], in_=wdT[:, :].rearrange("(t p) c -> p t c", p=128)), GW, 16)
    gp_wd = GW.n
    t_wgu_tok = [0] * 12
    for h in range(12):
        QG.emit(lambda e, b=h: e.dma_start(
            out=wgu_sb[b][:, :, :, :], in_=wguT[b, :, :, :, :]), GF[h], 16)
        t_wgu_tok[h] = GF[h].n
    QG.emit(lambda e: e.collective_compute(
        "ReduceScatter", ALU.add,
        replica_groups=[[0, 1, 2, 3], [4, 5, 6, 7]],
        ins=[cc_in[:, :]], outs=[cc_out[:, :]]), CCS)
    if not on("p5"):
        return _truncate()

    t_wgu_used = [0] * 12
    t_gate_act = 0
    t_hpm_last = 0
    t_hp_used = 0
    t_g_act = [0, 0]

    for tokb in range(4):
        for hc in range(24):
            bi = hc % 12
            if tokb == 0 and hc < 12:
                t_wpair = t_wgu_tok[bi]
            else:
                QG.wait(TEN, t_wgu_used[bi])
                QG.emit(lambda e, b=bi, h=hc: e.dma_start(
                    out=wgu_sb[b][:, :, :, :], in_=wguT[h, :, :, :, :]), GF[bi], 16)
                t_wpair = GF[bi].n

            QT.wait(GF[bi], t_wpair)
            QT.wait(ACT, t_gate_act)
            for m in range(8):
                QT.emit(lambda e, b=bi, mm=m, tb=tokb: e.matmul(
                    ps_g[:, :], wgu_sb[b][:, 0, mm, :],
                    xnf_flat[:, mm, tb * 512:(tb + 1) * 512],
                    start=(mm == 0), stop=(mm == 7)), TEN)
            t_gmm = TEN.n
            QT.wait(VEC, t_hpm_last)
            for m in range(8):
                QT.emit(lambda e, b=bi, mm=m, tb=tokb: e.matmul(
                    ps_u[:, :], wgu_sb[b][:, 1, mm, :],
                    xnf_flat[:, mm, tb * 512:(tb + 1) * 512],
                    start=(mm == 0), stop=(mm == 7)), TEN)
            t_wgu_used[bi] = TEN.n
            t_umm = TEN.n

            QA.wait(TEN, t_gmm)
            QA.wait(VEC, t_hpm_last)
            QA.emit(lambda e: e.activation(sg_sb[:, :], ps_g[:, :], AF.Silu,
                                           bias=0.0, scale=1.0), ACT)
            t_gate_act = ACT.n
            QV.wait(ACT, t_gate_act)
            QV.wait(TEN, t_umm)
            QV.wait(TEN, t_hp_used)
            QV.emit(lambda e, h=hc: e.tensor_tensor(
                hp[:, h, :], sg_sb[:, :], ps_u[:, :], ALU.mult), VEC)
            t_hpm_last = VEC.n

        t_hp_all = VEC.n
        QT.wait(VEC, t_hp_all)
        if tokb == 0:
            QT.wait(GW, gp_wd)
        for st128 in range(4):
            bi = st128 % 2
            QT.wait(ACT, t_g_act[bi])
            for hc in range(24):
                for half in range(2):
                    QT.emit(lambda e, b=bi, h=hc, s_=st128, hf=half: e.matmul(
                        ps_d[b][:, hf * 512:(hf + 1) * 512],
                        hp[:, h, s_ * 128:(s_ + 1) * 128],
                        wd_all[:, h, hf * 512:(hf + 1) * 512],
                        start=(h == 0), stop=(h == 23)), TEN)
            t_dmm = TEN.n
            QA.wait(TEN, t_dmm)
            QA.emit(lambda e, b=bi, tb=tokb, s_=st128: e.activation(
                g_sb[:, tb * 4 + s_, :], ps_d[b][:, :], AF.Silu,
                bias=0.0, scale=DEQ), ACT)
            t_g_act[bi] = ACT.n
        t_hp_used = TEN.n

    t_g_done = max(t_g_act)
    barrier()
    nc.sbuf_base = sbuf_M5
    nc.psum_base = psum_p1
    if not on("p6"):
        return _truncate()

    s_tile = [sb(f"s_tile{i}", [128, D]) for i in range(2)]
    x_tile = [sb(f"x_tile{i}", [128, D], F32) for i in range(2)]
    o_tile = [sb(f"o_tile{i}", [128, D], F32) for i in range(2)]
    c_tmp = [sb(f"c_tmp{i}", [128, D], F32) for i in range(2)]
    d_tmp = [sb(f"d_tmp{i}", [128, D], F32) for i in range(2)]

    QSY.wait(CCS, 1)
    t_used = [0, 0]
    t_osent = [0, 0]
    for tt in range(TOK // 128):
        bi = tt % 2
        QSY.wait(VEC, t_used[bi])
        QSY.emit(lambda e, b=bi, t2=tt: e.dma_start(
            out=s_tile[b][:, :], in_=cc_out[t2 * 128:(t2 + 1) * 128, :]), SS6[bi], 16)
        QSY.emit(lambda e, b=bi, t2=tt: e.dma_start(
            out=x_tile[b][:, :], in_=xrows[t2 * 128:(t2 + 1) * 128, :]), SS6[bi], 16)
        t_ld = SS6[bi].n
        QV.wait(SS6[bi], t_ld)
        QV.wait(SO6[bi], t_osent[bi])
        QV.emit(lambda e, b=bi: e.tensor_tensor(
            c_tmp[b][:, :], s_tile[b][:, :], borep[:, :], ALU.add), VEC)
        QV.emit(lambda e, b=bi, t2=tt: e.tensor_tensor(
            d_tmp[b][:, :], c_tmp[b][:, :], g_sb[:, t2, :], ALU.mult), VEC)
        QV.emit(lambda e, b=bi: e.tensor_tensor(
            o_tile[b][:, :], d_tmp[b][:, :], x_tile[b][:, :], ALU.add), VEC)
        t_used[bi] = VEC.n
        QSY.wait(VEC, t_used[bi])
        QSY.emit(lambda e, b=bi, t2=tt: e.dma_start(
            out=out[t2 * 128:(t2 + 1) * 128, :], in_=o_tile[b][:, :]), SO6[bi], 16)
        t_osent[bi] = SO6[bi].n

    QSY.wait(SO6[0], SO6[0].n)
    QSY.wait(SO6[1], SO6[1].n)
    _ = t_g_done

    return _replay()


_NC_CACHE = [None]


def _get_nc():
    if _NC_CACHE[0] is None:
        _NC_CACHE[0] = build_program()
    return _NC_CACHE[0]


def host_in_maps(L0_attn_x, L0_attn_Wq, L0_attn_bq, L0_attn_Wk, L0_attn_bk,
                 L0_attn_Wv, L0_attn_bv, L0_attn_Wo, L0_attn_bo, L0_attn_head_mask,
                 L0_ffn_w_gate, L0_ffn_w_up, L0_ffn_w_down):
    x = np.asarray(L0_attn_x, dtype=np.float32)
    Wq = np.asarray(L0_attn_Wq, dtype=np.float32)
    Wk = np.asarray(L0_attn_Wk, dtype=np.float32)
    Wv = np.asarray(L0_attn_Wv, dtype=np.float32)
    Wo = np.asarray(L0_attn_Wo, dtype=np.float32)
    bq = np.asarray(L0_attn_bq, dtype=np.float32)
    bk = np.asarray(L0_attn_bk, dtype=np.float32)
    bv = np.asarray(L0_attn_bv, dtype=np.float32)
    bo = np.asarray(L0_attn_bo, dtype=np.float32)
    hm = np.asarray(L0_attn_head_mask, dtype=np.float32)
    wg = np.asarray(L0_ffn_w_gate, dtype=np.int32)
    wu = np.asarray(L0_ffn_w_up, dtype=np.int32)
    wd = np.asarray(L0_ffn_w_down, dtype=np.int32)

    C = np.ascontiguousarray
    wgu_h = np.empty((24, 128, 2, 8, 128), dtype=np.int32)
    wgu_h[:, :, 0] = wg.reshape(24, 128, 8, 128).transpose(0, 3, 2, 1)
    wgu_h[:, :, 1] = wu.reshape(24, 128, 8, 128).transpose(0, 3, 2, 1)
    wdT_h = C(wd.T)
    in_maps = []
    for c in range(8):
        b = c // 4
        g = c % 4
        qt = c % 4
        xb = x[b]
        xTb = C(xb.T)
        in_maps.append({
            "xT": xTb,
            "xoT": C(xb[qt * TOK:(qt + 1) * TOK].T),
            "xrows": C(xb[qt * TOK:(qt + 1) * TOK]),
            "wqT": C(Wq[g * DLOC:(g + 1) * DLOC, :].T),
            "wkT": C(Wk[g * DLOC:(g + 1) * DLOC, :].T),
            "wvT": C(Wv[g * DLOC:(g + 1) * DLOC, :].T),
            "woT": C(Wo[:, g * DLOC:(g + 1) * DLOC].T),
            "bqc": C(bq[g * DLOC:(g + 1) * DLOC].reshape(2, 128).T),
            "bkc": C(bk[g * DLOC:(g + 1) * DLOC].reshape(2, 128).T),
            "bvr": C(bv[g * DLOC:(g + 1) * DLOC].reshape(1, DLOC)),
            "bor": C(bo.reshape(1, D)),
            "hmr": C(hm[g * H_LOC:(g + 1) * H_LOC].reshape(1, H_LOC)),
            "wguT": wgu_h,
            "wdT": wdT_h,
        })
    return in_maps


def kernel(**inputs):
    in_maps = host_in_maps(**inputs)
    nc = _get_nc()
    res = run_bass_kernel_spmd(nc, in_maps, core_ids=list(range(8)))
    outp = np.empty((2, S, D), dtype=np.float32)
    for c in range(8):
        b = c // 4
        qt = c % 4
        outp[b, qt * TOK:(qt + 1) * TOK, :] = res.results[c]["out"]
    return outp


# version 19
# speedup vs baseline: 1.0780x; 1.0475x over previous
import numpy as np
import concourse.bass as bass
import concourse.mybir as mybir
from concourse.bass_utils import run_bass_kernel_spmd

F32 = mybir.dt.float32
BF16 = mybir.dt.bfloat16
I32 = mybir.dt.int32
I8 = mybir.dt.int8
AF = mybir.ActivationFunctionType
ALU = mybir.AluOpType
AX = mybir.AxisListType

D = 1024
S = 8192
H_LOC = 4
HD = 64
DLOC = H_LOC * HD
TOK = 2048
FH = 3072
CW = 256
QB = 512
NQB = S // QB
NKCH = S // 128
GRP = 3
EPS = 1e-5
DEQ = 1.0 / 127.0


class Ctr:
    def __init__(self, nc, name):
        self.h = nc.alloc_semaphore(name=name)
        self.n = 0


class EngQ:

    def __init__(self, self_ctr=None):
        self.fns = []
        self.self_ctr = self_ctr
        self.last_wait = {}

    def wait(self, ctr, val=None):
        v = ctr.n if val is None else val
        if v > 0 and v > self.last_wait.get(ctr.h.num, 0):
            self.last_wait[ctr.h.num] = v
            self.fns.append(lambda e, s=ctr.h, vv=v: e.wait_ge(s, vv))

    def emit(self, build, ctr=None, inc=1):
        if self.self_ctr is not None:
            prev = self.self_ctr.n
            if prev > 0:
                self.fns.append(lambda e, s=self.self_ctr.h, vv=prev: e.wait_ge(s, vv))
            if ctr is None:
                ctr, inc = self.self_ctr, 1
        if ctr is None:
            self.fns.append(build)
            return None
        ctr.n += inc
        tok = ctr.n

        def fn(e, b=build, s=ctr.h, k=inc):
            b(e).then_inc(s, k)

        self.fns.append(fn)
        return tok


STAGES = ["p0", "p1", "p1c", "p2", "p3", "cc", "p1b", "p5", "p6"]
LIMIT = "p6"


def build_program(limit=None):
    limit = LIMIT if limit is None else limit
    lim_i = STAGES.index(limit)

    def on(stage):
        return STAGES.index(stage) <= lim_i

    nc = bass.Bass("TRN2", target_bir_lowering=False)

    xT = nc.dram_tensor("xT", [D, S], F32, kind="ExternalInput")
    xoT = nc.dram_tensor("xoT", [D, TOK], F32, kind="ExternalInput")
    xrows = nc.dram_tensor("xrows", [TOK, D], F32, kind="ExternalInput")
    wqT = nc.dram_tensor("wqT", [D, DLOC], F32, kind="ExternalInput")
    wkT = nc.dram_tensor("wkT", [D, DLOC], F32, kind="ExternalInput")
    wvT = nc.dram_tensor("wvT", [D, DLOC], F32, kind="ExternalInput")
    woT = nc.dram_tensor("woT", [DLOC, D], F32, kind="ExternalInput")
    bqc = nc.dram_tensor("bqc", [128, 2], F32, kind="ExternalInput")
    bkc = nc.dram_tensor("bkc", [128, 2], F32, kind="ExternalInput")
    bvr = nc.dram_tensor("bvr", [1, DLOC], F32, kind="ExternalInput")
    bor = nc.dram_tensor("bor", [1, D], F32, kind="ExternalInput")
    hmr = nc.dram_tensor("hmr", [1, H_LOC], F32, kind="ExternalInput")
    wguT = nc.dram_tensor("wguT", [24, 128, 2, 8, 128], I8, kind="ExternalInput")
    wdT = nc.dram_tensor("wdT", [FH, D], I8, kind="ExternalInput")
    out = nc.dram_tensor("out", [TOK, D], F32, kind="ExternalOutput")
    cc_in = nc.dram_tensor("cc_in", [S, D], BF16)
    cc_out = nc.dram_tensor("cc_out", [TOK, D], BF16)
    mxd = nc.dram_tensor("mxd", [128, 1], F32)

    TEN = Ctr(nc, "TEN")
    ACT = Ctr(nc, "ACT")
    VEC = Ctr(nc, "VEC")
    POO = Ctr(nc, "POO")
    CCS = Ctr(nc, "CCS")
    PH = Ctr(nc, "PH")
    EPI = Ctr(nc, "EPI")
    SYC = Ctr(nc, "SYC")
    SXF = [Ctr(nc, f"SXF{i}") for i in range(2)]
    GXB = [Ctr(nc, f"GXB{i}") for i in range(2)]
    GW = Ctr(nc, "GW")
    GF = [Ctr(nc, f"GF{i}") for i in range(12)]
    SSP = [Ctr(nc, f"SSP{i}") for i in range(4)]
    SMX = Ctr(nc, "SMX")
    SS6 = [Ctr(nc, f"SS6{i}") for i in range(2)]
    SO6 = [Ctr(nc, f"SO6{i}") for i in range(2)]

    QSY, QT, QG = EngQ(), EngQ(), EngQ()
    QV, QA = EngQ(self_ctr=VEC), EngQ(self_ctr=ACT)
    allq = {"sync": QSY, "tensor": QT, "vector": QV, "scalar": QA, "gpsimd": QG}
    nbar = [0]

    def barrier():
        nbar[0] += 1
        tgt = 5 * nbar[0]
        for q in allq.values():
            q.emit(lambda e: e.nop(), PH)
            q.wait(PH, tgt)

    sb = lambda name, shape, dt=BF16: nc.alloc_sbuf_tensor(name, shape, dt)
    ps = lambda name, shape: nc.alloc_psum_tensor(name, shape, F32)

    def _replay():
        with nc.Block() as block:
            @block.sync
            def _(e):
                for fn in QSY.fns:
                    fn(e)

            @block.tensor
            def _(e):
                for fn in QT.fns:
                    fn(e)

            @block.vector
            def _(e):
                for fn in QV.fns:
                    fn(e)

            @block.scalar
            def _(e):
                for fn in QA.fns:
                    fn(e)

            @block.gpsimd
            def _(e):
                for fn in QG.fns:
                    fn(e)
        return nc

    def _truncate():
        od = nc.alloc_sbuf_tensor("o_dummy", [128, D], F32)
        tz = QV.emit(lambda e: e.memset(od[:, :], 0.0), VEC)
        QSY.wait(VEC, tz)
        for tt in range(TOK // 128):
            QSY.emit(lambda e, t2=tt: e.dma_start(
                out=out[t2 * 128:(t2 + 1) * 128, :], in_=od[:, :]), SO6[0], 16)
        QSY.wait(SO6[0], SO6[0].n)
        return _replay()

    ones_mean = sb("ones_mean", [128, 128])
    ones_col = sb("ones_col", [128, 1])
    ones_row = sb("ones_row", [1, 128], F32)
    sel33 = sb("sel33", [33, 128], F32)
    rb33 = sb("rb33", [33, 512], F32)
    bq_sb = sb("bq_sb", [128, 2], F32)
    bk_sb = sb("bk_sb", [128, 2], F32)
    bvrow = sb("bvrow", [1, DLOC], F32)
    borow = sb("borow", [1, D], F32)
    hm_sb = sb("hm_sb", [1, H_LOC], F32)
    bvrep = sb("bvrep", [128, DLOC], F32)
    borep = sb("borep", [128, D], F32)
    crep = sb("crep", [128, 1], F32)
    xnf = sb("xnf", [128, 8, TOK])
    xnf_flat = xnf
    sbuf_M0 = nc.sbuf_base
    kt = [sb(f"kt{i}", [128, S]) for i in range(2)]
    qt_ = [sb(f"qt{i}", [128, S]) for i in range(2)]
    vv = sb("vv", [128, NKCH, DLOC])
    sbuf_M1 = nc.sbuf_base

    QV.emit(lambda e: e.memset(ones_mean[:, :], 1.0 / 1024.0))
    QV.emit(lambda e: e.memset(ones_col[:, :], 1.0))
    QV.emit(lambda e: e.memset(ones_row[:, :], 1.0))
    QV.emit(lambda e: e.memset(sel33[:, :], 0.0))
    QV.emit(lambda e: e.memset(sel33[0:1, 0:64], 1.0))
    QV.emit(lambda e: e.memset(sel33[32:33, 64:128], 1.0))
    QV.emit(lambda e: e.memset(rb33[:, :], 0.0))
    QV.emit(lambda e: e.memset(crep[:, :], 0.0), VEC)
    v_consts = VEC.n

    for (dst, src) in ((bq_sb, bqc), (bk_sb, bkc), (bvrow, bvr),
                       (borow, bor), (hm_sb, hmr)):
        QSY.emit(lambda e, d=dst, s=src: e.dma_start(out=d[:, :], in_=s[:, :]), SYC, 16)
    sy_consts = SYC.n

    psum_base0 = nc.psum_base

    ps_rep = ps("ps_rep", [128, 512])
    QT.wait(SYC, sy_consts)
    QT.wait(VEC, v_consts)
    QT.emit(lambda e: e.matmul(ps_rep[:, 0:DLOC], ones_row[:, :], bvrow[:, :],
                               start=True, stop=True), TEN)
    QV.wait(TEN)
    t = QV.emit(lambda e: e.tensor_copy(bvrep[:, :], ps_rep[:, 0:DLOC]), VEC)
    for half in range(2):
        QT.wait(VEC, t)
        QT.emit(lambda e, h=half: e.matmul(ps_rep[:, :], ones_row[:, :],
                                           borow[:, h * 512:(h + 1) * 512],
                                           start=True, stop=True), TEN)
        QV.wait(TEN)
        t = QV.emit(lambda e, h=half: e.tensor_copy(borep[:, h * 512:(h + 1) * 512],
                                                    ps_rep[:, :]), VEC)
    barrier()
    nc.psum_base = psum_base0
    if not on("p1"):
        return _truncate()

    def ln_pipeline(tag, src_dram, n_chunks, rstd_bias, consumers):
        xf = [sb(f"xf_{tag}{i}", [128, 8, CW], F32) for i in range(2)]
        xb = [sb(f"xb_{tag}{i}", [128, 8, CW]) for i in range(2)]
        xq = [sb(f"xq_{tag}{i}", [128, 8, CW]) for i in range(2)]
        xn = [sb(f"xn_{tag}{i}", [128, 8, CW]) for i in range(2)]
        m_sb = [sb(f"m_{tag}{i}", [128, CW], F32) for i in range(2)]
        msq = [sb(f"msq_{tag}{i}", [128, CW], F32) for i in range(2)]
        var = [sb(f"var_{tag}{i}", [128, CW], F32) for i in range(2)]
        lnv = [sb(f"lnv_{tag}{i}", [128, CW], F32) for i in range(2)]
        rstd = [sb(f"rstd_{tag}{i}", [128, CW], F32) for i in range(2)]
        tmp = sb(f"tmp_{tag}", [128, 4, CW], F32)
        eps_col = sb(f"eps_{tag}", [128, 1], F32)
        rb_col = [sb(f"rbias_{tag}{i}", [128, 1], F32) for i in range(2)]
        QV.emit(lambda e: e.memset(eps_col[:, :], float(EPS)))
        QV.emit(lambda e: e.memset(rb_col[0][:, :], 0.0))
        t0m = QV.emit(lambda e: e.memset(rb_col[1][:, :], float(np.log(DEQ))), VEC)
        QA.wait(VEC, t0m)
        ps_m = [ps(f"ps_m_{tag}{i}", [128, CW]) for i in range(2)]
        ps_s = [ps(f"ps_s_{tag}{i}", [128, CW]) for i in range(2)]

        t_xf_cons = [0, 0]
        t_xb_cons_t = [0, 0]
        t_xb_cons_p = [0, 0]
        t_xq_cons = [0, 0]
        t_xn_cons = [0, 0]
        t_xn_cons_ctr = [None, None]
        t_stats_read2 = [0, 0]
        t_rstd_read = [0, 0]
        t_act_rows = [0, 0]
        pending_cons = []

        for ci in range(n_chunks):
            bi = ci % 2
            src = src_dram(ci)
            QSY.wait(VEC, t_xf_cons[bi])
            QSY.emit(lambda e, a=src, d=xf[bi]: e.dma_start(out=d[:, :, :], in_=a), SXF[bi], 16)
            t_xf = SXF[bi].n
            QG.wait(TEN, t_xb_cons_t[bi])
            QG.wait(ACT, t_xb_cons_p[bi])
            QG.emit(lambda e, a=src, d=xb[bi]: e.dma_start(out=d[:, :, :], in_=a), GXB[bi], 16)
            t_xb = GXB[bi].n

            QA.wait(GXB[bi], t_xb)
            QA.wait(TEN, t_xq_cons[bi])
            QA.emit(lambda e, b=bi: e.activation(
                xq[b][:, :, :], xb[b][:, :, :], AF.Square, bias=0.0, scale=1.0), ACT)
            t_xsq = ACT.n

            QT.wait(GXB[bi], t_xb)
            QT.wait(ACT, t_xsq)
            QT.wait(VEC, t_stats_read2[bi])
            for tt in range(8):
                QT.emit(lambda e, b=bi, m=tt: e.matmul(
                    ps_m[b][:, :], ones_mean[:, :], xb[b][:, m, :],
                    start=(m == 0), stop=(m == 7)), TEN)
            t_xb_cons_save = TEN.n
            for tt in range(8):
                QT.emit(lambda e, b=bi, m=tt: e.matmul(
                    ps_s[b][:, :], ones_mean[:, :], xq[b][:, m, :],
                    start=(m == 0), stop=(m == 7)), TEN)
            t_xb_cons_t[bi] = t_xb_cons_save
            t_xb_cons_p[bi] = t_xsq
            t_xq_cons[bi] = TEN.n
            t_stats = TEN.n

            QV.wait(TEN, t_stats)
            QV.wait(ACT, t_act_rows[bi])
            QV.emit(lambda e, b=bi: e.tensor_copy(m_sb[b][:, :], ps_m[b][:, :]), VEC)
            QV.emit(lambda e, b=bi: e.tensor_tensor(msq[b][:, :], m_sb[b][:, :], m_sb[b][:, :], ALU.mult), VEC)
            QV.emit(lambda e, b=bi: e.tensor_tensor(var[b][:, :], ps_s[b][:, :], msq[b][:, :], ALU.subtract), VEC)
            t_stats_read2[bi] = VEC.n
            QA.wait(VEC, t_stats_read2[bi])
            QA.wait(VEC, t_rstd_read[bi])
            QA.emit(lambda e, b=bi: e.activation(lnv[b][:, :], var[b][:, :], AF.Ln,
                                                 bias=eps_col[:, 0:1], scale=1.0), ACT)
            QA.emit(lambda e, b=bi, rc=rb_col[1 if rstd_bias(ci) else 0]: e.activation(
                rstd[b][:, :], lnv[b][:, :], AF.Exp,
                bias=rc[:, 0:1], scale=-0.5), ACT)
            t_act_rows[bi] = ACT.n

            QV.wait(SXF[bi], t_xf)
            QV.wait(ACT, t_act_rows[bi])
            if t_xn_cons_ctr[bi] is not None:
                QV.wait(t_xn_cons_ctr[bi], t_xn_cons[bi])
            for hh in range(2):
                QV.emit(lambda e, b=bi, h=hh: e.tensor_tensor(
                    tmp[:, :, :], xf[b][:, 4 * h:4 * h + 4, :],
                    m_sb[b][:, None, :].broadcast_to([128, 4, CW]), ALU.subtract), VEC)
                QV.emit(lambda e, b=bi, h=hh: e.tensor_tensor(
                    xn[b][:, 4 * h:4 * h + 4, :], tmp[:, :, :],
                    rstd[b][:, None, :].broadcast_to([128, 4, CW]), ALU.mult), VEC)
            t_xn = VEC.n
            t_xf_cons[bi] = t_xn
            t_rstd_read[bi] = t_xn

            pending_cons.append((ci, bi, xn[bi], t_xn))
            if len(pending_cons) > 1:
                pci, pbi, pxn, ptok = pending_cons.pop(0)
                ctr, tok = consumers(pci, pxn, ptok)
                t_xn_cons_ctr[pbi] = ctr
                t_xn_cons[pbi] = tok
            yield ci

        while pending_cons:
            pci, pbi, pxn, ptok = pending_cons.pop(0)
            ctr, tok = consumers(pci, pxn, ptok)
            t_xn_cons_ctr[pbi] = ctr
            t_xn_cons[pbi] = tok

    sbuf_p1 = nc.sbuf_base
    psum_p1 = nc.psum_base

    wq_sb = sb("wq_sb", [128, 8, DLOC])
    wk_sb = sb("wk_sb", [128, 8, DLOC])
    wv_sb = sb("wv_sb", [128, 8, DLOC])
    for (dst, srcw) in ((wq_sb, wqT), (wk_sb, wkT), (wv_sb, wvT)):
        QG.emit(lambda e, d=dst, s=srcw: e.dma_start(
            out=d[:, :, :], in_=s[:, :].rearrange("(t p) c -> p t c", p=128)), GW, 16)
    gp_w = GW.n

    ps_q = ps("ps_q", [128, 512])
    ps_k = ps("ps_k", [128, 512])
    ps_v = ps("ps_v", [128, 512])
    qkv_state = {"evac": 0, "evac_a": 0}

    def qkv_consumers(ci, xn, t_xn):
        QT.wait(VEC, t_xn)
        if ci == 0:
            QT.wait(GW, gp_w)
        QT.wait(VEC, qkv_state["evac"])
        QT.wait(ACT, qkv_state["evac_a"])
        for (pps, wsb) in ((ps_q, wq_sb), (ps_k, wk_sb)):
            for dt_ in range(2):
                for tt in range(8):
                    QT.emit(lambda e, p=pps, w=wsb, d=dt_, m=tt, b=xn: e.matmul(
                        p[:, d * 256:(d + 1) * 256],
                        w[:, m, d * 128:(d + 1) * 128],
                        b[:, m, :],
                        start=(m == 0), stop=(m == 7)), TEN)
        for s_ in range(2):
            for tt in range(8):
                QT.emit(lambda e, s2=s_, m=tt, b=xn: e.matmul(
                    ps_v[:, s2 * 256:(s2 + 1) * 256],
                    b[:, m, s2 * 128:(s2 + 1) * 128],
                    wv_sb[:, m, :],
                    start=(m == 0), stop=(m == 7)), TEN)
        t_mm = TEN.n
        QA.wait(TEN, t_mm)
        for dt_ in range(2):
            QA.emit(lambda e, d=dt_, c=ci: e.activation(
                qt_[d][:, c * CW:(c + 1) * CW], ps_q[:, d * 256:(d + 1) * 256],
                AF.Identity, bias=bq_sb[:, d:d + 1], scale=1.0), ACT)
            QA.emit(lambda e, d=dt_, c=ci: e.activation(
                kt[d][:, c * CW:(c + 1) * CW], ps_k[:, d * 256:(d + 1) * 256],
                AF.Identity, bias=bk_sb[:, d:d + 1], scale=1.0), ACT)
        qkv_state["evac_a"] = ACT.n
        QV.wait(TEN, t_mm)
        for s_ in range(2):
            QV.emit(lambda e, s2=s_, c=ci: e.tensor_tensor(
                vv[:, 2 * c + s2, :], ps_v[:, s2 * 256:(s2 + 1) * 256],
                bvrep[:, :], ALU.add), VEC)
        qkv_state["evac"] = VEC.n
        return TEN, t_mm

    NCH_B = S // CW

    def xnf_consumers(ci, xn, t_xn):
        c = ci - NCH_B
        for tt in range(8):
            QV.emit(lambda e, m=tt, cc2=c: e.tensor_copy(
                xnf[:, m, cc2 * CW:(cc2 + 1) * CW], xn[:, m, :]), VEC)
        return VEC, VEC.n

    def p1_src(ci):
        if ci < NCH_B:
            return xT[:, ci * CW:(ci + 1) * CW].rearrange("(t p) c -> p t c", p=128)
        c = ci - NCH_B
        return xoT[:, c * CW:(c + 1) * CW].rearrange("(t p) c -> p t c", p=128)

    def p1_consumers(ci, xn, t_xn):
        if ci < NCH_B:
            return qkv_consumers(ci, xn, t_xn)
        return xnf_consumers(ci, xn, t_xn)

    for _ in ln_pipeline(
            "a", p1_src, NCH_B + TOK // CW,
            lambda ci: ci >= NCH_B, p1_consumers):
        pass
    barrier()
    nc.sbuf_base = sbuf_M1
    nc.psum_base = psum_p1
    if not on("p1c"):
        return _truncate()

    ps_smp = [ps(f"ps_smp{i}", [128, 256]) for i in range(2)]
    runmax = sb("runmax", [128, 1], F32)
    colmax = sb("colmax", [128, 1], F32)
    rowback = sb("rowback", [1, 128], F32)
    cval = sb("cval", [1, 1], F32)
    t_red = 0
    for pr in range(2):
        for kb in range(NKCH // 2):
            QT.wait(VEC, t_red)
            for sub in range(2):
                k = 2 * kb + sub
                QT.emit(lambda e, p=pr, kk=k, s_=sub: e.matmul(
                    ps_smp[0][:, s_ * 128:(s_ + 1) * 128],
                    kt[p][0:64, kk * 128:(kk + 1) * 128],
                    qt_[p][0:64, 0:128],
                    start=True, stop=True, tile_position=(0, 0)), TEN)
                QT.emit(lambda e, p=pr, kk=k, s_=sub: e.matmul(
                    ps_smp[1][:, s_ * 128:(s_ + 1) * 128],
                    kt[p][64:128, kk * 128:(kk + 1) * 128],
                    qt_[p][64:128, 0:128],
                    start=True, stop=True, tile_position=(64, 0)), TEN)
            t_mm = TEN.n
            QV.wait(TEN, t_mm)
            for ii in range(2):
                QV.emit(lambda e, i=ii: e.reduce_max(colmax[:, :], ps_smp[i][:, :], AX.X), VEC)
                if pr == 0 and kb == 0 and ii == 0:
                    QV.emit(lambda e: e.tensor_copy(runmax[:, :], colmax[:, :]), VEC)
                else:
                    QV.emit(lambda e: e.tensor_tensor(runmax[:, :], runmax[:, :],
                                                      colmax[:, :], ALU.max), VEC)
            t_red = VEC.n
    QSY.wait(VEC, t_red)
    QSY.emit(lambda e: e.dma_start(out=mxd[:, :], in_=runmax[:, :]), SMX, 16)
    t_st = SMX.n
    QSY.wait(SMX, t_st)
    QSY.emit(lambda e: e.dma_start(
        out=rowback[:, :], in_=mxd[:, :].rearrange("p one -> one p")), SMX, 16)
    t_ld = SMX.n
    QV.wait(SMX, t_ld)
    QV.emit(lambda e: e.reduce_max(cval[:, :], rowback[:, :], AX.X), VEC)
    QV.emit(lambda e: e.tensor_scalar(cval[:, :], cval[:, :], -0.125, None, ALU.mult), VEC)
    t_c = VEC.n
    QT.wait(VEC, t_c)
    QT.emit(lambda e: e.matmul(ps_smp[0][:, 0:1], ones_row[:, :], cval[:, :],
                               start=True, stop=True), TEN)
    QV.wait(TEN)
    QV.emit(lambda e: e.tensor_copy(crep[:, :], ps_smp[0][:, 0:1]), VEC)
    barrier()
    nc.sbuf_base = sbuf_M1
    nc.psum_base = psum_p1
    if not on("p2"):
        return _truncate()

    ot = [sb(f"ot{i}", [128, S]) for i in range(2)]
    sbuf_M2 = nc.sbuf_base
    PBW = 6 * 512
    pbuf = [sb(f"pbuf{i}", [128, PBW]) for i in range(3)]
    acc6 = sb("acc6", [128, PBW])
    of_sb = sb("of_sb", [128, 512], F32)
    re_row = sb("re_row", [1, 512], F32)
    re2_row = sb("re2_row", [1, 512], F32)
    rep_sb = sb("rep_sb", [128, 512], F32)
    s_ps = [ps(f"s_ps{i}", [128, GRP * 512]) for i in range(2)]
    o_ps = ps("o_ps", [128, 1024])

    t_sfree = [0, 0]
    t_pv_pb = [0, 0, 0]
    t_add_pb = [0, 0, 0]
    t_of_copy = 0

    NT = 2 * NKCH
    NG = (NT + GRP - 1) // GRP
    pending_epi = [None]

    for qb in range(NQB):
        for pr in range(2):
            unit_idx = qb * 2 + pr
            grp_mm_tok = [0] * NG
            grp_exp_tok = [0] * NG
            slot_first_add = True

            def tloc(j):
                g = j // GRP
                return g, (j % GRP) * 512, (g // 2) % 3, (g % 2) * 1536

            pv_emitted = 0
            pv_enabled = [pending_epi[0] is None]

            def emit_pv_upto(gdone):
                nonlocal pv_emitted
                if not pv_enabled[0]:
                    return
                while pv_emitted < NKCH:
                    k = pv_emitted
                    jo = 2 * k + 1
                    gmax = jo // GRP
                    if gmax > gdone:
                        break
                    ge, ce, sle, bce = tloc(2 * k)
                    go, co, slo, bco = tloc(jo)
                    QT.wait(ACT, grp_exp_tok[gmax])
                    if k == 0:
                        QT.wait(EPI, unit_idx)
                    stf = (k == 0)
                    spf = (k == NKCH - 1)
                    QT.emit(lambda e, kk=k, p=pr, pi=sle, bc=bce + ce, s_=stf, f=spf: e.matmul(
                        o_ps[0:64, 0:512],
                        vv[:, kk, 2 * p * 64: 2 * p * 64 + 64],
                        pbuf[pi][:, bc:bc + 512],
                        start=s_, stop=f, tile_position=(0, 0)), TEN)
                    QT.emit(lambda e, kk=k, p=pr, pi=slo, bc=bco + co, s_=stf, f=spf: e.matmul(
                        o_ps[64:128, 512:1024],
                        vv[:, kk, (2 * p + 1) * 64: (2 * p + 1) * 64 + 64],
                        pbuf[pi][:, bc:bc + 512],
                        start=s_, stop=f, tile_position=(0, 64)), TEN)
                    t_pv_pb[sle] = TEN.n
                    t_pv_pb[slo] = TEN.n
                    pv_emitted += 1

            for j0 in range(0, NT, 2):
                kch = j0 // 2
                for (j, row0, tp) in ((j0, 0, (0, 0)), (j0 + 1, 64, (64, 0))):
                    g, col, slot, bufcol = tloc(j)
                    par = g % 2
                    if j % GRP == 0:
                        QT.wait(ACT, t_sfree[par])
                    QT.emit(lambda e, p=par, c=col, r=row0, k=kch, t=pr, q=qb, tp_=tp: e.matmul(
                        s_ps[p][:, c:c + 512],
                        kt[t][r:r + 64, k * 128:(k + 1) * 128],
                        qt_[t][r:r + 64, q * QB:(q + 1) * QB],
                        start=True, stop=True, tile_position=tp_), TEN)
                    grp_mm_tok[g] = TEN.n
                    if (j % GRP == GRP - 1) or (j == NT - 1):
                        width = col + 512
                        QA.wait(TEN, grp_mm_tok[g])
                        QA.wait(TEN, t_pv_pb[slot])
                        QA.wait(VEC, t_add_pb[slot])
                        QA.emit(lambda e, p=par, w=width, pi=slot, bc=bufcol: e.activation(
                            pbuf[pi][:, bc:bc + w], s_ps[p][:, 0:w], AF.Exp,
                            bias=crep[:, 0:1], scale=0.125), ACT)
                        grp_exp_tok[g] = ACT.n
                        t_sfree[par] = ACT.n
                        if (g % 2 == 1) or (j == NT - 1):
                            cols = bufcol + width
                            QV.wait(ACT, grp_exp_tok[g])
                            if slot_first_add:
                                QV.emit(lambda e, pi=slot, cl=cols: e.tensor_copy(
                                    acc6[:, 0:cl], pbuf[pi][:, 0:cl]), VEC)
                                slot_first_add = False
                            else:
                                QV.emit(lambda e, pi=slot, cl=cols: e.tensor_tensor(
                                    acc6[:, 0:cl], acc6[:, 0:cl], pbuf[pi][:, 0:cl],
                                    ALU.add), VEC)
                            t_add_pb[slot] = VEC.n
                        emit_pv_upto(g - 1)
                    if j == 2 and pending_epi[0] is not None:
                        pending_epi[0][0]()
                    if j == 8 and pending_epi[0] is not None:
                        pending_epi[0][1]()
                        pending_epi[0] = None
                        pv_enabled[0] = True
                        emit_pv_upto((j // GRP) - 1)

            emit_pv_upto(NG)

            QV.wait(TEN)
            QV.emit(lambda e: e.tensor_copy(of_sb[0:64, :], o_ps[0:64, 0:512]), VEC)
            QV.emit(lambda e: e.tensor_copy(of_sb[64:128, :], o_ps[64:128, 512:1024]), VEC)
            t_of_copy = VEC.n

            def mk_epilogue(pr=pr, qb=qb, t_of=t_of_copy):
                state = {}

                def flush_sums():
                    QT.wait(VEC, t_of)
                    for r6 in range(3):
                        QT.emit(lambda e, r=r6: e.matmul(
                            o_ps[0:1, 0:512], ones_col[:, :],
                            acc6[:, r * 1024:r * 1024 + 512],
                            start=(r == 0), stop=(r == 2), tile_position=(0, 0)), TEN)
                    for r6 in range(3):
                        QT.emit(lambda e, r=r6: e.matmul(
                            o_ps[32:33, 512:1024], ones_col[:, :],
                            acc6[:, r * 1024 + 512:r * 1024 + 1024],
                            start=(r == 0), stop=(r == 2), tile_position=(0, 32)), TEN)
                    t_sums = TEN.n

                    QV.wait(TEN, t_sums)
                    for (row, coff, hloc) in ((0, 0, 2 * pr), (32, 512, 2 * pr + 1)):
                        QV.emit(lambda e, r=row, c=coff: e.reciprocal(
                            re_row[:, :], o_ps[r:r + 1, c:c + 512]), VEC)
                        QV.emit(lambda e, h=hloc: e.tensor_scalar(
                            re2_row[:, :], re_row[:, :], hm_sb[:, h:h + 1], None,
                            ALU.mult), VEC)
                        QV.emit(lambda e, r=row: e.tensor_copy(
                            rb33[r:r + 1, :], re2_row[:, :]), VEC)
                    state["t_rb"] = VEC.n

                def flush_norm():
                    QT.wait(VEC, state["t_rb"])
                    QT.emit(lambda e: e.matmul(o_ps[:, 0:512], sel33[:, :], rb33[:, :],
                                               start=True, stop=True), TEN)
                    t_bcast = TEN.n

                    QV.wait(TEN, t_bcast)
                    epi_tok = QV.emit(lambda e: e.tensor_copy(
                        rep_sb[:, :], o_ps[:, 0:512]), EPI)
                    QV.wait(EPI, epi_tok)
                    QV.emit(lambda e, p=pr, q=qb: e.tensor_tensor(
                        ot[p][:, q * QB:(q + 1) * QB], of_sb[:, :], rep_sb[:, :],
                        ALU.mult), VEC)
                return (flush_sums, flush_norm)

            pending_epi[0] = mk_epilogue()

    pending_epi[0][0]()
    pending_epi[0][1]()
    pending_epi[0] = None
    barrier()
    nc.sbuf_base = sbuf_M2
    nc.psum_base = psum_p1
    if not on("p3"):
        return _truncate()

    wo_sb = sb("wo_sb", [128, 2, D])
    sp_sb = [sb(f"sp_sb{i}", [128, D]) for i in range(4)]
    ps_sp = [ps(f"ps_sp{i}", [128, D]) for i in range(4)]

    QG.emit(lambda e: e.dma_start(
        out=wo_sb[:, :, :], in_=woT[:, :].rearrange("(t p) c -> p t c", p=128)), GW, 16)
    gp_wo = GW.n

    t_sp_dma = [0] * 4
    t_sp_evac = [0] * 4
    t_sp_evac_a = [0] * 4
    QT.wait(GW, gp_wo)
    for tt in range(S // 128):
        bi = tt % 4
        QT.wait(VEC, t_sp_evac[bi])
        QT.wait(ACT, t_sp_evac_a[bi])
        for ch in range(2):
            for half in range(2):
                QT.emit(lambda e, t2=tt, c=ch, h=half, b=bi: e.matmul(
                    ps_sp[b][:, h * 512:(h + 1) * 512],
                    ot[c][:, t2 * 128:(t2 + 1) * 128],
                    wo_sb[:, c, h * 512:(h + 1) * 512],
                    start=(c == 0), stop=(c == 1)), TEN)
        t_mm = TEN.n
        QV.wait(TEN, t_mm)
        QV.wait(SSP[bi], t_sp_dma[bi])
        QV.emit(lambda e, b=bi: e.tensor_copy(sp_sb[b][:, 0:512], ps_sp[b][:, 0:512]), VEC)
        t_ev_v = VEC.n
        QA.wait(TEN, t_mm)
        QA.wait(SSP[bi], t_sp_dma[bi])
        QA.emit(lambda e, b=bi: e.activation(
            sp_sb[b][:, 512:1024], ps_sp[b][:, 512:1024], AF.Identity,
            bias=0.0, scale=1.0), ACT)
        t_ev_a = ACT.n
        t_sp_evac[bi] = t_ev_v
        t_sp_evac_a[bi] = t_ev_a
        QSY.wait(VEC, t_ev_v)
        QSY.wait(ACT, t_ev_a)
        QSY.emit(lambda e, b=bi, t2=tt: e.dma_start(
            out=cc_in[t2 * 128:(t2 + 1) * 128, :], in_=sp_sb[b][:, :]), SSP[bi], 16)
        t_sp_dma[bi] = SSP[bi].n

    if not on("cc"):
        return _truncate()
    barrier()
    nc.sbuf_base = sbuf_M0
    nc.psum_base = psum_p1

    g_sb = sb("g_sb", [128, 16, D])
    sbuf_M5 = nc.sbuf_base
    wgu_sb = [sb(f"wgu_sb{i}", [128, 2, 8, 128]) for i in range(12)]
    wd_all = sb("wd_all", [128, 24, D])
    sg_sb = sb("sg_sb", [128, 512], F32)
    hp = sb("hp", [128, 24, 512])
    ps_g = ps("ps_g", [128, 512])
    ps_u = ps("ps_u", [128, 512])
    ps_d = [ps(f"ps_d{i}", [128, D]) for i in range(2)]

    for _s in SSP:
        QG.wait(_s, _s.n)
    QG.emit(lambda e: e.dma_start(
        out=wd_all[:, :, :], in_=wdT[:, :].rearrange("(t p) c -> p t c", p=128)), GW, 16)
    gp_wd = GW.n
    t_wgu_tok = [0] * 12
    for h in range(12):
        QG.emit(lambda e, b=h: e.dma_start(
            out=wgu_sb[b][:, :, :, :], in_=wguT[b, :, :, :, :]), GF[h], 16)
        t_wgu_tok[h] = GF[h].n
    QG.emit(lambda e: e.collective_compute(
        "ReduceScatter", ALU.add,
        replica_groups=[[0, 1, 2, 3], [4, 5, 6, 7]],
        ins=[cc_in[:, :]], outs=[cc_out[:, :]]), CCS)
    if not on("p5"):
        return _truncate()

    t_wgu_used = [0] * 12
    t_gate_act = 0
    t_hpm_last = 0
    t_hp_used = 0
    t_g_act = [0, 0]

    for tokb in range(4):
        for hc in range(24):
            bi = hc % 12
            if tokb == 0 and hc < 12:
                t_wpair = t_wgu_tok[bi]
            else:
                QG.wait(TEN, t_wgu_used[bi])
                QG.emit(lambda e, b=bi, h=hc: e.dma_start(
                    out=wgu_sb[b][:, :, :, :], in_=wguT[h, :, :, :, :]), GF[bi], 16)
                t_wpair = GF[bi].n

            QT.wait(GF[bi], t_wpair)
            QT.wait(ACT, t_gate_act)
            for m in range(8):
                QT.emit(lambda e, b=bi, mm=m, tb=tokb: e.matmul(
                    ps_g[:, :], wgu_sb[b][:, 0, mm, :],
                    xnf_flat[:, mm, tb * 512:(tb + 1) * 512],
                    start=(mm == 0), stop=(mm == 7)), TEN)
            t_gmm = TEN.n
            QT.wait(VEC, t_hpm_last)
            for m in range(8):
                QT.emit(lambda e, b=bi, mm=m, tb=tokb: e.matmul(
                    ps_u[:, :], wgu_sb[b][:, 1, mm, :],
                    xnf_flat[:, mm, tb * 512:(tb + 1) * 512],
                    start=(mm == 0), stop=(mm == 7)), TEN)
            t_wgu_used[bi] = TEN.n
            t_umm = TEN.n

            QA.wait(TEN, t_gmm)
            QA.wait(VEC, t_hpm_last)
            QA.emit(lambda e: e.activation(sg_sb[:, :], ps_g[:, :], AF.Silu,
                                           bias=0.0, scale=1.0), ACT)
            t_gate_act = ACT.n
            QV.wait(ACT, t_gate_act)
            QV.wait(TEN, t_umm)
            QV.wait(TEN, t_hp_used)
            QV.emit(lambda e, h=hc: e.tensor_tensor(
                hp[:, h, :], sg_sb[:, :], ps_u[:, :], ALU.mult), VEC)
            t_hpm_last = VEC.n

        t_hp_all = VEC.n
        QT.wait(VEC, t_hp_all)
        if tokb == 0:
            QT.wait(GW, gp_wd)
        for st128 in range(4):
            bi = st128 % 2
            QT.wait(ACT, t_g_act[bi])
            for hc in range(24):
                for half in range(2):
                    QT.emit(lambda e, b=bi, h=hc, s_=st128, hf=half: e.matmul(
                        ps_d[b][:, hf * 512:(hf + 1) * 512],
                        hp[:, h, s_ * 128:(s_ + 1) * 128],
                        wd_all[:, h, hf * 512:(hf + 1) * 512],
                        start=(h == 0), stop=(h == 23)), TEN)
            t_dmm = TEN.n
            QA.wait(TEN, t_dmm)
            QA.emit(lambda e, b=bi, tb=tokb, s_=st128: e.activation(
                g_sb[:, tb * 4 + s_, :], ps_d[b][:, :], AF.Silu,
                bias=0.0, scale=DEQ), ACT)
            t_g_act[bi] = ACT.n
        t_hp_used = TEN.n

    t_g_done = max(t_g_act)
    barrier()
    nc.sbuf_base = sbuf_M5
    nc.psum_base = psum_p1
    if not on("p6"):
        return _truncate()

    s_tile = [sb(f"s_tile{i}", [128, D]) for i in range(2)]
    x_tile = [sb(f"x_tile{i}", [128, D], F32) for i in range(2)]
    o_tile = [sb(f"o_tile{i}", [128, D], F32) for i in range(2)]
    c_tmp = [sb(f"c_tmp{i}", [128, D], F32) for i in range(2)]
    d_tmp = [sb(f"d_tmp{i}", [128, D], F32) for i in range(2)]

    QSY.wait(CCS, 1)
    t_used = [0, 0]
    t_osent = [0, 0]
    for tt in range(TOK // 128):
        bi = tt % 2
        QSY.wait(VEC, t_used[bi])
        QSY.emit(lambda e, b=bi, t2=tt: e.dma_start(
            out=s_tile[b][:, :], in_=cc_out[t2 * 128:(t2 + 1) * 128, :]), SS6[bi], 16)
        QSY.emit(lambda e, b=bi, t2=tt: e.dma_start(
            out=x_tile[b][:, :], in_=xrows[t2 * 128:(t2 + 1) * 128, :]), SS6[bi], 16)
        t_ld = SS6[bi].n
        QV.wait(SS6[bi], t_ld)
        QV.wait(SO6[bi], t_osent[bi])
        QV.emit(lambda e, b=bi: e.tensor_tensor(
            c_tmp[b][:, :], s_tile[b][:, :], borep[:, :], ALU.add), VEC)
        QV.emit(lambda e, b=bi, t2=tt: e.tensor_tensor(
            d_tmp[b][:, :], c_tmp[b][:, :], g_sb[:, t2, :], ALU.mult), VEC)
        QV.emit(lambda e, b=bi: e.tensor_tensor(
            o_tile[b][:, :], d_tmp[b][:, :], x_tile[b][:, :], ALU.add), VEC)
        t_used[bi] = VEC.n
        QSY.wait(VEC, t_used[bi])
        QSY.emit(lambda e, b=bi, t2=tt: e.dma_start(
            out=out[t2 * 128:(t2 + 1) * 128, :], in_=o_tile[b][:, :]), SO6[bi], 16)
        t_osent[bi] = SO6[bi].n

    QSY.wait(SO6[0], SO6[0].n)
    QSY.wait(SO6[1], SO6[1].n)
    _ = t_g_done

    return _replay()


_NC_CACHE = [None]


def _get_nc():
    if _NC_CACHE[0] is None:
        _NC_CACHE[0] = build_program()
    return _NC_CACHE[0]


def host_in_maps(L0_attn_x, L0_attn_Wq, L0_attn_bq, L0_attn_Wk, L0_attn_bk,
                 L0_attn_Wv, L0_attn_bv, L0_attn_Wo, L0_attn_bo, L0_attn_head_mask,
                 L0_ffn_w_gate, L0_ffn_w_up, L0_ffn_w_down):
    x = np.asarray(L0_attn_x, dtype=np.float32)
    Wq = np.asarray(L0_attn_Wq, dtype=np.float32)
    Wk = np.asarray(L0_attn_Wk, dtype=np.float32)
    Wv = np.asarray(L0_attn_Wv, dtype=np.float32)
    Wo = np.asarray(L0_attn_Wo, dtype=np.float32)
    bq = np.asarray(L0_attn_bq, dtype=np.float32)
    bk = np.asarray(L0_attn_bk, dtype=np.float32)
    bv = np.asarray(L0_attn_bv, dtype=np.float32)
    bo = np.asarray(L0_attn_bo, dtype=np.float32)
    hm = np.asarray(L0_attn_head_mask, dtype=np.float32)
    wg = np.asarray(L0_ffn_w_gate, dtype=np.int32)
    wu = np.asarray(L0_ffn_w_up, dtype=np.int32)
    wd = np.asarray(L0_ffn_w_down, dtype=np.int32)

    C = np.ascontiguousarray
    wgu_h = np.empty((24, 128, 2, 8, 128), dtype=np.int8)
    wgu_h[:, :, 0] = wg.reshape(24, 128, 8, 128).transpose(0, 3, 2, 1)
    wgu_h[:, :, 1] = wu.reshape(24, 128, 8, 128).transpose(0, 3, 2, 1)
    wdT_h = C(wd.T.astype(np.int8))
    in_maps = []
    for c in range(8):
        b = c // 4
        g = c % 4
        qt = c % 4
        xb = x[b]
        xTb = C(xb.T)
        in_maps.append({
            "xT": xTb,
            "xoT": C(xb[qt * TOK:(qt + 1) * TOK].T),
            "xrows": C(xb[qt * TOK:(qt + 1) * TOK]),
            "wqT": C(Wq[g * DLOC:(g + 1) * DLOC, :].T),
            "wkT": C(Wk[g * DLOC:(g + 1) * DLOC, :].T),
            "wvT": C(Wv[g * DLOC:(g + 1) * DLOC, :].T),
            "woT": C(Wo[:, g * DLOC:(g + 1) * DLOC].T),
            "bqc": C(bq[g * DLOC:(g + 1) * DLOC].reshape(2, 128).T),
            "bkc": C(bk[g * DLOC:(g + 1) * DLOC].reshape(2, 128).T),
            "bvr": C(bv[g * DLOC:(g + 1) * DLOC].reshape(1, DLOC)),
            "bor": C(bo.reshape(1, D)),
            "hmr": C(hm[g * H_LOC:(g + 1) * H_LOC].reshape(1, H_LOC)),
            "wguT": wgu_h,
            "wdT": wdT_h,
        })
    return in_maps


def kernel(**inputs):
    in_maps = host_in_maps(**inputs)
    nc = _get_nc()
    res = run_bass_kernel_spmd(nc, in_maps, core_ids=list(range(8)))
    outp = np.empty((2, S, D), dtype=np.float32)
    for c in range(8):
        b = c // 4
        qt = c % 4
        outp[b, qt * TOK:(qt + 1) * TOK, :] = res.results[c]["out"]
    return outp


# version 20
# speedup vs baseline: 1.1034x; 1.0235x over previous
import numpy as np
import concourse.bass as bass
import concourse.mybir as mybir
from concourse.bass_utils import run_bass_kernel_spmd

F32 = mybir.dt.float32
BF16 = mybir.dt.bfloat16
I32 = mybir.dt.int32
I8 = mybir.dt.int8
AF = mybir.ActivationFunctionType
ALU = mybir.AluOpType
AX = mybir.AxisListType

D = 1024
S = 8192
H_LOC = 4
HD = 64
DLOC = H_LOC * HD
TOK = 2048
FH = 3072
CW = 256
NWGU = 8
QB = 512
NQB = S // QB
NKCH = S // 128
GRP = 3
EPS = 1e-5
DEQ = 1.0 / 127.0


class Ctr:
    def __init__(self, nc, name):
        self.h = nc.alloc_semaphore(name=name)
        self.n = 0


class EngQ:

    def __init__(self, self_ctr=None):
        self.fns = []
        self.self_ctr = self_ctr
        self.last_wait = {}

    def wait(self, ctr, val=None):
        v = ctr.n if val is None else val
        if v > 0 and v > self.last_wait.get(ctr.h.num, 0):
            self.last_wait[ctr.h.num] = v
            self.fns.append(lambda e, s=ctr.h, vv=v: e.wait_ge(s, vv))

    def emit(self, build, ctr=None, inc=1):
        if self.self_ctr is not None:
            prev = self.self_ctr.n
            if prev > 0:
                self.fns.append(lambda e, s=self.self_ctr.h, vv=prev: e.wait_ge(s, vv))
            if ctr is None:
                ctr, inc = self.self_ctr, 1
        if ctr is None:
            self.fns.append(build)
            return None
        ctr.n += inc
        tok = ctr.n

        def fn(e, b=build, s=ctr.h, k=inc):
            b(e).then_inc(s, k)

        self.fns.append(fn)
        return tok


STAGES = ["p0", "p1", "p1c", "p2", "p3", "cc", "p1b", "p5", "p6"]
LIMIT = "p6"


def build_program(limit=None):
    limit = LIMIT if limit is None else limit
    lim_i = STAGES.index(limit)

    def on(stage):
        return STAGES.index(stage) <= lim_i

    nc = bass.Bass("TRN2", target_bir_lowering=False)

    xT = nc.dram_tensor("xT", [D, S], F32, kind="ExternalInput")
    xoT = nc.dram_tensor("xoT", [D, TOK], F32, kind="ExternalInput")
    xrows = nc.dram_tensor("xrows", [TOK, D], F32, kind="ExternalInput")
    wqT = nc.dram_tensor("wqT", [D, DLOC], F32, kind="ExternalInput")
    wkT = nc.dram_tensor("wkT", [D, DLOC], F32, kind="ExternalInput")
    wvT = nc.dram_tensor("wvT", [D, DLOC], F32, kind="ExternalInput")
    woT = nc.dram_tensor("woT", [DLOC, D], F32, kind="ExternalInput")
    bqc = nc.dram_tensor("bqc", [128, 2], F32, kind="ExternalInput")
    bkc = nc.dram_tensor("bkc", [128, 2], F32, kind="ExternalInput")
    bvr = nc.dram_tensor("bvr", [1, DLOC], F32, kind="ExternalInput")
    bor = nc.dram_tensor("bor", [1, D], F32, kind="ExternalInput")
    hmr = nc.dram_tensor("hmr", [1, H_LOC], F32, kind="ExternalInput")
    wguT = nc.dram_tensor("wguT", [24, 128, 2, 8, 128], I8, kind="ExternalInput")
    wdT = nc.dram_tensor("wdT", [FH, D], I8, kind="ExternalInput")
    out = nc.dram_tensor("out", [TOK, D], F32, kind="ExternalOutput")
    cc_in = nc.dram_tensor("cc_in", [S, D], BF16)
    cc_out = nc.dram_tensor("cc_out", [TOK, D], BF16)
    mxd = nc.dram_tensor("mxd", [128, 1], F32)

    TEN = Ctr(nc, "TEN")
    ACT = Ctr(nc, "ACT")
    VEC = Ctr(nc, "VEC")
    POO = Ctr(nc, "POO")
    CCS = Ctr(nc, "CCS")
    PH = Ctr(nc, "PH")
    EPI = Ctr(nc, "EPI")
    SYC = Ctr(nc, "SYC")
    SXF = [Ctr(nc, f"SXF{i}") for i in range(2)]
    GXB = [Ctr(nc, f"GXB{i}") for i in range(2)]
    GW = Ctr(nc, "GW")
    GF = [Ctr(nc, f"GF{i}") for i in range(12)]
    SSP = [Ctr(nc, f"SSP{i}") for i in range(4)]
    SMX = Ctr(nc, "SMX")
    SS6 = [Ctr(nc, f"SS6{i}") for i in range(2)]
    SO6 = [Ctr(nc, f"SO6{i}") for i in range(2)]

    QSY, QT, QG = EngQ(), EngQ(), EngQ()
    QV, QA = EngQ(self_ctr=VEC), EngQ(self_ctr=ACT)
    allq = {"sync": QSY, "tensor": QT, "vector": QV, "scalar": QA, "gpsimd": QG}
    nbar = [0]

    def barrier():
        nbar[0] += 1
        tgt = 5 * nbar[0]
        for q in allq.values():
            q.emit(lambda e: e.nop(), PH)
            q.wait(PH, tgt)

    sb = lambda name, shape, dt=BF16: nc.alloc_sbuf_tensor(name, shape, dt)
    ps = lambda name, shape: nc.alloc_psum_tensor(name, shape, F32)

    def _replay():
        with nc.Block() as block:
            @block.sync
            def _(e):
                for fn in QSY.fns:
                    fn(e)

            @block.tensor
            def _(e):
                for fn in QT.fns:
                    fn(e)

            @block.vector
            def _(e):
                for fn in QV.fns:
                    fn(e)

            @block.scalar
            def _(e):
                for fn in QA.fns:
                    fn(e)

            @block.gpsimd
            def _(e):
                for fn in QG.fns:
                    fn(e)
        return nc

    def _truncate():
        od = nc.alloc_sbuf_tensor("o_dummy", [128, D], F32)
        tz = QV.emit(lambda e: e.memset(od[:, :], 0.0), VEC)
        QSY.wait(VEC, tz)
        for tt in range(TOK // 128):
            QSY.emit(lambda e, t2=tt: e.dma_start(
                out=out[t2 * 128:(t2 + 1) * 128, :], in_=od[:, :]), SO6[0], 16)
        QSY.wait(SO6[0], SO6[0].n)
        return _replay()

    ones_mean = sb("ones_mean", [128, 128])
    ones_col = sb("ones_col", [128, 1])
    ones_row = sb("ones_row", [1, 128], F32)
    sel33 = sb("sel33", [33, 128], F32)
    rb33 = sb("rb33", [33, 512], F32)
    bq_sb = sb("bq_sb", [128, 2], F32)
    bk_sb = sb("bk_sb", [128, 2], F32)
    bvrow = sb("bvrow", [1, DLOC], F32)
    borow = sb("borow", [1, D], F32)
    hm_sb = sb("hm_sb", [1, H_LOC], F32)
    bvrep = sb("bvrep", [128, DLOC], F32)
    borep = sb("borep", [128, D], F32)
    crep = sb("crep", [128, 1], F32)
    xnf = sb("xnf", [128, 8, TOK])
    xnf_flat = xnf
    sbuf_M0 = nc.sbuf_base
    kt = [sb(f"kt{i}", [128, S]) for i in range(2)]
    qt_ = [sb(f"qt{i}", [128, S]) for i in range(2)]
    vv = sb("vv", [128, NKCH, DLOC])
    sbuf_M1 = nc.sbuf_base

    QV.emit(lambda e: e.memset(ones_mean[:, :], 1.0 / 1024.0))
    QV.emit(lambda e: e.memset(ones_col[:, :], 1.0))
    QV.emit(lambda e: e.memset(ones_row[:, :], 1.0))
    QV.emit(lambda e: e.memset(sel33[:, :], 0.0))
    QV.emit(lambda e: e.memset(sel33[0:1, 0:64], 1.0))
    QV.emit(lambda e: e.memset(sel33[32:33, 64:128], 1.0))
    QV.emit(lambda e: e.memset(rb33[:, :], 0.0))
    QV.emit(lambda e: e.memset(crep[:, :], 0.0), VEC)
    v_consts = VEC.n

    for (dst, src) in ((bq_sb, bqc), (bk_sb, bkc), (bvrow, bvr),
                       (borow, bor), (hm_sb, hmr)):
        QSY.emit(lambda e, d=dst, s=src: e.dma_start(out=d[:, :], in_=s[:, :]), SYC, 16)
    sy_consts = SYC.n

    psum_base0 = nc.psum_base

    ps_rep = ps("ps_rep", [128, 512])
    QT.wait(SYC, sy_consts)
    QT.wait(VEC, v_consts)
    QT.emit(lambda e: e.matmul(ps_rep[:, 0:DLOC], ones_row[:, :], bvrow[:, :],
                               start=True, stop=True), TEN)
    QV.wait(TEN)
    t = QV.emit(lambda e: e.tensor_copy(bvrep[:, :], ps_rep[:, 0:DLOC]), VEC)
    for half in range(2):
        QT.wait(VEC, t)
        QT.emit(lambda e, h=half: e.matmul(ps_rep[:, :], ones_row[:, :],
                                           borow[:, h * 512:(h + 1) * 512],
                                           start=True, stop=True), TEN)
        QV.wait(TEN)
        t = QV.emit(lambda e, h=half: e.tensor_copy(borep[:, h * 512:(h + 1) * 512],
                                                    ps_rep[:, :]), VEC)
    barrier()
    nc.psum_base = psum_base0
    if not on("p1"):
        return _truncate()

    def ln_pipeline(tag, src_dram, n_chunks, rstd_bias, consumers):
        xf = [sb(f"xf_{tag}{i}", [128, 8, CW], F32) for i in range(2)]
        xb = [sb(f"xb_{tag}{i}", [128, 8, CW]) for i in range(2)]
        xq = [sb(f"xq_{tag}{i}", [128, 8, CW]) for i in range(2)]
        xn = [sb(f"xn_{tag}{i}", [128, 8, CW]) for i in range(2)]
        m_sb = [sb(f"m_{tag}{i}", [128, CW], F32) for i in range(2)]
        msq = [sb(f"msq_{tag}{i}", [128, CW], F32) for i in range(2)]
        var = [sb(f"var_{tag}{i}", [128, CW], F32) for i in range(2)]
        lnv = [sb(f"lnv_{tag}{i}", [128, CW], F32) for i in range(2)]
        rstd = [sb(f"rstd_{tag}{i}", [128, CW], F32) for i in range(2)]
        tmp = sb(f"tmp_{tag}", [128, 4, CW], F32)
        eps_col = sb(f"eps_{tag}", [128, 1], F32)
        rb_col = [sb(f"rbias_{tag}{i}", [128, 1], F32) for i in range(2)]
        QV.emit(lambda e: e.memset(eps_col[:, :], float(EPS)))
        QV.emit(lambda e: e.memset(rb_col[0][:, :], 0.0))
        t0m = QV.emit(lambda e: e.memset(rb_col[1][:, :], float(np.log(DEQ))), VEC)
        QA.wait(VEC, t0m)
        ps_m = [ps(f"ps_m_{tag}{i}", [128, CW]) for i in range(2)]
        ps_s = [ps(f"ps_s_{tag}{i}", [128, CW]) for i in range(2)]

        t_xf_cons = [0, 0]
        t_xb_cons_t = [0, 0]
        t_xb_cons_p = [0, 0]
        t_xq_cons = [0, 0]
        t_xn_cons = [0, 0]
        t_xn_cons_ctr = [None, None]
        t_stats_read2 = [0, 0]
        t_rstd_read = [0, 0]
        t_act_rows = [0, 0]
        pending_cons = []

        for ci in range(n_chunks):
            bi = ci % 2
            src = src_dram(ci)
            QSY.wait(VEC, t_xf_cons[bi])
            QSY.emit(lambda e, a=src, d=xf[bi]: e.dma_start(out=d[:, :, :], in_=a), SXF[bi], 16)
            t_xf = SXF[bi].n
            QG.wait(TEN, t_xb_cons_t[bi])
            QG.wait(ACT, t_xb_cons_p[bi])
            QG.emit(lambda e, a=src, d=xb[bi]: e.dma_start(out=d[:, :, :], in_=a), GXB[bi], 16)
            t_xb = GXB[bi].n

            QA.wait(GXB[bi], t_xb)
            QA.wait(TEN, t_xq_cons[bi])
            QA.emit(lambda e, b=bi: e.activation(
                xq[b][:, :, :], xb[b][:, :, :], AF.Square, bias=0.0, scale=1.0), ACT)
            t_xsq = ACT.n

            QT.wait(GXB[bi], t_xb)
            QT.wait(ACT, t_xsq)
            QT.wait(VEC, t_stats_read2[bi])
            for tt in range(8):
                QT.emit(lambda e, b=bi, m=tt: e.matmul(
                    ps_m[b][:, :], ones_mean[:, :], xb[b][:, m, :],
                    start=(m == 0), stop=(m == 7)), TEN)
            t_xb_cons_save = TEN.n
            for tt in range(8):
                QT.emit(lambda e, b=bi, m=tt: e.matmul(
                    ps_s[b][:, :], ones_mean[:, :], xq[b][:, m, :],
                    start=(m == 0), stop=(m == 7)), TEN)
            t_xb_cons_t[bi] = t_xb_cons_save
            t_xb_cons_p[bi] = t_xsq
            t_xq_cons[bi] = TEN.n
            t_stats = TEN.n

            QV.wait(TEN, t_stats)
            QV.wait(ACT, t_act_rows[bi])
            QV.emit(lambda e, b=bi: e.tensor_copy(m_sb[b][:, :], ps_m[b][:, :]), VEC)
            QV.emit(lambda e, b=bi: e.tensor_tensor(msq[b][:, :], m_sb[b][:, :], m_sb[b][:, :], ALU.mult), VEC)
            QV.emit(lambda e, b=bi: e.tensor_tensor(var[b][:, :], ps_s[b][:, :], msq[b][:, :], ALU.subtract), VEC)
            t_stats_read2[bi] = VEC.n
            QA.wait(VEC, t_stats_read2[bi])
            QA.wait(VEC, t_rstd_read[bi])
            QA.emit(lambda e, b=bi: e.activation(lnv[b][:, :], var[b][:, :], AF.Ln,
                                                 bias=eps_col[:, 0:1], scale=1.0), ACT)
            QA.emit(lambda e, b=bi, rc=rb_col[1 if rstd_bias(ci) else 0]: e.activation(
                rstd[b][:, :], lnv[b][:, :], AF.Exp,
                bias=rc[:, 0:1], scale=-0.5), ACT)
            t_act_rows[bi] = ACT.n

            QV.wait(SXF[bi], t_xf)
            QV.wait(ACT, t_act_rows[bi])
            if t_xn_cons_ctr[bi] is not None:
                QV.wait(t_xn_cons_ctr[bi], t_xn_cons[bi])
            for hh in range(2):
                QV.emit(lambda e, b=bi, h=hh: e.tensor_tensor(
                    tmp[:, :, :], xf[b][:, 4 * h:4 * h + 4, :],
                    m_sb[b][:, None, :].broadcast_to([128, 4, CW]), ALU.subtract), VEC)
                QV.emit(lambda e, b=bi, h=hh: e.tensor_tensor(
                    xn[b][:, 4 * h:4 * h + 4, :], tmp[:, :, :],
                    rstd[b][:, None, :].broadcast_to([128, 4, CW]), ALU.mult), VEC)
            t_xn = VEC.n
            t_xf_cons[bi] = t_xn
            t_rstd_read[bi] = t_xn

            pending_cons.append((ci, bi, xn[bi], t_xn))
            if len(pending_cons) > 1:
                pci, pbi, pxn, ptok = pending_cons.pop(0)
                ctr, tok = consumers(pci, pxn, ptok)
                t_xn_cons_ctr[pbi] = ctr
                t_xn_cons[pbi] = tok
            yield ci

        while pending_cons:
            pci, pbi, pxn, ptok = pending_cons.pop(0)
            ctr, tok = consumers(pci, pxn, ptok)
            t_xn_cons_ctr[pbi] = ctr
            t_xn_cons[pbi] = tok

    sbuf_p1 = nc.sbuf_base
    psum_p1 = nc.psum_base

    wq_sb = sb("wq_sb", [128, 8, DLOC])
    wk_sb = sb("wk_sb", [128, 8, DLOC])
    wv_sb = sb("wv_sb", [128, 8, DLOC])
    for (dst, srcw) in ((wq_sb, wqT), (wk_sb, wkT), (wv_sb, wvT)):
        QG.emit(lambda e, d=dst, s=srcw: e.dma_start(
            out=d[:, :, :], in_=s[:, :].rearrange("(t p) c -> p t c", p=128)), GW, 16)
    gp_w = GW.n

    ps_q = ps("ps_q", [128, 512])
    ps_k = ps("ps_k", [128, 512])
    ps_v = ps("ps_v", [128, 512])
    qkv_state = {"evac": 0, "evac_a": 0}

    def qkv_consumers(ci, xn, t_xn):
        QT.wait(VEC, t_xn)
        if ci == 0:
            QT.wait(GW, gp_w)
        QT.wait(VEC, qkv_state["evac"])
        QT.wait(ACT, qkv_state["evac_a"])
        for (pps, wsb) in ((ps_q, wq_sb), (ps_k, wk_sb)):
            for dt_ in range(2):
                for tt in range(8):
                    QT.emit(lambda e, p=pps, w=wsb, d=dt_, m=tt, b=xn: e.matmul(
                        p[:, d * 256:(d + 1) * 256],
                        w[:, m, d * 128:(d + 1) * 128],
                        b[:, m, :],
                        start=(m == 0), stop=(m == 7)), TEN)
        for s_ in range(2):
            for tt in range(8):
                QT.emit(lambda e, s2=s_, m=tt, b=xn: e.matmul(
                    ps_v[:, s2 * 256:(s2 + 1) * 256],
                    b[:, m, s2 * 128:(s2 + 1) * 128],
                    wv_sb[:, m, :],
                    start=(m == 0), stop=(m == 7)), TEN)
        t_mm = TEN.n
        QA.wait(TEN, t_mm)
        for dt_ in range(2):
            QA.emit(lambda e, d=dt_, c=ci: e.activation(
                qt_[d][:, c * CW:(c + 1) * CW], ps_q[:, d * 256:(d + 1) * 256],
                AF.Identity, bias=bq_sb[:, d:d + 1], scale=1.0), ACT)
            QA.emit(lambda e, d=dt_, c=ci: e.activation(
                kt[d][:, c * CW:(c + 1) * CW], ps_k[:, d * 256:(d + 1) * 256],
                AF.Identity, bias=bk_sb[:, d:d + 1], scale=1.0), ACT)
        qkv_state["evac_a"] = ACT.n
        QV.wait(TEN, t_mm)
        for s_ in range(2):
            QV.emit(lambda e, s2=s_, c=ci: e.tensor_tensor(
                vv[:, 2 * c + s2, :], ps_v[:, s2 * 256:(s2 + 1) * 256],
                bvrep[:, :], ALU.add), VEC)
        qkv_state["evac"] = VEC.n
        return TEN, t_mm

    NCH_B = S // CW

    def xnf_consumers(ci, xn, t_xn):
        c = ci - NCH_B
        for tt in range(8):
            QV.emit(lambda e, m=tt, cc2=c: e.tensor_copy(
                xnf[:, m, cc2 * CW:(cc2 + 1) * CW], xn[:, m, :]), VEC)
        return VEC, VEC.n

    def p1_src(ci):
        if ci < NCH_B:
            return xT[:, ci * CW:(ci + 1) * CW].rearrange("(t p) c -> p t c", p=128)
        c = ci - NCH_B
        return xoT[:, c * CW:(c + 1) * CW].rearrange("(t p) c -> p t c", p=128)

    def p1_consumers(ci, xn, t_xn):
        if ci < NCH_B:
            return qkv_consumers(ci, xn, t_xn)
        return xnf_consumers(ci, xn, t_xn)

    for _ in ln_pipeline(
            "a", p1_src, NCH_B + TOK // CW,
            lambda ci: ci >= NCH_B, p1_consumers):
        pass
    barrier()
    nc.sbuf_base = sbuf_M1
    nc.psum_base = psum_p1
    if not on("p1c"):
        return _truncate()

    ps_smp = [ps(f"ps_smp{i}", [128, 256]) for i in range(2)]
    runmax = sb("runmax", [128, 1], F32)
    colmax = sb("colmax", [128, 1], F32)
    rowback = sb("rowback", [1, 128], F32)
    cval = sb("cval", [1, 1], F32)
    t_red = 0
    for pr in range(2):
        for kb in range(NKCH // 2):
            QT.wait(VEC, t_red)
            for sub in range(2):
                k = 2 * kb + sub
                QT.emit(lambda e, p=pr, kk=k, s_=sub: e.matmul(
                    ps_smp[0][:, s_ * 128:(s_ + 1) * 128],
                    kt[p][0:64, kk * 128:(kk + 1) * 128],
                    qt_[p][0:64, 0:128],
                    start=True, stop=True, tile_position=(0, 0)), TEN)
                QT.emit(lambda e, p=pr, kk=k, s_=sub: e.matmul(
                    ps_smp[1][:, s_ * 128:(s_ + 1) * 128],
                    kt[p][64:128, kk * 128:(kk + 1) * 128],
                    qt_[p][64:128, 0:128],
                    start=True, stop=True, tile_position=(64, 0)), TEN)
            t_mm = TEN.n
            QV.wait(TEN, t_mm)
            for ii in range(2):
                QV.emit(lambda e, i=ii: e.reduce_max(colmax[:, :], ps_smp[i][:, :], AX.X), VEC)
                if pr == 0 and kb == 0 and ii == 0:
                    QV.emit(lambda e: e.tensor_copy(runmax[:, :], colmax[:, :]), VEC)
                else:
                    QV.emit(lambda e: e.tensor_tensor(runmax[:, :], runmax[:, :],
                                                      colmax[:, :], ALU.max), VEC)
            t_red = VEC.n
    QSY.wait(VEC, t_red)
    QSY.emit(lambda e: e.dma_start(out=mxd[:, :], in_=runmax[:, :]), SMX, 16)
    t_st = SMX.n
    QSY.wait(SMX, t_st)
    QSY.emit(lambda e: e.dma_start(
        out=rowback[:, :], in_=mxd[:, :].rearrange("p one -> one p")), SMX, 16)
    t_ld = SMX.n
    QV.wait(SMX, t_ld)
    QV.emit(lambda e: e.reduce_max(cval[:, :], rowback[:, :], AX.X), VEC)
    QV.emit(lambda e: e.tensor_scalar(cval[:, :], cval[:, :], -0.125, None, ALU.mult), VEC)
    t_c = VEC.n
    QT.wait(VEC, t_c)
    QT.emit(lambda e: e.matmul(ps_smp[0][:, 0:1], ones_row[:, :], cval[:, :],
                               start=True, stop=True), TEN)
    QV.wait(TEN)
    QV.emit(lambda e: e.tensor_copy(crep[:, :], ps_smp[0][:, 0:1]), VEC)
    barrier()
    nc.sbuf_base = sbuf_M1
    nc.psum_base = psum_p1
    if not on("p2"):
        return _truncate()

    ot = [sb(f"ot{i}", [128, S]) for i in range(2)]
    sbuf_M2 = nc.sbuf_base
    PBW = 6 * 512
    pbuf = [sb(f"pbuf{i}", [128, PBW]) for i in range(3)]
    acc6 = sb("acc6", [128, PBW])
    of_sb = sb("of_sb", [128, 512], F32)
    re_row = sb("re_row", [1, 512], F32)
    re2_row = sb("re2_row", [1, 512], F32)
    rep_sb = sb("rep_sb", [128, 512], F32)
    s_ps = [ps(f"s_ps{i}", [128, GRP * 512]) for i in range(2)]
    o_ps = ps("o_ps", [128, 1024])

    t_sfree = [0, 0]
    t_pv_pb = [0, 0, 0]
    t_add_pb = [0, 0, 0]
    t_of_copy = 0

    NT = 2 * NKCH
    NG = (NT + GRP - 1) // GRP
    pending_epi = [None]

    for qb in range(NQB):
        for pr in range(2):
            unit_idx = qb * 2 + pr
            grp_mm_tok = [0] * NG
            grp_exp_tok = [0] * NG
            slot_first_add = True

            def tloc(j):
                g = j // GRP
                return g, (j % GRP) * 512, (g // 2) % 3, (g % 2) * 1536

            pv_emitted = 0
            pv_enabled = [pending_epi[0] is None]

            def emit_pv_upto(gdone):
                nonlocal pv_emitted
                if not pv_enabled[0]:
                    return
                while pv_emitted < NKCH:
                    k = pv_emitted
                    jo = 2 * k + 1
                    gmax = jo // GRP
                    if gmax > gdone:
                        break
                    ge, ce, sle, bce = tloc(2 * k)
                    go, co, slo, bco = tloc(jo)
                    QT.wait(ACT, grp_exp_tok[gmax])
                    if k == 0:
                        QT.wait(EPI, unit_idx)
                    stf = (k == 0)
                    spf = (k == NKCH - 1)
                    QT.emit(lambda e, kk=k, p=pr, pi=sle, bc=bce + ce, s_=stf, f=spf: e.matmul(
                        o_ps[0:64, 0:512],
                        vv[:, kk, 2 * p * 64: 2 * p * 64 + 64],
                        pbuf[pi][:, bc:bc + 512],
                        start=s_, stop=f, tile_position=(0, 0)), TEN)
                    QT.emit(lambda e, kk=k, p=pr, pi=slo, bc=bco + co, s_=stf, f=spf: e.matmul(
                        o_ps[64:128, 512:1024],
                        vv[:, kk, (2 * p + 1) * 64: (2 * p + 1) * 64 + 64],
                        pbuf[pi][:, bc:bc + 512],
                        start=s_, stop=f, tile_position=(0, 64)), TEN)
                    t_pv_pb[sle] = TEN.n
                    t_pv_pb[slo] = TEN.n
                    pv_emitted += 1

            for j0 in range(0, NT, 2):
                kch = j0 // 2
                for (j, row0, tp) in ((j0, 0, (0, 0)), (j0 + 1, 64, (64, 0))):
                    g, col, slot, bufcol = tloc(j)
                    par = g % 2
                    if j % GRP == 0:
                        QT.wait(ACT, t_sfree[par])
                    QT.emit(lambda e, p=par, c=col, r=row0, k=kch, t=pr, q=qb, tp_=tp: e.matmul(
                        s_ps[p][:, c:c + 512],
                        kt[t][r:r + 64, k * 128:(k + 1) * 128],
                        qt_[t][r:r + 64, q * QB:(q + 1) * QB],
                        start=True, stop=True, tile_position=tp_), TEN)
                    grp_mm_tok[g] = TEN.n
                    if (j % GRP == GRP - 1) or (j == NT - 1):
                        width = col + 512
                        QA.wait(TEN, grp_mm_tok[g])
                        QA.wait(TEN, t_pv_pb[slot])
                        QA.wait(VEC, t_add_pb[slot])
                        QA.emit(lambda e, p=par, w=width, pi=slot, bc=bufcol: e.activation(
                            pbuf[pi][:, bc:bc + w], s_ps[p][:, 0:w], AF.Exp,
                            bias=crep[:, 0:1], scale=0.125), ACT)
                        grp_exp_tok[g] = ACT.n
                        t_sfree[par] = ACT.n
                        if (g % 2 == 1) or (j == NT - 1):
                            cols = bufcol + width
                            QV.wait(ACT, grp_exp_tok[g])
                            if slot_first_add:
                                QV.emit(lambda e, pi=slot, cl=cols: e.tensor_copy(
                                    acc6[:, 0:cl], pbuf[pi][:, 0:cl]), VEC)
                                slot_first_add = False
                            else:
                                QV.emit(lambda e, pi=slot, cl=cols: e.tensor_tensor(
                                    acc6[:, 0:cl], acc6[:, 0:cl], pbuf[pi][:, 0:cl],
                                    ALU.add), VEC)
                            t_add_pb[slot] = VEC.n
                        emit_pv_upto(g - 1)
                    if j == 2 and pending_epi[0] is not None:
                        pending_epi[0][0]()
                    if j == 8 and pending_epi[0] is not None:
                        pending_epi[0][1]()
                        pending_epi[0] = None
                        pv_enabled[0] = True
                        emit_pv_upto((j // GRP) - 1)

            emit_pv_upto(NG)

            QV.wait(TEN)
            QV.emit(lambda e: e.tensor_copy(of_sb[0:64, :], o_ps[0:64, 0:512]), VEC)
            QV.emit(lambda e: e.tensor_copy(of_sb[64:128, :], o_ps[64:128, 512:1024]), VEC)
            t_of_copy = VEC.n

            def mk_epilogue(pr=pr, qb=qb, t_of=t_of_copy):
                state = {}

                def flush_sums():
                    QT.wait(VEC, t_of)
                    for r6 in range(3):
                        QT.emit(lambda e, r=r6: e.matmul(
                            o_ps[0:1, 0:512], ones_col[:, :],
                            acc6[:, r * 1024:r * 1024 + 512],
                            start=(r == 0), stop=(r == 2), tile_position=(0, 0)), TEN)
                    for r6 in range(3):
                        QT.emit(lambda e, r=r6: e.matmul(
                            o_ps[32:33, 512:1024], ones_col[:, :],
                            acc6[:, r * 1024 + 512:r * 1024 + 1024],
                            start=(r == 0), stop=(r == 2), tile_position=(0, 32)), TEN)
                    t_sums = TEN.n

                    QV.wait(TEN, t_sums)
                    for (row, coff, hloc) in ((0, 0, 2 * pr), (32, 512, 2 * pr + 1)):
                        QV.emit(lambda e, r=row, c=coff: e.reciprocal(
                            re_row[:, :], o_ps[r:r + 1, c:c + 512]), VEC)
                        QV.emit(lambda e, h=hloc: e.tensor_scalar(
                            re2_row[:, :], re_row[:, :], hm_sb[:, h:h + 1], None,
                            ALU.mult), VEC)
                        QV.emit(lambda e, r=row: e.tensor_copy(
                            rb33[r:r + 1, :], re2_row[:, :]), VEC)
                    state["t_rb"] = VEC.n

                def flush_norm():
                    QT.wait(VEC, state["t_rb"])
                    QT.emit(lambda e: e.matmul(o_ps[:, 0:512], sel33[:, :], rb33[:, :],
                                               start=True, stop=True), TEN)
                    t_bcast = TEN.n

                    QV.wait(TEN, t_bcast)
                    epi_tok = QV.emit(lambda e: e.tensor_copy(
                        rep_sb[:, :], o_ps[:, 0:512]), EPI)
                    QV.wait(EPI, epi_tok)
                    QV.emit(lambda e, p=pr, q=qb: e.tensor_tensor(
                        ot[p][:, q * QB:(q + 1) * QB], of_sb[:, :], rep_sb[:, :],
                        ALU.mult), VEC)
                return (flush_sums, flush_norm)

            pending_epi[0] = mk_epilogue()

    pending_epi[0][0]()
    pending_epi[0][1]()
    pending_epi[0] = None
    barrier()
    nc.sbuf_base = sbuf_M2
    nc.psum_base = psum_p1
    if not on("p3"):
        return _truncate()

    wo_sb = sb("wo_sb", [128, 2, D])
    sp_sb = [sb(f"sp_sb{i}", [128, D]) for i in range(4)]
    ps_sp = [ps(f"ps_sp{i}", [128, D]) for i in range(4)]

    QG.emit(lambda e: e.dma_start(
        out=wo_sb[:, :, :], in_=woT[:, :].rearrange("(t p) c -> p t c", p=128)), GW, 16)
    gp_wo = GW.n

    t_sp_dma = [0] * 4
    t_sp_evac = [0] * 4
    t_sp_evac_a = [0] * 4
    QT.wait(GW, gp_wo)
    for tt in range(S // 128):
        bi = tt % 4
        QT.wait(VEC, t_sp_evac[bi])
        QT.wait(ACT, t_sp_evac_a[bi])
        for ch in range(2):
            for half in range(2):
                QT.emit(lambda e, t2=tt, c=ch, h=half, b=bi: e.matmul(
                    ps_sp[b][:, h * 512:(h + 1) * 512],
                    ot[c][:, t2 * 128:(t2 + 1) * 128],
                    wo_sb[:, c, h * 512:(h + 1) * 512],
                    start=(c == 0), stop=(c == 1)), TEN)
        t_mm = TEN.n
        QV.wait(TEN, t_mm)
        QV.wait(SSP[bi], t_sp_dma[bi])
        QV.emit(lambda e, b=bi: e.tensor_copy(sp_sb[b][:, 0:512], ps_sp[b][:, 0:512]), VEC)
        t_ev_v = VEC.n
        QA.wait(TEN, t_mm)
        QA.wait(SSP[bi], t_sp_dma[bi])
        QA.emit(lambda e, b=bi: e.activation(
            sp_sb[b][:, 512:1024], ps_sp[b][:, 512:1024], AF.Identity,
            bias=0.0, scale=1.0), ACT)
        t_ev_a = ACT.n
        t_sp_evac[bi] = t_ev_v
        t_sp_evac_a[bi] = t_ev_a
        QSY.wait(VEC, t_ev_v)
        QSY.wait(ACT, t_ev_a)
        QSY.emit(lambda e, b=bi, t2=tt: e.dma_start(
            out=cc_in[t2 * 128:(t2 + 1) * 128, :], in_=sp_sb[b][:, :]), SSP[bi], 16)
        t_sp_dma[bi] = SSP[bi].n

    if not on("cc"):
        return _truncate()
    barrier()
    nc.sbuf_base = sbuf_M0
    nc.psum_base = psum_p1

    g_sb = sb("g_sb", [128, 16, D])
    sbuf_M5 = nc.sbuf_base
    wgu_sb = [sb(f"wgu_sb{i}", [128, 2, 8, 128]) for i in range(NWGU)]
    wd_all = sb("wd_all", [128, 24, D])
    sg_sb = sb("sg_sb", [128, 512], F32)
    hp = sb("hp", [128, 24, 512])
    ps_g = ps("ps_g", [128, 512])
    ps_u = ps("ps_u", [128, 512])
    ps_d = [ps(f"ps_d{i}", [128, D]) for i in range(2)]

    for _s in SSP:
        QG.wait(_s, _s.n)
    QG.emit(lambda e: e.dma_start(
        out=wd_all[:, :, :], in_=wdT[:, :].rearrange("(t p) c -> p t c", p=128)), GW, 16)
    gp_wd = GW.n
    t_wgu_tok = [0] * NWGU
    for h in range(NWGU):
        QG.emit(lambda e, b=h: e.dma_start(
            out=wgu_sb[b][:, :, :, :], in_=wguT[b, :, :, :, :]), GF[h], 16)
        t_wgu_tok[h] = GF[h].n
    QG.emit(lambda e: e.collective_compute(
        "ReduceScatter", ALU.add,
        replica_groups=[[0, 1, 2, 3], [4, 5, 6, 7]],
        ins=[cc_in[:, :]], outs=[cc_out[:, :]]), CCS)
    if not on("p5"):
        return _truncate()

    t_wgu_used = [0] * NWGU
    t_gate_act = 0
    t_hpm_last = 0
    t_hp_used = 0
    t_g_act = [0, 0]
    t_g_tok = [0] * 16

    s_tile = [sb(f"s_tile{i}", [128, D]) for i in range(2)]
    x_tile = [sb(f"x_tile{i}", [128, D], F32) for i in range(2)]
    o_tile = [sb(f"o_tile{i}", [128, D], F32) for i in range(2)]
    t_used = [0, 0]
    t_osent = [0, 0]
    QSY.wait(CCS, 1)

    def emit_p6_tile(tt):
        if not on("p6"):
            return
        bi = tt % 2
        QSY.wait(VEC, t_used[bi])
        QSY.emit(lambda e, b=bi, t2=tt: e.dma_start(
            out=s_tile[b][:, :], in_=cc_out[t2 * 128:(t2 + 1) * 128, :]), SS6[bi], 16)
        QSY.emit(lambda e, b=bi, t2=tt: e.dma_start(
            out=x_tile[b][:, :], in_=xrows[t2 * 128:(t2 + 1) * 128, :]), SS6[bi], 16)
        t_ld = SS6[bi].n
        QV.wait(SS6[bi], t_ld)
        QV.wait(SO6[bi], t_osent[bi])
        QV.wait(ACT, t_g_tok[tt])
        QV.emit(lambda e, b=bi: e.tensor_tensor(
            o_tile[b][:, :], s_tile[b][:, :], borep[:, :], ALU.add), VEC)
        QV.emit(lambda e, b=bi, t2=tt: e.tensor_tensor(
            o_tile[b][:, :], o_tile[b][:, :], g_sb[:, t2, :], ALU.mult), VEC)
        QV.emit(lambda e, b=bi: e.tensor_tensor(
            o_tile[b][:, :], o_tile[b][:, :], x_tile[b][:, :], ALU.add), VEC)
        t_used[bi] = VEC.n
        QSY.wait(VEC, t_used[bi])
        QSY.emit(lambda e, b=bi, t2=tt: e.dma_start(
            out=out[t2 * 128:(t2 + 1) * 128, :], in_=o_tile[b][:, :]), SO6[bi], 16)
        t_osent[bi] = SO6[bi].n

    for tokb in range(4):
        for hc in range(24):
            bi = hc % NWGU
            if tokb == 0 and hc < NWGU:
                t_wpair = t_wgu_tok[bi]
            else:
                QG.wait(TEN, t_wgu_used[bi])
                QG.emit(lambda e, b=bi, h=hc: e.dma_start(
                    out=wgu_sb[b][:, :, :, :], in_=wguT[h, :, :, :, :]), GF[bi], 16)
                t_wpair = GF[bi].n

            QT.wait(GF[bi], t_wpair)
            QT.wait(ACT, t_gate_act)
            for m in range(8):
                QT.emit(lambda e, b=bi, mm=m, tb=tokb: e.matmul(
                    ps_g[:, :], wgu_sb[b][:, 0, mm, :],
                    xnf_flat[:, mm, tb * 512:(tb + 1) * 512],
                    start=(mm == 0), stop=(mm == 7)), TEN)
            t_gmm = TEN.n
            QT.wait(VEC, t_hpm_last)
            for m in range(8):
                QT.emit(lambda e, b=bi, mm=m, tb=tokb: e.matmul(
                    ps_u[:, :], wgu_sb[b][:, 1, mm, :],
                    xnf_flat[:, mm, tb * 512:(tb + 1) * 512],
                    start=(mm == 0), stop=(mm == 7)), TEN)
            t_wgu_used[bi] = TEN.n
            t_umm = TEN.n

            QA.wait(TEN, t_gmm)
            QA.wait(VEC, t_hpm_last)
            QA.emit(lambda e: e.activation(sg_sb[:, :], ps_g[:, :], AF.Silu,
                                           bias=0.0, scale=1.0), ACT)
            t_gate_act = ACT.n
            QV.wait(ACT, t_gate_act)
            QV.wait(TEN, t_umm)
            QV.wait(TEN, t_hp_used)
            QV.emit(lambda e, h=hc: e.tensor_tensor(
                hp[:, h, :], sg_sb[:, :], ps_u[:, :], ALU.mult), VEC)
            t_hpm_last = VEC.n

        t_hp_all = VEC.n
        QT.wait(VEC, t_hp_all)
        if tokb == 0:
            QT.wait(GW, gp_wd)
        for st128 in range(4):
            bi = st128 % 2
            QT.wait(ACT, t_g_act[bi])
            for hc in range(24):
                for half in range(2):
                    QT.emit(lambda e, b=bi, h=hc, s_=st128, hf=half: e.matmul(
                        ps_d[b][:, hf * 512:(hf + 1) * 512],
                        hp[:, h, s_ * 128:(s_ + 1) * 128],
                        wd_all[:, h, hf * 512:(hf + 1) * 512],
                        start=(h == 0), stop=(h == 23)), TEN)
            t_dmm = TEN.n
            QA.wait(TEN, t_dmm)
            QA.emit(lambda e, b=bi, tb=tokb, s_=st128: e.activation(
                g_sb[:, tb * 4 + s_, :], ps_d[b][:, :], AF.Silu,
                bias=0.0, scale=DEQ), ACT)
            t_g_act[bi] = ACT.n
            t_g_tok[tokb * 4 + st128] = ACT.n
        t_hp_used = TEN.n
        if tokb >= 1:
            for st in range(4):
                emit_p6_tile((tokb - 1) * 4 + st)

    for st in range(4):
        emit_p6_tile(12 + st)
    _ = sbuf_M5

    QSY.wait(SO6[0], SO6[0].n)
    QSY.wait(SO6[1], SO6[1].n)

    return _replay()


_NC_CACHE = [None]


def _get_nc():
    if _NC_CACHE[0] is None:
        _NC_CACHE[0] = build_program()
    return _NC_CACHE[0]


def host_in_maps(L0_attn_x, L0_attn_Wq, L0_attn_bq, L0_attn_Wk, L0_attn_bk,
                 L0_attn_Wv, L0_attn_bv, L0_attn_Wo, L0_attn_bo, L0_attn_head_mask,
                 L0_ffn_w_gate, L0_ffn_w_up, L0_ffn_w_down):
    x = np.asarray(L0_attn_x, dtype=np.float32)
    Wq = np.asarray(L0_attn_Wq, dtype=np.float32)
    Wk = np.asarray(L0_attn_Wk, dtype=np.float32)
    Wv = np.asarray(L0_attn_Wv, dtype=np.float32)
    Wo = np.asarray(L0_attn_Wo, dtype=np.float32)
    bq = np.asarray(L0_attn_bq, dtype=np.float32)
    bk = np.asarray(L0_attn_bk, dtype=np.float32)
    bv = np.asarray(L0_attn_bv, dtype=np.float32)
    bo = np.asarray(L0_attn_bo, dtype=np.float32)
    hm = np.asarray(L0_attn_head_mask, dtype=np.float32)
    wg = np.asarray(L0_ffn_w_gate, dtype=np.int32)
    wu = np.asarray(L0_ffn_w_up, dtype=np.int32)
    wd = np.asarray(L0_ffn_w_down, dtype=np.int32)

    C = np.ascontiguousarray
    wgu_h = np.empty((24, 128, 2, 8, 128), dtype=np.int8)
    wgu_h[:, :, 0] = wg.reshape(24, 128, 8, 128).transpose(0, 3, 2, 1)
    wgu_h[:, :, 1] = wu.reshape(24, 128, 8, 128).transpose(0, 3, 2, 1)
    wdT_h = C(wd.T.astype(np.int8))
    in_maps = []
    for c in range(8):
        b = c // 4
        g = c % 4
        qt = c % 4
        xb = x[b]
        xTb = C(xb.T)
        in_maps.append({
            "xT": xTb,
            "xoT": C(xb[qt * TOK:(qt + 1) * TOK].T),
            "xrows": C(xb[qt * TOK:(qt + 1) * TOK]),
            "wqT": C(Wq[g * DLOC:(g + 1) * DLOC, :].T),
            "wkT": C(Wk[g * DLOC:(g + 1) * DLOC, :].T),
            "wvT": C(Wv[g * DLOC:(g + 1) * DLOC, :].T),
            "woT": C(Wo[:, g * DLOC:(g + 1) * DLOC].T),
            "bqc": C(bq[g * DLOC:(g + 1) * DLOC].reshape(2, 128).T),
            "bkc": C(bk[g * DLOC:(g + 1) * DLOC].reshape(2, 128).T),
            "bvr": C(bv[g * DLOC:(g + 1) * DLOC].reshape(1, DLOC)),
            "bor": C(bo.reshape(1, D)),
            "hmr": C(hm[g * H_LOC:(g + 1) * H_LOC].reshape(1, H_LOC)),
            "wguT": wgu_h,
            "wdT": wdT_h,
        })
    return in_maps


def kernel(**inputs):
    in_maps = host_in_maps(**inputs)
    nc = _get_nc()
    res = run_bass_kernel_spmd(nc, in_maps, core_ids=list(range(8)))
    outp = np.empty((2, S, D), dtype=np.float32)
    for c in range(8):
        b = c // 4
        qt = c % 4
        outp[b, qt * TOK:(qt + 1) * TOK, :] = res.results[c]["out"]
    return outp
